# version 05
# baseline (speedup 1.0000x reference)
import contextlib
import numpy as np
import ml_dtypes
import concourse.bass as bass
import concourse.mybir as mybir
from concourse.bass_utils import run_bass_kernel_spmd

F32 = mybir.dt.float32
BF16 = mybir.dt.bfloat16
I32 = mybir.dt.int32
U8 = mybir.dt.uint8
ALU = mybir.AluOpType
ACT = mybir.ActivationFunctionType
AX = mybir.AxisListType
bf16 = ml_dtypes.bfloat16

D = 1024
T_FULL = 2048
B_FULL = 16
N_CORES = 8
B_LOC = B_FULL // N_CORES
TB = 512
N_BLK = B_LOC * (T_FULL // TB)
NTT = TB // 128
NCD = D // 128

LN2 = float(np.log(2.0))
INV_LN2 = float(1.0 / np.log(2.0))
BIG = 3.0e38

W_SPECS = {
    "wdt":  (1024, 1024, "sign"),
    "wx":   (1024, 2048, "sign"),
    "w1":   (1024, 4096, "cast"),
    "w2":   (4096, 1024, "e4m3"),
    "fr":   (1024, 1024, "cast"),
    "dp":   (1024, 1024, "cast"),
    "bp":   (1024, 1024, "cast"),
    "cp":   (1024, 1024, "cast"),
    "m1":   (1024, 2048, "cast"),
    "m2":   (2048, 1024, "cast"),
    "gate": (2048, 1024, "cast"),
    "f1":   (1024, 4096, "cast"),
    "f2":   (4096, 1024, "cast"),
}

BIAS_TILES = [("bdt", 8), ("bx", 16), ("alog", 8), ("ng", 8), ("nb", 8),
              ("frb", 8), ("dpb", 8), ("bpb", 8), ("cpb", 8), ("skp", 8),
              ("m1b", 16), ("m2b", 8), ("gab", 8), ("fng", 8), ("fnb", 8),
              ("f1b", 32), ("f2b", 8), ("b1", 32), ("b2", 8)]


def wg_of(K):
    return 512 if K <= 2048 else 256



def _pieces(w_ko, Wg):
    K, O = w_ko.shape
    nk = K // 128
    nog = O // Wg
    out = np.empty((nog * nk, 128, Wg), np.float32)
    i = 0
    for og in range(nog):
        for kc in range(nk):
            out[i] = w_ko[kc * 128:(kc + 1) * 128, og * Wg:(og + 1) * Wg]
            i += 1
    return np.ascontiguousarray(out)


def _coltile(v, n):
    return np.ascontiguousarray(np.asarray(v, np.float32).reshape(n, 128).T)


def host_prepare(inputs):
    f = np.float32
    g = {}
    g["wdt"] = _pieces(np.asarray(inputs["L0_attn_a_W_dt"], f).T, wg_of(1024))
    g["wx"] = _pieces(np.asarray(inputs["L0_attn_a_W_x"], f).T, wg_of(1024))
    g["w1"] = _pieces(np.asarray(inputs["L0_ffn_w1"], f).T, wg_of(1024))
    g["w2"] = _pieces(np.asarray(inputs["L0_ffn_w2"], f).T, wg_of(4096))
    g["fr"] = _pieces(np.asarray(inputs["L0_attn_b_fr_w"], f), wg_of(1024))
    g["dp"] = _pieces(np.asarray(inputs["L0_attn_b_dp_w"], f), wg_of(1024))
    g["bp"] = _pieces(np.asarray(inputs["L0_attn_b_bp_w"], f), wg_of(1024))
    g["cp"] = _pieces(np.asarray(inputs["L0_attn_b_cp_w"], f), wg_of(1024))
    g["m1"] = _pieces(np.asarray(inputs["L0_attn_b_m1_w"], f), wg_of(1024))
    g["m2"] = _pieces(np.asarray(inputs["L0_attn_b_m2_w"], f), wg_of(2048))
    g["gate"] = _pieces(np.asarray(inputs["L0_attn_b_gate_w"], f), wg_of(2048))
    g["f1"] = _pieces(np.asarray(inputs["L0_attn_b_ffn1_w"], f), wg_of(1024))
    g["f2"] = _pieces(np.asarray(inputs["L0_attn_b_ffn2_w"], f), wg_of(4096))
    g["bdt"] = _coltile(inputs["L0_attn_a_b_dt"], 8)
    g["bx"] = _coltile(inputs["L0_attn_a_b_x"], 16)
    g["alog"] = _coltile(inputs["L0_attn_a_A_log"], 8)
    g["ng"] = _coltile(inputs["L0_attn_b_norm_g"], 8)
    g["nb"] = _coltile(inputs["L0_attn_b_norm_b"], 8)
    g["frb"] = _coltile(inputs["L0_attn_b_fr_b"], 8)
    g["dpb"] = _coltile(inputs["L0_attn_b_dp_b"], 8)
    g["bpb"] = _coltile(inputs["L0_attn_b_bp_b"], 8)
    g["cpb"] = _coltile(inputs["L0_attn_b_cp_b"], 8)
    g["skp"] = _coltile(inputs["L0_attn_b_skip"], 8)
    g["m1b"] = _coltile(inputs["L0_attn_b_m1_b"], 16)
    g["m2b"] = _coltile(inputs["L0_attn_b_m2_b"], 8)
    g["gab"] = _coltile(inputs["L0_attn_b_gate_b"], 8)
    g["fng"] = _coltile(inputs["L0_attn_b_ffn_g"], 8)
    g["fnb"] = _coltile(inputs["L0_attn_b_ffn_bn"], 8)
    g["f1b"] = _coltile(inputs["L0_attn_b_ffn1_b"], 32)
    g["f2b"] = _coltile(inputs["L0_attn_b_ffn2_b"], 8)
    g["b1"] = _coltile(inputs["L0_ffn_b1"], 32)
    g["b2"] = _coltile(inputs["L0_ffn_b2"], 8)
    g["ident"] = np.eye(128, dtype=np.float32)
    return g



ENGINES = ("sy", "ve", "sc", "te", "gp")
DEBUG_POINT = ["endA"]
DMA_CLASSES = ("bq", "sq", "wq", "xq", "pw", "oq")
DMA_ENGINE = {"bq": "sy", "sq": "sy", "wq": "sy", "xq": "sy",
              "pw": "gp", "oq": "gp"}


class Prog:
    def __init__(self):
        self.ops = {e: [] for e in ENGINES}
        self.n = {e: 0 for e in ENGINES}
        self.waited = {e: {} for e in ENGINES}
        self.q = {c: 0 for c in DMA_CLASSES}
        self.eng = None
        self.sem = None
        self.dsem = None

    def emit(self, e, fn):
        self.ops[e].append(fn)
        return None

    def mark(self, e, fn):
        self.n[e] += 1
        tok = self.n[e]
        def wrapped(_fn=fn, _e=e):
            _fn().then_inc(self.sem[_e], 1)
        self.ops[e].append(wrapped)
        return (e, tok)

    def wait(self, e, dep):
        if dep is None:
            return
        if dep[0] in ENGINES:
            kind, tok = dep
            assert tok <= self.n[kind], \
                f"deadlock: {e} waits {kind}@{tok} > {self.n[kind]}"
            if kind == e:
                return
            if self.waited[e].get(kind, 0) >= tok:
                return
            self.waited[e][kind] = tok
            def w(_k=kind, _t=tok, _e=e):
                self.eng[_e].wait_ge(self.sem[_k], _t)
            self.ops[e].append(w)
        else:
            cls, idx, val = dep
            key = (cls, idx)
            if self.waited[e].get(key, 0) >= val:
                return
            self.waited[e][key] = val
            def w(_key=key, _v=val, _e=e):
                self.eng[_e].wait_ge(self.dsem[_key], _v)
            self.ops[e].append(w)

    def dma(self, cls, out_fn, in_fn, pre=None):
        eng = DMA_ENGINE[cls]
        if isinstance(pre, list):
            for p in pre:
                self.wait(eng, p)
        else:
            self.wait(eng, pre)
        i = self.q[cls]
        self.q[cls] += 1
        idx = i % 2
        val = ((i // 2) + 1) * 16
        def d(_o=out_fn, _i=in_fn, _k=(cls, idx), _e=eng):
            self.eng[_e].dma_start(out=_o(), in_=_i()).then_inc(self.dsem[_k], 16)
        self.ops[eng].append(d)
        return (cls, idx, val)

    def tok(self, e):
        return (e, self.n[e])


def build_program(debug_taps=(), debug_fm=()):
    nc = bass.Bass("TRN2", target_bir_lowering=False)
    P = Prog()
    stack = contextlib.ExitStack()

    dram = {}
    dram["x"] = nc.dram_tensor("x", [B_LOC, T_FULL, D], F32, kind="ExternalInput").ap()
    dram["out"] = nc.dram_tensor("out", [B_LOC, T_FULL, D], F32,
                                 kind="ExternalOutput").ap()
    for w, (K, O, _) in W_SPECS.items():
        Wg = wg_of(K)
        npc = (K // 128) * (O // Wg)
        dram[w] = nc.dram_tensor(w, [npc, 128, Wg], F32, kind="ExternalInput").ap()
        dram["scr_" + w] = nc.dram_tensor("scr_" + w, [npc, 128, Wg], BF16,
                                          kind="Internal").ap()
    for b, n in BIAS_TILES:
        dram[b] = nc.dram_tensor(b, [128, n], F32, kind="ExternalInput").ap()
    dram["ident"] = nc.dram_tensor("ident", [128, 128], F32,
                                   kind="ExternalInput").ap()
    fmtaps = {}
    for ent in debug_fm:
        tag, off, cols = ent[0], ent[1], ent[2]
        point = ent[3] if len(ent) > 3 else "endA"
        dt_ = BF16 if (isinstance(off, str) and off.startswith("ab:")) else F32
        fmtaps[tag] = (nc.dram_tensor("fmtap_" + tag, [128, cols], dt_,
                                      kind="ExternalOutput").ap(), off, cols, point)
    taps = {}
    for blk, phase in debug_taps:
        taps[(blk, phase)] = nc.dram_tensor(f"tap_{blk}_{phase}", [TB, D], F32,
                                            kind="ExternalOutput").ap()

    T = {}
    def sbuf(name, cols, dtype=F32):
        T[name] = stack.enter_context(nc.sbuf_tensor("sb_" + name, [128, cols], dtype))

    sbuf("ident", 128)
    sbuf("ones1", 128)
    sbuf("zeros", 512)
    sbuf("R", 512)
    sbuf("rowbuf", 640)
    for b, n in BIAS_TILES:
        sbuf(b, n)
    sbuf("acol", 8)
    sbuf("frbh", 8); sbuf("dpbh", 8); sbuf("gabh", 8)
    sbuf("b2q", 8)
    sbuf("wsdt", 2); sbuf("wsx", 2)
    sbuf("wtmp", 8)
    sbuf("cb", 8)
    for s in ["st_sum", "st_ssq", "st_mneg", "st_var", "st_r", "st_rv",
              "st_mrneg", "st_am", "st_scf", "st_sum2", "st_ssq2", "st_am2"]:
        sbuf(s, 8)
    sbuf("wacc", 16)
    sbuf("ccarry", 8)
    sbuf("cscarry", 8)
    sbuf("x_tm", NTT * 1024)
    sbuf("AF", 22016)
    sbuf("AB", 28672, BF16)
    sbuf("wbuf0", 8192, BF16)
    sbuf("wbuf1", 8192, BF16)
    sbuf("stri", 2048, I32)
    sbuf("strm", 512, U8)

    PSUM = {}
    for p in ["pm0", "pm1", "pm2", "pm3", "pm4", "pm5", "pt0", "pt1"]:
        PSUM[p] = stack.enter_context(nc.psum_tensor(p, [128, 512], F32))

    def af(off, n):
        return T["AF"][:, off:off + n]

    def ab(off, n):
        return T["AB"][:, off:off + n]

    def ap3(tname, off, stride, n, w):
        h = T[tname]
        row = h.shape[1] if hasattr(h, "shape") else None
        hh = h[:, 0:1]
        rowlen = hh.ap[0][0]
        return bass.AP(hh.tensor, off, [[rowlen, 128], [stride, n], [1, w]])

    def psum3(bank, n, w):
        hh = PSUM[bank][:, 0:1]
        rowlen = hh.ap[0][0]
        return bass.AP(hh.tensor, 0, [[rowlen, 128], [w, n], [1, w]])

    ZT = 0
    WSTG = [1024, 3072]
    QV = [(5120, 7168), (9216, 11264)]
    QG = [13312, 15360]
    A_KF = 1024
    A_DT = 2048
    A_GN = 6144
    A_TC = 10240
    A_PB = 14336
    A_S = [18440 + i * 512 for i in range(6)]
    B_HB = [1024 + i * 512 for i in range(4)]
    B_QV = [(3072 + i * 1024, 3584 + i * 1024) for i in range(4)]
    B_QG = [7168 + i * 512 for i in range(4)]
    B_S = [9216 + i * 512 for i in range(4)]
    C_XN = 1024
    C_MB = 5120
    C_SS = 9216
    C_Y = 13312
    C_S = [17408 + i * 512 for i in range(6)]

    WQB = [0, 2048]
    AB_K = 0
    AB_XN = 0
    AB_LOW = 4096
    AB_HIGH = 8192
    AB_GEL = 12288
    AB_MAM = 20480
    AB_SSD = 24576
    AB_HQ = 4096

    def TT(e, out, a, b, op, mark=False):
        f = lambda: P.eng[e].tensor_tensor(out, a, b, op)
        return P.mark(e, f) if mark else P.emit(e, f)

    def TS(e, out, a, s1, s2, op0, op1=None, mark=False):
        if op1 is None:
            f = lambda: P.eng[e].tensor_scalar(out, a, s1, None, op0)
        else:
            f = lambda: P.eng[e].tensor_scalar(out, a, s1, s2, op0, op1)
        return P.mark(e, f) if mark else P.emit(e, f)

    def STT(e, out, a, s, b, op0, op1, accum=None, mark=False):
        if accum is None:
            f = lambda: P.eng[e].scalar_tensor_tensor(out, a, s, b, op0, op1)
        else:
            f = lambda: P.eng[e].scalar_tensor_tensor(out, a, s, b, op0, op1,
                                                      accum_out=accum)
        return P.mark(e, f) if mark else P.emit(e, f)

    def CP(e, out, a, mark=False):
        f = lambda: P.eng[e].tensor_copy(out, a)
        return P.mark(e, f) if mark else P.emit(e, f)

    def AV(out, a, func, bias=0.0, scale=1.0, mark=False):
        f = lambda: P.eng["sc"].activation(out=out, in_=a, func=func,
                                           bias=bias, scale=scale)
        return P.mark("sc", f) if mark else P.emit("sc", f)

    def RED(out, a, op="sum", absolute=False, mark=False):
        if op == "max":
            f = lambda: P.eng["ve"].reduce_max(out, a, axis=AX.X,
                                               apply_absolute_value=absolute)
        else:
            f = lambda: P.eng["ve"].reduce_sum(out, a, axis=AX.X,
                                               apply_absolute_value=absolute)
        return P.mark("ve", f) if mark else P.emit("ve", f)

    def RECIP(out, a, mark=False):
        f = lambda: P.eng["ve"].reciprocal(out, a)
        return P.mark("ve", f) if mark else P.emit("ve", f)

    def SCAN(out, d0, d1, init, op0, op1, mark=False):
        f = lambda: P.eng["ve"].tensor_tensor_scan(out=out, data0=d0, data1=d1,
                                                   initial=init, op0=op0, op1=op1)
        return P.mark("ve", f) if mark else P.emit("ve", f)

    def SEL(out, mask, t_, f_, mark=False):
        f = lambda: P.eng["ve"].select(out, mask, t_, f_)
        return P.mark("ve", f) if mark else P.emit("ve", f)

    def MM(psum, lhsT, rhs, start, stop, mark=False):
        f = lambda: P.eng["te"].matmul(psum, lhsT, rhs, start=start, stop=stop)
        return P.mark("te", f) if mark else P.emit("te", f)

    def TR(psum, a, mark=False):
        f = lambda: P.eng["te"].transpose(psum, a, T["ident"][:])
        return P.mark("te", f) if mark else P.emit("te", f)

    def MS(e, out, val, mark=False):
        f = lambda: P.eng[e].memset(out, val)
        return P.mark(e, f) if mark else P.emit(e, f)

    psum_free = {}
    rot = {"pm": 0, "pt": 0}

    def next_bank(kind):
        names = (["pm0", "pm1", "pm2", "pm3", "pm4", "pm5"] if kind == "pm"
                 else ["pt0", "pt1"])
        b = names[rot[kind] % len(names)]
        rot[kind] += 1
        if b in psum_free:
            P.wait("te", psum_free[b])
        return b

    def free_bank(b, dep):
        psum_free[b] = dep

    wbuf_free = [None, None]

    def quant_chain(dst, src, n, man, clamp, maxv, pos, sv, sg, i32_off):
        sa, sb_ = af(sv[0], n), af(sv[1], n)
        sG = af(sg, n)
        si = T["stri"][:, i32_off:i32_off + n]
        if pos:
            t1 = AV(sb_, src, ACT.Ln, bias=T["cb"][:, 0:1], mark=True)
            c1 = INV_LN2
        else:
            AV(sa, src, ACT.Square)
            t1 = AV(sb_, sa, ACT.Ln, bias=T["cb"][:, 0:1], mark=True)
            c1 = 0.5 * INV_LN2
        narrow = n < 128
        if narrow:
            si2 = T["stri"][:, i32_off + n:i32_off + 2 * n]
            sa2 = af(sg + n, n)
        P.wait("ve", t1)
        if narrow:
            tn = TS("ve", si, sb_, -c1, 0.5, ALU.mult, ALU.add, mark=True)
            P.wait("gp", tn)
            tg0 = CP("gp", si2, si, mark=True)
            P.wait("ve", tg0)
            tv1 = TS("ve", sa, si2, float(clamp), float(man), ALU.min, ALU.add,
                     mark=True)
        else:
            TS("ve", si, sb_, -c1, 0.5, ALU.mult, ALU.add)
            tv1 = TS("ve", sa, si, float(clamp), float(man), ALU.min, ALU.add,
                     mark=True)
        P.wait("sc", tv1)
        AV(sb_, sa, ACT.Exp, scale=LN2)
        t2 = AV(sa, sa, ACT.Exp, scale=-LN2, mark=True)
        P.wait("ve", t2)
        if narrow:
            src_done = TT("ve", si, src, sb_, ALU.mult, mark=True)
            P.wait("gp", src_done)
            CP("gp", si2, si)
            tg1 = CP("gp", sa2, sa, mark=True)
            P.wait("ve", tg1)
            tv2 = TT("ve", sb_, si2, sa2, ALU.mult, mark=True)
            P.wait("gp", tv2)
            TT("gp", sG, sb_, sb_, ALU.mult)
            TS("gp", sG, sG, float(maxv) * float(maxv), None, ALU.is_gt)
            TS("gp", sG, sG, BIG, 1.0, ALU.mult, ALU.add)
            tg = TT("gp", dst, sb_, sG, ALU.mult, mark=True)
            return src_done, tg
        src_done = TT("ve", si, src, sb_, ALU.mult, mark=True)
        tv2 = TT("ve", sb_, si, sa, ALU.mult, mark=True)
        if pos:
            P.wait("sc", tv2)
            ts = AV(sG, sb_, ACT.Relu, bias=T["cb"][:, 3:4], mark=True)
            P.wait("ve", ts)
            tg = STT("ve", dst, sG, BIG, sb_, ALU.mult, ALU.add, mark=True)
        else:
            TT("ve", sG, sb_, sb_, ALU.mult)
            TS("ve", sG, sG, float(maxv) * float(maxv), None, ALU.is_gt)
            TS("ve", sG, sG, BIG, 1.0, ALU.mult, ALU.add)
            tg = TT("ve", dst, sb_, sG, ALU.mult, mark=True)
        return src_done, tg

    stg_free = {0: None, 1: None}
    qb_free = {0: None, 1: None}
    prep_w_done = {}
    qpar = [0]
    qprev = {0: None, 1: None}

    def prep_stream(w, batch_cb):
        K, O, _ = W_SPECS[w]
        Wg = wg_of(K)
        npc = (K // 128) * (O // Wg)
        per = 2048 // Wg
        for bi, i0 in enumerate(range(0, npc, per)):
            n = min(per, npc - i0)
            sl = bi % 2
            tok = P.dma("sq",
                        lambda _s=sl, _n=n, _Wg=Wg: af(WSTG[_s], _n * _Wg),
                        lambda _w=w, _i=i0, _n=n:
                            dram[_w][_i:_i + _n].rearrange("n p w -> p n w"),
                        pre=stg_free[sl])
            P.wait("ve", tok)
            P.wait("sc", tok)
            stg_free[sl] = batch_cb(WSTG[sl], n * Wg, i0, n, sl)

    def wscale_pass(w, wtile):
        K, O, _ = W_SPECS[w]
        nb = [0]
        def cb(off, ncols, i0, n, sl):
            tv = RED(T["wacc"][:, nb[0]:nb[0] + 1], af(off, ncols), "sum",
                     absolute=True, mark=True)
            nb[0] += 1
            return tv
        prep_stream(w, cb)
        assert nb[0] <= 8
        tv = P.tok("ve")
        P.wait("gp", tv)
        tg = CP("gp", T["st_sum2"][:, 0:nb[0]], T["wacc"][:, 0:nb[0]], mark=True)
        P.wait("ve", tg)
        t1 = RED(T["wtmp"][:, 1:2], T["st_sum2"][:, 0:nb[0]], "sum", mark=True)
        P.wait("gp", t1)
        tg = CP("gp", T["wtmp"][:, 0:1], T["wtmp"][:, 1:2], mark=True)
        P.wait("te", tg)
        bank = next_bank("pt")
        tt = TR(PSUM[bank][0:1, 0:128], T["wtmp"][:, 0:1], mark=True)
        P.wait("ve", tt)
        CP("ve", T["rowbuf"][0:1, 0:128], PSUM[bank][0:1, 0:128])
        tv = RED(T["wtmp"][0:1, 2:3], T["rowbuf"][0:1, 0:128], "sum", mark=True)
        free_bank(bank, tv)
        P.wait("gp", tv)
        tg = TS("gp", T["wtmp"][0:1, 3:4], T["wtmp"][0:1, 2:3],
                1.0 / (K * O), 1e-5, ALU.mult, ALU.add, mark=True)
        bank2 = next_bank("pt")
        P.wait("te", tg)
        tm = MM(PSUM[bank2][:, 0:1], T["ones1"][0:1, 0:128],
                T["wtmp"][0:1, 3:4], True, True, mark=True)
        P.wait("ve", tm)
        tv = CP("ve", T[wtile][:, 0:1], PSUM[bank2][:, 0:1], mark=True)
        P.wait("gp", tv)
        tv = TS("gp", T[wtile][:, 1:2], T[wtile][:, 0:1], 0.5, None, ALU.mult,
                mark=True)
        free_bank(bank2, tv)

    def prep_weight(w, K, O, q):
        wt = "wsdt" if w == "wdt" else "wsx"
        def cb(off, ncols, i0, n, sl):
            qb = ab(WQB[sl], ncols)
            if qb_free[sl] is not None:
                P.wait("ve", qb_free[sl])
                P.wait("gp", qb_free[sl])
            src = af(off, ncols)
            if q == "cast":
                done = CP("ve", qb, src, mark=True)
                wdep = done
            elif q == "sign":
                par = qpar[0] % 2
                qpar[0] += 1
                sa = af(QV[par][0], ncols)
                sb_ = af(QV[par][1], ncols)
                if qprev[par] is not None:
                    P.wait("ve", qprev[par])
                STT("ve", sa, src, -1.0, src, ALU.mult, ALU.max)
                tv0 = TS("ve", sa, sa, T[wt][:, 1:2], None, ALU.is_gt, mark=True)
                P.wait("sc", tv0)
                ts = AV(sb_, src, ACT.Sign, mark=True)
                P.wait("ve", ts)
                done = TT("ve", qb, sa, sb_, ALU.mult, mark=True)
                qprev[par] = done
                wdep = done
            else:
                par = qpar[0] % 2
                qpar[0] += 1
                if qprev[par] is not None:
                    P.wait("ve", qprev[par])
                    P.wait("sc", qprev[par])
                done, tg = quant_chain(qb, src, ncols, man=3, clamp=6,
                                       maxv=240.0, pos=False, sv=QV[par],
                                       sg=QG[par], i32_off=0 if par == 0 else 0)
                qprev[par] = tg
                wdep = tg
            wtok = P.dma("pw",
                         lambda _w=w, _i=i0, _n=n:
                             dram["scr_" + _w][_i:_i + _n].rearrange("n p w -> p n w"),
                         lambda _q=qb: _q,
                         pre=wdep)
            qb_free[sl] = wtok
            prep_w_done[w] = wtok
            return done
        prep_stream(w, cb)

    def prep():
        toks = []
        for b, _ in BIAS_TILES:
            toks.append(P.dma("bq", lambda _b=b: T[_b][:], lambda _b=b: dram[_b]))
        toks.append(P.dma("bq", lambda: T["ident"][:], lambda: dram["ident"]))
        last = toks[-1]
        prev = toks[-2]
        for e in ("ve", "sc", "te"):
            P.wait(e, last)
            P.wait(e, prev)
        MS("ve", T["ones1"][:, :], 1.0)
        MS("ve", T["zeros"][:], 0.0)
        MS("ve", T["cb"][:, 0:1], 1e-38)
        MS("ve", T["cb"][:, 1:2], 1e-5)
        MS("ve", T["cb"][:, 2:3], 1.0)
        MS("ve", T["cb"][:, 3:4], -240.0)
        MS("ve", T["cb"][:, 5:6], 3.0 * LN2)
        MS("ve", T["cb"][:, 6:7], -3.0 * LN2)
        cb_tok = MS("ve", T["cb"][:, 4:5], 0.5, mark=True)
        P.wait("sc", cb_tok)
        TS("ve", T["frbh"][:], T["frb"][:], 0.5, None, ALU.mult)
        TS("ve", T["dpbh"][:], T["dpb"][:], 0.5, None, ALU.mult)
        TS("ve", T["gabh"][:], T["gab"][:], 0.5, None, ALU.mult)
        AV(T["st_sum"][:], T["alog"][:], ACT.Abs)
        AV(T["st_ssq"][:], T["st_sum"][:], ACT.Exp, scale=-1.0)
        t = AV(T["st_var"][:], T["st_ssq"][:], ACT.Ln, bias=T["cb"][:, 2:3], mark=True)
        P.wait("gp", t)
        TS("gp", T["st_mneg"][:], T["alog"][:], 0.0, None, ALU.max)
        TT("gp", T["st_mneg"][:], T["st_mneg"][:], T["st_var"][:], ALU.add)
        tv = TS("gp", T["acol"][:], T["st_mneg"][:], -1.0, None, ALU.mult,
                mark=True)
        P.wait("sc", tv)
        P.wait("ve", tv)
        quant_chain(T["b2q"][:], T["b2"][:], 8, man=10, clamp=14, maxv=65504.0,
                    pos=False, sv=QV[0], sg=QG[0], i32_off=0)
        wscale_pass("wdt", "wsdt")
        wscale_pass("wx", "wsx")
        for w in ["wdt", "wx", "w1", "w2"]:
            K, O, q = W_SPECS[w]
            prep_weight(w, K, O, q)

    def prep2():
        stg2 = [ab(4096, 4096).bitcast(F32), ab(8192, 4096).bitcast(F32)]
        out2 = [ab(12288, 2048), ab(14336, 2048)]
        sfree = {0: None, 1: None}
        ofree = {0: None, 1: None}
        for w in ["fr", "dp", "bp", "cp", "m1", "m2", "gate", "f1", "f2"]:
            K, O, _ = W_SPECS[w]
            Wg = wg_of(K)
            npc = (K // 128) * (O // Wg)
            per = 2048 // Wg
            for bi, i0 in enumerate(range(0, npc, per)):
                n = min(per, npc - i0)
                sl = bi % 2
                tok = P.dma("sq",
                            lambda _s=sl, _n=n, _Wg=Wg: stg2[_s][:, 0:_n * _Wg],
                            lambda _w=w, _i=i0, _n=n:
                                dram[_w][_i:_i + _n].rearrange("n p w -> p n w"),
                            pre=sfree[sl])
                P.wait("sc", tok)
                if ofree[sl] is not None:
                    P.wait("sc", ofree[sl])
                ts = AV(out2[sl][:, 0:n * Wg], stg2[sl][:, 0:n * Wg],
                        ACT.Identity, mark=True)
                sfree[sl] = ts
                wtok = P.dma("pw",
                             lambda _w=w, _i=i0, _n=n:
                                 dram["scr_" + _w][_i:_i + _n].rearrange("n p w -> p n w"),
                             lambda _sl=sl, _n=n, _Wg=Wg: out2[_sl][:, 0:_n * _Wg],
                             pre=ts)
                ofree[sl] = wtok
                prep_w_done[w] = wtok
        return prep_w_done["f2"]

    def ln_stats_all(x_ap_fn, js=range(NTT)):
        trs = {}
        for j in js:
            sl = slice(j, j + 1)
            RED(T["st_sum"][:, sl], x_ap_fn(j), "sum")
            trs[j] = STT("ve", af(6144, 1024), x_ap_fn(j), 1.0, x_ap_fn(j),
                         ALU.bypass, ALU.mult, accum=T["st_ssq"][:, sl],
                         mark=True)
        deps = {}
        for j in js:
            sl = slice(j, j + 1)
            P.wait("gp", trs[j])
            TS("gp", T["st_mneg"][:, sl], T["st_sum"][:, sl], -1.0 / D, None,
               ALU.mult)
            TT("gp", T["st_var"][:, sl], T["st_mneg"][:, sl],
               T["st_mneg"][:, sl], ALU.mult)
            TS("gp", T["st_ssq2"][:, sl], T["st_ssq"][:, sl], 1.0 / D, None,
               ALU.mult)
            tg = TT("gp", T["st_var"][:, sl], T["st_ssq2"][:, sl],
                    T["st_var"][:, sl], ALU.subtract, mark=True)
            P.wait("sc", tg)
            AV(T["st_r"][:, sl], T["st_var"][:, sl], ACT.Ln, bias=T["cb"][:, 1:2])
            t = AV(T["st_r"][:, sl], T["st_r"][:, sl], ACT.Exp, scale=-0.5,
                   mark=True)
            P.wait("gp", t)
            CP("gp", T["st_rv"][:, sl], T["st_r"][:, sl])
            deps[j] = TT("gp", T["st_mrneg"][:, sl], T["st_mneg"][:, sl],
                         T["st_rv"][:, sl], ALU.mult, mark=True)
        return deps

    def ln_stats(x_ap, j):
        sl = slice(j, j + 1)
        RED(T["st_sum"][:, sl], x_ap, "sum")
        tr = STT("ve", af(6144, 1024), x_ap, 1.0, x_ap, ALU.bypass, ALU.mult,
                 accum=T["st_ssq"][:, sl], mark=True)
        P.wait("gp", tr)
        TS("gp", T["st_mneg"][:, sl], T["st_sum"][:, sl], -1.0 / D, None,
           ALU.mult)
        TT("gp", T["st_var"][:, sl], T["st_mneg"][:, sl], T["st_mneg"][:, sl],
           ALU.mult)
        TS("gp", T["st_ssq2"][:, sl], T["st_ssq"][:, sl], 1.0 / D, None,
           ALU.mult)
        tg = TT("gp", T["st_var"][:, sl], T["st_ssq2"][:, sl],
                T["st_var"][:, sl], ALU.subtract, mark=True)
        P.wait("sc", tg)
        AV(T["st_r"][:, sl], T["st_var"][:, sl], ACT.Ln, bias=T["cb"][:, 1:2])
        t = AV(T["st_r"][:, sl], T["st_r"][:, sl], ACT.Exp, scale=-0.5,
               mark=True)
        P.wait("gp", t)
        CP("gp", T["st_rv"][:, sl], T["st_r"][:, sl])
        return TT("gp", T["st_mrneg"][:, sl], T["st_mneg"][:, sl],
                  T["st_rv"][:, sl], ALU.mult, mark=True)

    def normalize_zt(x_ap, j, dep):
        P.wait("sc", dep)
        return AV(af(ZT, 1024), x_ap, ACT.Identity,
                  bias=T["st_mrneg"][:, j:j + 1], scale=T["st_rv"][:, j:j + 1],
                  mark=True)

    def tm_to_fm(src_off, j, evac, src_tensor="AF"):
        tt = None
        for half in range(2):
            bank = next_bank("pt")
            for m in range(4):
                c = half * 4 + m
                src = af(src_off + c * 128, 128)
                tt = TR(PSUM[bank][:, m * 128:(m + 1) * 128], src, mark=(m == 3))
            evac(PSUM[bank][:, 0:512], half * 4, j, tt, bank)
        return tt

    def fm_to_tm_add(src_off, c, dst_tensor="x_tm"):
        bank = next_bank("pt")
        tt = None
        for j in range(NTT):
            tt = TR(PSUM[bank][:, j * 128:(j + 1) * 128],
                    af(src_off + j * 128, 128), mark=(j == NTT - 1))
        P.wait("ve", tt)
        dst = ap3(dst_tensor, c * 128, 1024, NTT, 128)
        tv = TT("ve", dst, dst, psum3(bank, NTT, 128), ALU.add, mark=True)
        free_bank(bank, tv)

    def mm_pass(w, rhs_fn, epi, rhs_dep, rhs_dep_fn=None):
        K, O, _ = W_SPECS[w]
        Wg = wg_of(K)
        nk = K // 128
        nog = O // Wg
        for og in range(nog):
            sl = og % 2
            wb = T["wbuf0" if sl == 0 else "wbuf1"]
            tok = P.dma("wq",
                        lambda _wb=wb, _n=nk * Wg: _wb[:, 0:_n],
                        lambda _w=w, _i=og * nk, _n=nk:
                            dram["scr_" + _w][_i:_i + _n].rearrange("n p w -> p n w"),
                        pre=[wbuf_free[sl], prep_w_done.get(w)])
            P.wait("te", tok)
            if rhs_dep_fn is None:
                if isinstance(rhs_dep, list):
                    for d in rhs_dep:
                        P.wait("te", d)
                else:
                    P.wait("te", rhs_dep)
            for o in range(Wg // 128):
                oc = og * (Wg // 128) + o
                bank = next_bank("pm")
                for kc in range(nk):
                    if rhs_dep_fn is not None:
                        P.wait("te", rhs_dep_fn(kc))
                    MM(PSUM[bank][:, 0:512],
                       wb[:, kc * Wg + o * 128: kc * Wg + (o + 1) * 128],
                       rhs_fn(kc), start=(kc == 0), stop=(kc == nk - 1),
                       mark=(kc == nk - 1))
                epi(bank, oc, P.tok("te"))
            wbuf_free[sl] = P.tok("te")

    x_store_dep = [None]
    final_toks = []

    def load_x(blk):
        seq, t0 = blk // (T_FULL // TB), (blk % (T_FULL // TB)) * TB
        toks = []
        for j in range(NTT):
            toks.append(P.dma("xq",
                lambda _j=j: T["x_tm"][:, _j * 1024:(_j + 1) * 1024],
                lambda _j=j, _s=seq, _t=t0:
                    dram["x"][_s, _t + _j * 128:_t + (_j + 1) * 128, :],
                pre=x_store_dep[0]))
        return toks[-2:]

    def store_x(blk):
        seq, t0 = blk // (T_FULL // TB), (blk % (T_FULL // TB)) * TB
        dep = P.tok("ve")
        toks = []
        for j in range(NTT):
            toks.append(P.dma("oq",
                lambda _j=j, _s=seq, _t=t0:
                    dram["out"][_s, _t + _j * 128:_t + (_j + 1) * 128, :],
                lambda _j=j: T["x_tm"][:, _j * 1024:(_j + 1) * 1024],
                pre=dep))
        x_store_dep[0] = toks[-1]
        final_toks.extend(toks[-2:])

    def tap_dump(blk, phase):
        if (blk, phase) not in taps:
            return
        tap = taps[(blk, phase)]
        dep = P.tok("ve")
        toks = []
        for j in range(NTT):
            toks.append(P.dma("oq",
                lambda _j=j, _t=tap: _t[_j * 128:(_j + 1) * 128, :],
                lambda _j=j: T["x_tm"][:, _j * 1024:(_j + 1) * 1024],
                pre=dep))
        for t_ in toks[-2:]:
            P.wait("ve", t_)
        final_toks.extend(toks[-2:])

    def phase_A(blk, x_dep):
        seq_start = blk % (T_FULL // TB) == 0
        if seq_start:
            for c in range(NCD):
                MS("gp", af(A_PB + c * 513, 513)[:, 0:1], 1.0)
            MS("gp", T["cscarry"][:], 0.0, mark=True)
        kb_done = [None]

        def kb_evac(ps, c_lo, j, tt, bank):
            P.wait("ve", tt)
            dst = bass.AP(T["AB"][:, 0:1].tensor, AB_K + c_lo * 512 + j * 128,
                          [[28672, 128], [512, 4], [1, 128]])
            last = CP("ve", dst, psum3(bank, 4, 128), mark=True)
            free_bank(bank, last)
            kb_done[0] = last

        last_ktr = [None]
        for d in x_dep:
            P.wait("ve", d)
        deps = ln_stats_all(lambda j: T["x_tm"][:, j * 1024:(j + 1) * 1024])
        for j in range(NTT):
            xt = T["x_tm"][:, j * 1024:(j + 1) * 1024]
            tz = normalize_zt(xt, j, deps[j])
            P.wait("ve", tz)
            zt = af(ZT, 1024)
            ta = RED(T["st_am"][:, j:j + 1], zt, "max", absolute=True, mark=True)
            P.wait("gp", ta)
            tg = TS("gp", T["st_am2"][:, j:j + 1], T["st_am"][:, j:j + 1], 1e-5,
                    None, ALU.max, mark=True)
            P.wait("ve", tg)
            tv = RECIP(T["st_scf"][:, j:j + 1], T["st_am2"][:, j:j + 1],
                       mark=True)
            P.wait("gp", tv)
            tg2 = TS("gp", T["st_scf"][:, j:j + 1], T["st_scf"][:, j:j + 1],
                     127.0, None, ALU.mult, mark=True)
            P.wait("sc", tg2)
            if last_ktr[0] is not None:
                P.wait("sc", last_ktr[0])
            t = AV(af(A_KF, 1024), zt, ACT.Identity,
                   scale=T["st_scf"][:, j:j + 1], mark=True)
            P.wait("ve", t)
            CP("ve", T["stri"][:, 0:1024], af(A_KF, 1024))
            tv = TS("ve", af(A_KF, 1024), T["stri"][:, 0:1024], -128.0, 127.0,
                    ALU.max, ALU.min, mark=True)
            P.wait("te", tv)
            last_ktr[0] = tm_to_fm(A_KF, j, kb_evac)

        bank2 = next_bank("pt")
        tprev = None
        P.wait("gp", P.tok("gp"))
        for j in range(NTT):
            sl = j % 2
            rhs = T["rowbuf"][:, sl * 128:(sl + 1) * 128]
            if tprev is not None:
                P.wait("gp", tprev)
            tg = TS("gp", rhs, T["ident"][:], T["st_am2"][:, j:j + 1], None,
                    ALU.mult, mark=True)
            P.wait("te", tg)
            tprev = MM(PSUM[bank2][:, j * 128:(j + 1) * 128], T["ones1"][:],
                       rhs, True, True, mark=True)
        P.wait("ve", tprev)
        tv = TS("ve", T["R"][:], PSUM[bank2][:, 0:512], 1.0 / 127.0, None,
                ALU.mult, mark=True)
        free_bank(bank2, tv)

        kb_dep = kb_done[0]
        epar = [0]

        def epi_dt(bank, oc, te_tok):
            par = epar[0] % 2
            epar[0] += 1
            s0 = af(A_S[par * 2], 512)
            s1 = af(A_S[par * 2 + 1], 512)
            P.wait("ve", te_tok)
            e1 = TT("ve", s0, PSUM[bank][:, 0:512], T["R"][:], ALU.mult,
                    mark=True)
            free_bank(bank, e1)
            tv = TS("ve", s0, s0, T["wsdt"][:, 0:1], T["bdt"][:, oc:oc + 1],
                    ALU.mult, ALU.add, mark=True)
            P.wait("sc", tv)
            AV(s1, s0, ACT.Abs)
            AV(s1, s1, ACT.Exp, scale=-1.0)
            t = AV(s1, s1, ACT.Ln, bias=T["cb"][:, 2:3], mark=True)
            P.wait("ve", t)
            STT("ve", s0, s0, 0.0, s1, ALU.max, ALU.add)
            TS("ve", af(A_DT + oc * 512, 512), s0, 1e-3, 1.0, ALU.max, ALU.min,
               mark=True)

        mm_pass("wdt", lambda kc: ab(AB_K + kc * 512, 512), epi_dt, kb_dep)
        fm_dump(blk, "afterdt")

        last_tanh = {0: None, 1: None}

        def epi_xp(bank, oc, te_tok):
            par = epar[0] % 2
            epar[0] += 1
            s0 = af(A_S[par * 2], 512)
            s1 = af(A_S[par * 2 + 1], 512)
            P.wait("ve", te_tok)
            if last_tanh[par] is not None:
                P.wait("ve", last_tanh[par])
            e1 = TT("ve", s0, PSUM[bank][:, 0:512], T["R"][:], ALU.mult,
                    mark=True)
            free_bank(bank, e1)
            TS("ve", s0, s0, T["wsx"][:, 0:1], T["bx"][:, oc:oc + 1],
               ALU.mult, ALU.add)
            tv = TS("ve", s0, s0, -5.0, 5.0, ALU.max, ALU.min, mark=True)
            if oc < 8:
                TT("ve", af(A_GN + oc * 512, 512), s0,
                   af(A_DT + oc * 512, 512), ALU.mult, mark=True)
            else:
                P.wait("sc", tv)
                last_tanh[par] = AV(af(A_TC + (oc - 8) * 512, 512), s0,
                                    ACT.Tanh, mark=True)

        mm_pass("wx", lambda kc: ab(AB_K + kc * 512, 512), epi_xp, kb_dep)
        fm_dump(blk, "afterxp")

        P.wait("ve", P.tok("sc"))
        last_carry = [None]
        for c in range(NCD):
            if last_carry[0] is not None:
                P.wait("ve", last_carry[0])
            pb = T["AF"][:, A_PB + c * 513: A_PB + (c + 1) * 513]
            dtc = af(A_DT + c * 512, 512)
            TS("ve", af(A_S[0], 512), dtc, T["acol"][:, c:c + 1], -10.0,
               ALU.mult, ALU.max)
            tv = TS("ve", af(A_S[0], 512), af(A_S[0], 512), 0.0, None, ALU.min,
                    mark=True)
            P.wait("sc", tv)
            if c == 0:
                fm_dump(blk, "c0_ua")
            t = AV(af(A_S[1], 512), af(A_S[0], 512), ACT.Exp, mark=True)
            P.wait("ve", t)
            if c == 0:
                fm_dump(blk, "c0_exp")
            SCAN(pb[:, 1:513], af(A_S[1], 512), af(A_S[1], 512), pb[:, 0:1],
                 ALU.mult, ALU.bypass)
            if c == 0:
                fm_dump(blk, "c0_scan")
            RECIP(af(A_S[2], 512), pb[:, 0:512])
            TT("ve", af(A_S[2], 512), af(A_GN + c * 512, 512), af(A_S[2], 512),
               ALU.mult)
            if c == 0:
                fm_dump(blk, "c0_g")
            tsc = SCAN(af(A_S[3], 512), af(A_S[2], 512), af(A_S[2], 512),
                       T["cscarry"][:, c:c + 1], ALU.add, ALU.bypass, mark=True)
            if c == 0:
                fm_dump(blk, "c0_cs")
            tv = TT("ve", af(A_S[4], 512), pb[:, 1:513], af(A_S[3], 512),
                    ALU.mult, mark=True)
            P.wait("gp", tv)
            CP("gp", T["cscarry"][:, c:c + 1], af(A_S[3] + 511, 1))
            last_carry[0] = CP("gp", pb[:, 0:1], pb[:, 512:513], mark=True)
            TT("ve", af(A_S[4], 512), af(A_S[4], 512), af(A_TC + c * 512, 512),
               ALU.mult)
            TT("ve", af(A_S[5], 512), af(A_S[4], 512), af(A_S[4], 512),
               ALU.subtract)
            TS("ve", T["strm"][:], af(A_S[5], 512), 0.0, None, ALU.is_equal)
            tv = SEL(af(A_S[5], 512), T["strm"][:], af(A_S[4], 512),
                     T["zeros"][:], mark=True)
            if c == 0:
                fm_dump(blk, "c0_sel")
            P.wait("te", tv)
            fm_to_tm_add(A_S[5], c)

    def fm_dump(blk, point="endA"):
        if blk != 0 or not fmtaps:
            return
        sel = {t: v for t, v in fmtaps.items() if v[3] == point}
        if not sel:
            return
        dep = P.tok("ve")
        toks = []
        for tag, (tap, off, cols, _pt) in sel.items():
            if isinstance(off, str) and off.startswith("ab:"):
                src_fn = (lambda _o=int(off[3:]), _c=cols: ab(_o, _c))
            elif isinstance(off, str):
                src_fn = (lambda _n=off, _c=cols: T[_n][:, 0:_c])
            else:
                src_fn = (lambda _o=off, _c=cols: af(_o, _c))
            toks.append(P.dma("oq", lambda _t=tap: _t, src_fn, pre=dep))
        for t_ in toks[-2:]:
            P.wait("ve", t_)
        final_toks.extend(toks[-2:])

    def phase_B(blk):
        xn_done = [None]

        def xn_evac(ps, c_lo, j, tt, bank):
            P.wait("ve", tt)
            dst = bass.AP(T["AB"][:, 0:1].tensor, AB_XN + c_lo * 512 + j * 128,
                          [[28672, 128], [512, 4], [1, 128]])
            last = CP("ve", dst, psum3(bank, 4, 128), mark=True)
            free_bank(bank, last)
            xn_done[0] = last

        last_tr = [None]
        deps = ln_stats_all(lambda j: T["x_tm"][:, j * 1024:(j + 1) * 1024])
        for j in range(NTT):
            xt = T["x_tm"][:, j * 1024:(j + 1) * 1024]
            if last_tr[0] is not None:
                P.wait("sc", last_tr[0])
            tz = normalize_zt(xt, j, deps[j])
            P.wait("te", tz)
            last_tr[0] = tm_to_fm(ZT, j, xn_evac)

        qp = [0]
        qdeps = {0: None, 1: None, 2: None, 3: None}
        last_g = [None]
        hq_tok_by_kc = {}

        def epi_h(bank, oc, te_tok):
            par = qp[0] % 4
            qp[0] += 1
            hln = af(B_HB[par], 512)
            sa, sb_ = af(B_QV[par][0], 512), af(B_QV[par][1], 512)
            sG = af(B_QG[par], 512)
            si = T["stri"][:, par * 512:(par + 1) * 512]
            dst = ab(AB_HQ + oc * 512, 512)
            P.wait("ve", te_tok)
            if qdeps[par] is not None:
                P.wait("sc", qdeps[par])
                P.wait("ve", qdeps[par])
            tv0 = TS("ve", hln, PSUM[bank][:, 0:512], T["b1"][:, oc:oc + 1],
                     2.0 ** -6, ALU.add, ALU.max, mark=True)
            P.wait("sc", tv0)
            AV(sa, hln, ACT.Ln)
            AV(si, sa, ACT.Identity, scale=-INV_LN2, bias=T["cb"][:, 4:5])
            AV(sa, si, ACT.Exp, scale=LN2, bias=T["cb"][:, 5:6])
            t2 = AV(sb_, si, ACT.Exp, scale=-LN2, bias=T["cb"][:, 6:7],
                    mark=True)
            P.wait("ve", t2)
            tk = STT("ve", si, PSUM[bank][:, 0:512], T["b1"][:, oc:oc + 1],
                     sa, ALU.add, ALU.mult, mark=True)
            free_bank(bank, tk)
            TT("ve", sb_, si, sb_, ALU.mult)
            tq = TS("ve", sG, sb_, 0.0, None, ALU.max, mark=True)
            P.wait("sc", tq)
            to = AV(sb_, sb_, ACT.Relu, bias=T["cb"][:, 3:4], mark=True)
            P.wait("ve", to)
            tg = STT("ve", dst, sb_, BIG, sG, ALU.mult, ALU.add, mark=True)
            qdeps[par] = tg
            last_g[0] = tg
            hq_tok_by_kc[oc] = tg

        mm_pass("w1", lambda kc: ab(AB_XN + kc * 512, 512), epi_h, xn_done[0])

        op = [0]
        hq_toks = dict(hq_tok_by_kc)

        def epi_o(bank, oc, te_tok):
            par = op[0] % 4
            op[0] += 1
            hb = af(B_HB[par], 512)
            P.wait("sc", te_tok)
            if qdeps[par] is not None:
                P.wait("ve", qdeps[par])
                P.wait("sc", qdeps[par])
            e1 = AV(hb, PSUM[bank][:, 0:512], ACT.Identity,
                    bias=T["b2q"][:, oc:oc + 1], mark=True)
            free_bank(bank, e1)
            _, tg = quant_chain(af(B_S[par], 512), hb, 512,
                                man=10, clamp=14, maxv=65504.0, pos=False,
                                sv=B_QV[par], sg=B_QG[par], i32_off=par * 512)
            qdeps[par] = tg
            P.wait("te", tg)
            fm_to_tm_add(B_S[par], oc)

        mm_pass("w2", lambda kc: ab(AB_HQ + kc * 512, 512), epi_o, None,
                rhs_dep_fn=lambda kc: hq_toks[kc])

    def phase_C(blk):
        seq_start = blk % (T_FULL // TB) == 0
        if seq_start:
            MS("gp", T["ccarry"][:], 0.0, mark=True)
        xn_done = [None]

        def xn_evac(ps, c_lo, j, tt, bank):
            P.wait("ve", tt)
            for m in range(4):
                c = c_lo + m
                dst32 = af(C_XN + c * 512 + j * 128, 128)
                TS("ve", dst32, ps[:, m * 128:(m + 1) * 128],
                   T["ng"][:, c:c + 1], T["nb"][:, c:c + 1], ALU.mult, ALU.add,
                   mark=(m == 3))
            free_bank(bank, P.tok("ve"))
            src32 = bass.AP(T["AF"][:, 0:1].tensor, C_XN + c_lo * 512 + j * 128,
                            [[22016, 128], [512, 4], [1, 128]])
            dstb = bass.AP(T["AB"][:, 0:1].tensor, AB_XN + c_lo * 512 + j * 128,
                           [[28672, 128], [512, 4], [1, 128]])
            last = CP("ve", dstb, src32, mark=True)
            xn_done[0] = last

        last_tr = [None]
        deps = ln_stats_all(lambda j: T["x_tm"][:, j * 1024:(j + 1) * 1024])
        for j in range(NTT):
            xt = T["x_tm"][:, j * 1024:(j + 1) * 1024]
            if last_tr[0] is not None:
                P.wait("sc", last_tr[0])
            tz = normalize_zt(xt, j, deps[j])
            P.wait("te", tz)
            last_tr[0] = tm_to_fm(ZT, j, xn_evac)
        CP("gp", T["wtmp"][:, 4:8], T["st_rv"][:, 0:4])
        sdep = CP("gp", T["wtmp"][:, 0:4], T["st_mrneg"][:, 0:4], mark=True)

        lh_done = [None]

        def epi_fr(bank, oc, te_tok):
            s0 = af(C_S[0], 512)
            s1 = af(C_S[1], 512)
            P.wait("sc", te_tok)
            P.wait("sc", lh_done[0])
            th = AV(s0, PSUM[bank][:, 0:512], ACT.Tanh, scale=0.5,
                    bias=T["frbh"][:, oc:oc + 1])
            tl = AV(s1, s0, ACT.Identity, scale=0.5, bias=T["cb"][:, 4:5], mark=True)
            free_bank(bank, tl)
            P.wait("ve", tl)
            xnc = af(C_XN + oc * 512, 512)
            TT("ve", ab(AB_LOW + oc * 512, 512), xnc, s1, ALU.mult)
            TS("ve", s1, s1, -1.0, 1.0, ALU.mult, ALU.add)
            lh_done[0] = TT("ve", ab(AB_HIGH + oc * 512, 512), xnc, s1,
                            ALU.mult, mark=True)

        fm_dump(blk, "C_xn")
        mm_pass("fr", lambda kc: ab(AB_XN + kc * 512, 512), epi_fr, xn_done[0])
        fm_dump(blk, "C_fr")

        ssd_done = [None]
        ssd_toks = {}
        mam_toks = {}
        nk = NCD
        cpbuf_free = [lh_done[0]]

        def cp_wslice(kc, o):
            return ab(AB_XN + kc * 512 + o * 128, 128)

        def scan_block():
            specs = [("dp", 0), ("bp", 1), ("cp", 2)]
            for og in range(2):
                dtoks = {}
                for w, sl in specs:
                    if sl == 2:
                        dtoks[w] = P.dma("wq",
                            lambda: ab(AB_XN, nk * 512),
                            lambda _w=w, _i=og * nk: dram["scr_" + _w][_i:_i + nk]
                                .rearrange("n p w -> p n w"),
                            pre=[cpbuf_free[0], prep_w_done.get(w)])
                    else:
                        dtoks[w] = P.dma("wq",
                            lambda _wb="wbuf0" if sl == 0 else "wbuf1":
                                T[_wb][:, 0:nk * 512],
                            lambda _w=w, _i=og * nk: dram["scr_" + _w][_i:_i + nk]
                                .rearrange("n p w -> p n w"),
                            pre=[wbuf_free[sl], prep_w_done.get(w)])
                for o in range(4):
                    c = og * 4 + o
                    banks = {}
                    for w, sl in specs:
                        P.wait("te", dtoks[w])
                        bank = next_bank("pm")
                        for kc in range(nk):
                            if sl == 2:
                                lhsT = cp_wslice(kc, o)
                                rhs = ab(AB_HIGH + kc * 512, 512)
                            else:
                                wb = T["wbuf0" if sl == 0 else "wbuf1"]
                                lhsT = wb[:, kc * 512 + o * 128: kc * 512 + (o + 1) * 128]
                                rhs = ab(AB_LOW + kc * 512, 512)
                            MM(PSUM[bank][:, 0:512], lhsT, rhs,
                               start=(kc == 0), stop=(kc == nk - 1),
                               mark=(kc == nk - 1))
                        banks[w] = (bank, P.tok("te"))
                    scan_chunk(c, banks)
                for w, sl in specs:
                    if sl == 2:
                        cpbuf_free[0] = P.tok("te")
                    else:
                        wbuf_free[sl] = P.tok("te")

        carry_guard = [None]

        def scan_chunk(c, banks):
            s0, s1, s2, s3, s4 = (af(C_S[i], 512) for i in range(5))
            if carry_guard[0] is not None:
                P.wait("ve", carry_guard[0])
            xnc = af(C_XN + c * 512, 512)
            bank_a, ta = banks["dp"]
            bank_b, tb = banks["bp"]
            bank_c, tc = banks["cp"]
            P.wait("sc", tc)
            P.wait("sc", ssd_done[0])
            AV(s0, PSUM[bank_a][:, 0:512], ACT.Tanh, scale=0.5,
               bias=T["dpbh"][:, c:c + 1])
            AV(s0, s0, ACT.Identity, scale=0.5, bias=T["cb"][:, 4:5])
            AV(s1, PSUM[bank_b][:, 0:512], ACT.Tanh, bias=T["bpb"][:, c:c + 1])
            t3 = AV(s2, PSUM[bank_c][:, 0:512], ACT.Tanh,
                    bias=T["cpb"][:, c:c + 1], mark=True)
            free_bank(bank_a, t3)
            free_bank(bank_b, t3)
            free_bank(bank_c, t3)
            P.wait("ve", t3)
            TT("ve", s1, s1, xnc, ALU.mult)
            tsc = SCAN(s3, s0, s1, T["ccarry"][:, c:c + 1], ALU.mult, ALU.add,
                       mark=True)
            P.wait("gp", tsc)
            carry_guard[0] = CP("gp", T["ccarry"][:, c:c + 1],
                                af(C_S[3] + 511, 1), mark=True)
            TT("ve", s4, s2, s3, ALU.mult)
            ssd32 = af(C_SS + c * 512, 512)
            STT("ve", ssd32, xnc, T["skp"][:, c:c + 1], s4, ALU.mult, ALU.add)
            ssd_done[0] = CP("ve", ab(AB_SSD + c * 512, 512), ssd32, mark=True)
            ssd_toks[c] = ssd_done[0]

        scan_block()
        fm_dump(blk, "C_scan")

        gel_done = [None]
        gel_toks = {}

        def epi_m1(bank, oc, te_tok):
            P.wait("sc", te_tok)
            t = AV(ab(AB_GEL + oc * 512, 512), PSUM[bank][:, 0:512], ACT.Gelu,
                   bias=T["m1b"][:, oc:oc + 1], mark=True)
            free_bank(bank, t)
            gel_done[0] = t
            gel_toks[oc] = t

        mm_pass("m1", lambda kc: ab(AB_LOW + kc * 512, 512), epi_m1, lh_done[0])

        mam_done = [None]

        def epi_m2(bank, oc, te_tok):
            P.wait("sc", te_tok)
            e1 = AV(af(C_MB + oc * 512, 512), PSUM[bank][:, 0:512],
                    ACT.Identity, bias=T["m2b"][:, oc:oc + 1], mark=True)
            free_bank(bank, e1)
            mam_done[0] = AV(ab(AB_MAM + oc * 512, 512),
                             af(C_MB + oc * 512, 512), ACT.Identity, mark=True)
            mam_toks[oc] = mam_done[0]

        mm_pass("m2", lambda kc: ab(AB_GEL + kc * 512, 512), epi_m2, None,
                rhs_dep_fn=lambda kc: gel_toks[kc])
        fm_dump(blk, "C_m2")


        gdep = [None]

        def epi_gate(bank, oc, te_tok):
            s0 = af(C_S[0], 512)
            s1 = af(C_S[1], 512)
            P.wait("sc", te_tok)
            P.wait("sc", gdep[0])
            th = AV(s0, PSUM[bank][:, 0:512], ACT.Tanh, scale=0.5,
                    bias=T["gabh"][:, oc:oc + 1])
            th = AV(s0, s0, ACT.Identity, scale=0.5, bias=T["cb"][:, 4:5], mark=True)
            free_bank(bank, th)
            P.wait("ve", th)
            TT("ve", s1, af(C_SS + oc * 512, 512), af(C_MB + oc * 512, 512),
               ALU.subtract)
            TT("ve", s1, s0, s1, ALU.mult)
            tv = TT("ve", s1, s1, af(C_MB + oc * 512, 512), ALU.add, mark=True)
            gdep[0] = tv
            P.wait("te", tv)
            for j in range(NTT):
                bk = next_bank("pt")
                tt = TR(PSUM[bk][:, 0:128], af(C_S[1] + j * 128, 128), mark=True)
                P.wait("ve", tt)
                tv2 = CP("ve", af(C_Y + j * 1024 + oc * 128, 128),
                         PSUM[bk][:, 0:128], mark=True)
                free_bank(bk, tv2)
                gdep[0] = tv2

        def gate_rhs(kc):
            if kc < 8:
                return ab(AB_SSD + kc * 512, 512)
            return ab(AB_MAM + (kc - 8) * 512, 512)

        mm_pass("gate", gate_rhs, epi_gate, None,
                rhs_dep_fn=lambda kc: ssd_toks[kc] if kc < 8 else mam_toks[kc - 8])
        fm_dump(blk, "C_gate")

        P.wait("sc", sdep)
        prev_use = None
        for j in range(NTT):
            xt = T["x_tm"][:, j * 1024:(j + 1) * 1024]
            yt = af(C_Y + j * 1024, 1024)
            if prev_use is not None:
                P.wait("sc", prev_use)
            t = AV(af(ZT, 1024), xt, ACT.Identity,
                   scale=T["wtmp"][:, 4 + j:5 + j], bias=T["wtmp"][:, j:j + 1],
                   mark=True)
            P.wait("ve", t)
            TT("ve", yt, yt, af(ZT, 1024), ALU.add)
            prev_use = TT("ve", xt, xt, yt, ALU.add, mark=True)

        yln_done = [None]

        def yln_evac(ps, c_lo, j, tt, bank):
            P.wait("ve", tt)
            last = None
            for m in range(4):
                c = c_lo + m
                last = TS("ve", ab(AB_XN + c * 512 + j * 128, 128),
                          ps[:, m * 128:(m + 1) * 128],
                          T["fng"][:, c:c + 1], T["fnb"][:, c:c + 1],
                          ALU.mult, ALU.add, mark=(m == 3))
            free_bank(bank, last)
            yln_done[0] = last

        last_tr2 = [None]
        deps = ln_stats_all(lambda j: af(C_Y + j * 1024, 1024))
        for j in range(NTT):
            yt_off = C_Y + j * 1024
            if last_tr2[0] is not None:
                P.wait("sc", last_tr2[0])
            tz = normalize_zt(af(yt_off, 1024), j, deps[j])
            P.wait("te", tz)
            last_tr2[0] = tm_to_fm(ZT, j, yln_evac)

        hgel_done = [None]

        def epi_f1(bank, oc, te_tok):
            P.wait("sc", te_tok)
            t = AV(ab(AB_HQ + oc * 512, 512), PSUM[bank][:, 0:512], ACT.Gelu,
                   bias=T["f1b"][:, oc:oc + 1], mark=True)
            free_bank(bank, t)
            hgel_done[0] = t

        fm_dump(blk, "C_y")
        mm_pass("f1", lambda kc: ab(AB_XN + kc * 512, 512), epi_f1, yln_done[0])

        def epi_f2(bank, oc, te_tok):
            P.wait("sc", te_tok)
            e1 = AV(af(C_S[5], 512), PSUM[bank][:, 0:512], ACT.Identity,
                    bias=T["f2b"][:, oc:oc + 1], mark=True)
            free_bank(bank, e1)
            P.wait("te", e1)
            fm_to_tm_add(C_S[5], oc)

        mm_pass("f2", lambda kc: ab(AB_HQ + kc * 512, 512), epi_f2, hgel_done[0])

    prep()
    prep2_done = [None]
    for blk in range(N_BLK):
        x_dep = load_x(blk)
        tap_dump(blk, "load")
        phase_A(blk, x_dep)
        if blk == 0:
            prep2_done[0] = prep2()
        tap_dump(blk, "A")
        if blk == 0:
            P.wait("ve", prep2_done[0])
            P.wait("sc", prep2_done[0])
        phase_B(blk)
        tap_dump(blk, "B")
        phase_C(blk)
        tap_dump(blk, "C")
        phase_B(blk)
        tap_dump(blk, "D")
        store_x(blk)

    n_oq = P.q["oq"]
    if n_oq:
        P.wait("sy", ("oq", 0, ((n_oq + 1) // 2) * 16))
        P.wait("sy", ("oq", 1, (n_oq // 2) * 16))

    with stack:
        sems = {}
        with contextlib.ExitStack() as semstack:
            for e in ENGINES:
                sems[e] = semstack.enter_context(nc.semaphore("sem_" + e))
            dsems = {}
            for c in DMA_CLASSES:
                for i in (0, 1):
                    dsems[(c, i)] = semstack.enter_context(
                        nc.semaphore(f"dsem_{c}{i}"))
            P.sem = sems
            P.dsem = dsems
            with nc.Block() as block:
                P.eng = {"sy": nc.sync, "ve": nc.vector, "sc": nc.scalar,
                         "te": nc.tensor, "gp": nc.gpsimd}

                @block.sync
                def _(eng):
                    for f in P.ops["sy"]:
                        f()

                @block.vector
                def _(eng):
                    for f in P.ops["ve"]:
                        f()

                @block.scalar
                def _(eng):
                    for f in P.ops["sc"]:
                        f()

                @block.tensor
                def _(eng):
                    for f in P.ops["te"]:
                        f()

                @block.gpsimd
                def _(eng):
                    for f in P.ops["gp"]:
                        f()
    return nc



def kernel(**inputs):
    g = host_prepare(inputs)
    x = np.asarray(inputs["L0_attn_a_x"], np.float32)
    nc = build_program()
    in_maps = []
    for core in range(N_CORES):
        m = dict(g)
        m["x"] = np.ascontiguousarray(x[core * B_LOC:(core + 1) * B_LOC])
        in_maps.append(m)
    res = run_bass_kernel_spmd(nc, in_maps, core_ids=list(range(N_CORES)))
    out = np.concatenate([np.asarray(r["out"]) for r in res.results], axis=0)
    return out.reshape(B_FULL, T_FULL, D)


# version 07
# speedup vs baseline: 1.0033x; 1.0033x over previous
import contextlib
import numpy as np
import ml_dtypes
import concourse.bass as bass
import concourse.mybir as mybir
from concourse.bass_utils import run_bass_kernel_spmd

F32 = mybir.dt.float32
BF16 = mybir.dt.bfloat16
I32 = mybir.dt.int32
U8 = mybir.dt.uint8
ALU = mybir.AluOpType
ACT = mybir.ActivationFunctionType
AX = mybir.AxisListType
bf16 = ml_dtypes.bfloat16

D = 1024
T_FULL = 2048
B_FULL = 16
N_CORES = 8
B_LOC = B_FULL // N_CORES
TB = 512
N_BLK = B_LOC * (T_FULL // TB)
NTT = TB // 128
NCD = D // 128

LN2 = float(np.log(2.0))
INV_LN2 = float(1.0 / np.log(2.0))
BIG = 3.0e38

W_SPECS = {
    "wdt":  (1024, 1024, "sign"),
    "wx":   (1024, 2048, "sign"),
    "w1":   (1024, 4096, "cast"),
    "w2":   (4096, 1024, "e4m3"),
    "fr":   (1024, 1024, "cast"),
    "dp":   (1024, 1024, "cast"),
    "bp":   (1024, 1024, "cast"),
    "cp":   (1024, 1024, "cast"),
    "m1":   (1024, 2048, "cast"),
    "m2":   (2048, 1024, "cast"),
    "gate": (2048, 1024, "cast"),
    "f1":   (1024, 4096, "cast"),
    "f2":   (4096, 1024, "cast"),
}

BIAS_TILES = [("bdt", 8), ("bx", 16), ("alog", 8), ("ng", 8), ("nb", 8),
              ("frb", 8), ("dpb", 8), ("bpb", 8), ("cpb", 8), ("skp", 8),
              ("m1b", 16), ("m2b", 8), ("gab", 8), ("fng", 8), ("fnb", 8),
              ("f1b", 32), ("f2b", 8), ("b1", 32), ("b2", 8)]


def wg_of(K):
    return 512 if K <= 2048 else 256



def _pieces(w_ko, Wg):
    K, O = w_ko.shape
    nk = K // 128
    nog = O // Wg
    out = np.empty((nog * nk, 128, Wg), np.float32)
    i = 0
    for og in range(nog):
        for kc in range(nk):
            out[i] = w_ko[kc * 128:(kc + 1) * 128, og * Wg:(og + 1) * Wg]
            i += 1
    return np.ascontiguousarray(out)


def _coltile(v, n):
    return np.ascontiguousarray(np.asarray(v, np.float32).reshape(n, 128).T)


def host_prepare(inputs):
    f = np.float32
    g = {}
    g["wdt"] = _pieces(np.asarray(inputs["L0_attn_a_W_dt"], f).T, wg_of(1024))
    g["wx"] = _pieces(np.asarray(inputs["L0_attn_a_W_x"], f).T, wg_of(1024))
    g["w1"] = _pieces(np.asarray(inputs["L0_ffn_w1"], f).T, wg_of(1024))
    g["w2"] = _pieces(np.asarray(inputs["L0_ffn_w2"], f).T, wg_of(4096))
    g["fr"] = _pieces(np.asarray(inputs["L0_attn_b_fr_w"], f), wg_of(1024))
    g["dp"] = _pieces(np.asarray(inputs["L0_attn_b_dp_w"], f), wg_of(1024))
    g["bp"] = _pieces(np.asarray(inputs["L0_attn_b_bp_w"], f), wg_of(1024))
    g["cp"] = _pieces(np.asarray(inputs["L0_attn_b_cp_w"], f), wg_of(1024))
    g["m1"] = _pieces(np.asarray(inputs["L0_attn_b_m1_w"], f), wg_of(1024))
    g["m2"] = _pieces(np.asarray(inputs["L0_attn_b_m2_w"], f), wg_of(2048))
    g["gate"] = _pieces(np.asarray(inputs["L0_attn_b_gate_w"], f), wg_of(2048))
    g["f1"] = _pieces(np.asarray(inputs["L0_attn_b_ffn1_w"], f), wg_of(1024))
    g["f2"] = _pieces(np.asarray(inputs["L0_attn_b_ffn2_w"], f), wg_of(4096))
    g["bdt"] = _coltile(inputs["L0_attn_a_b_dt"], 8)
    g["bx"] = _coltile(inputs["L0_attn_a_b_x"], 16)
    g["alog"] = _coltile(inputs["L0_attn_a_A_log"], 8)
    g["ng"] = _coltile(inputs["L0_attn_b_norm_g"], 8)
    g["nb"] = _coltile(inputs["L0_attn_b_norm_b"], 8)
    g["frb"] = _coltile(inputs["L0_attn_b_fr_b"], 8)
    g["dpb"] = _coltile(inputs["L0_attn_b_dp_b"], 8)
    g["bpb"] = _coltile(inputs["L0_attn_b_bp_b"], 8)
    g["cpb"] = _coltile(inputs["L0_attn_b_cp_b"], 8)
    g["skp"] = _coltile(inputs["L0_attn_b_skip"], 8)
    g["m1b"] = _coltile(inputs["L0_attn_b_m1_b"], 16)
    g["m2b"] = _coltile(inputs["L0_attn_b_m2_b"], 8)
    g["gab"] = _coltile(inputs["L0_attn_b_gate_b"], 8)
    g["fng"] = _coltile(inputs["L0_attn_b_ffn_g"], 8)
    g["fnb"] = _coltile(inputs["L0_attn_b_ffn_bn"], 8)
    g["f1b"] = _coltile(inputs["L0_attn_b_ffn1_b"], 32)
    g["f2b"] = _coltile(inputs["L0_attn_b_ffn2_b"], 8)
    g["b1"] = _coltile(inputs["L0_ffn_b1"], 32)
    g["b2"] = _coltile(inputs["L0_ffn_b2"], 8)
    g["ident"] = np.eye(128, dtype=np.float32)
    return g



ENGINES = ("sy", "ve", "sc", "te", "gp")
DEBUG_POINT = ["endA"]
DMA_CLASSES = ("bq", "sq", "wq", "xq", "pw", "oq")
DMA_ENGINE = {"bq": "sy", "sq": "sy", "wq": "sy", "xq": "sy",
              "pw": "gp", "oq": "gp"}


class Prog:
    def __init__(self):
        self.ops = {e: [] for e in ENGINES}
        self.n = {e: 0 for e in ENGINES}
        self.waited = {e: {} for e in ENGINES}
        self.q = {c: 0 for c in DMA_CLASSES}
        self.eng = None
        self.sem = None
        self.dsem = None

    def emit(self, e, fn):
        self.ops[e].append(fn)
        return None

    def mark(self, e, fn):
        self.n[e] += 1
        tok = self.n[e]
        def wrapped(_fn=fn, _e=e):
            _fn().then_inc(self.sem[_e], 1)
        self.ops[e].append(wrapped)
        return (e, tok)

    def wait(self, e, dep):
        if dep is None:
            return
        if dep[0] in ENGINES:
            kind, tok = dep
            assert tok <= self.n[kind], \
                f"deadlock: {e} waits {kind}@{tok} > {self.n[kind]}"
            if kind == e:
                return
            if self.waited[e].get(kind, 0) >= tok:
                return
            self.waited[e][kind] = tok
            def w(_k=kind, _t=tok, _e=e):
                self.eng[_e].wait_ge(self.sem[_k], _t)
            self.ops[e].append(w)
        else:
            cls, idx, val = dep
            key = (cls, idx)
            if self.waited[e].get(key, 0) >= val:
                return
            self.waited[e][key] = val
            def w(_key=key, _v=val, _e=e):
                self.eng[_e].wait_ge(self.dsem[_key], _v)
            self.ops[e].append(w)

    def dma(self, cls, out_fn, in_fn, pre=None):
        eng = DMA_ENGINE[cls]
        if isinstance(pre, list):
            for p in pre:
                self.wait(eng, p)
        else:
            self.wait(eng, pre)
        i = self.q[cls]
        self.q[cls] += 1
        idx = i % 2
        val = ((i // 2) + 1) * 16
        def d(_o=out_fn, _i=in_fn, _k=(cls, idx), _e=eng):
            self.eng[_e].dma_start(out=_o(), in_=_i()).then_inc(self.dsem[_k], 16)
        self.ops[eng].append(d)
        return (cls, idx, val)

    def tok(self, e):
        return (e, self.n[e])


def build_program(debug_taps=(), debug_fm=()):
    nc = bass.Bass("TRN2", target_bir_lowering=False)
    P = Prog()
    stack = contextlib.ExitStack()

    dram = {}
    dram["x"] = nc.dram_tensor("x", [B_LOC, T_FULL, D], F32, kind="ExternalInput").ap()
    dram["out"] = nc.dram_tensor("out", [B_LOC, T_FULL, D], F32,
                                 kind="ExternalOutput").ap()
    for w, (K, O, _) in W_SPECS.items():
        Wg = wg_of(K)
        npc = (K // 128) * (O // Wg)
        dram[w] = nc.dram_tensor(w, [npc, 128, Wg], F32, kind="ExternalInput").ap()
        dram["scr_" + w] = nc.dram_tensor("scr_" + w, [npc, 128, Wg], BF16,
                                          kind="Internal").ap()
    for b, n in BIAS_TILES:
        dram[b] = nc.dram_tensor(b, [128, n], F32, kind="ExternalInput").ap()
    dram["ident"] = nc.dram_tensor("ident", [128, 128], F32,
                                   kind="ExternalInput").ap()
    fmtaps = {}
    for ent in debug_fm:
        tag, off, cols = ent[0], ent[1], ent[2]
        point = ent[3] if len(ent) > 3 else "endA"
        dt_ = BF16 if (isinstance(off, str) and off.startswith("ab:")) else F32
        fmtaps[tag] = (nc.dram_tensor("fmtap_" + tag, [128, cols], dt_,
                                      kind="ExternalOutput").ap(), off, cols, point)
    taps = {}
    for blk, phase in debug_taps:
        taps[(blk, phase)] = nc.dram_tensor(f"tap_{blk}_{phase}", [TB, D], F32,
                                            kind="ExternalOutput").ap()

    T = {}
    def sbuf(name, cols, dtype=F32):
        T[name] = stack.enter_context(nc.sbuf_tensor("sb_" + name, [128, cols], dtype))

    sbuf("ident", 128)
    sbuf("ones1", 128)
    sbuf("zeros", 512)
    sbuf("R", 512)
    sbuf("rowbuf", 640)
    for b, n in BIAS_TILES:
        sbuf(b, n)
    sbuf("acol", 8)
    sbuf("frbh", 8); sbuf("dpbh", 8); sbuf("gabh", 8)
    sbuf("b2q", 8)
    sbuf("wsdt", 2); sbuf("wsx", 2)
    sbuf("wtmp", 8)
    sbuf("cb", 8)
    for s in ["st_sum", "st_ssq", "st_mneg", "st_var", "st_r", "st_rv",
              "st_mrneg", "st_am", "st_scf", "st_sum2", "st_ssq2", "st_am2"]:
        sbuf(s, 8)
    sbuf("wacc", 16)
    sbuf("ccarry", 8)
    sbuf("cscarry", 8)
    sbuf("x_tm", NTT * 1024)
    sbuf("AF", 22016)
    sbuf("AB", 28672, BF16)
    sbuf("wbuf0", 8192, BF16)
    sbuf("wbuf1", 8192, BF16)
    sbuf("stri", 2048, I32)
    sbuf("strm", 512, U8)

    PSUM = {}
    for p in ["pm0", "pm1", "pm2", "pm3", "pm4", "pm5", "pt0", "pt1"]:
        PSUM[p] = stack.enter_context(nc.psum_tensor(p, [128, 512], F32))

    def af(off, n):
        return T["AF"][:, off:off + n]

    def ab(off, n):
        return T["AB"][:, off:off + n]

    def ap3(tname, off, stride, n, w):
        h = T[tname]
        row = h.shape[1] if hasattr(h, "shape") else None
        hh = h[:, 0:1]
        rowlen = hh.ap[0][0]
        return bass.AP(hh.tensor, off, [[rowlen, 128], [stride, n], [1, w]])

    def psum3(bank, n, w):
        hh = PSUM[bank][:, 0:1]
        rowlen = hh.ap[0][0]
        return bass.AP(hh.tensor, 0, [[rowlen, 128], [w, n], [1, w]])

    ZT = 0
    WSTG = [1024, 3072]
    QV = [(5120, 7168), (9216, 11264)]
    QG = [13312, 15360]
    A_KF = 1024
    A_DT = 2048
    A_GN = 6144
    A_TC = 10240
    A_PB = 14336
    A_S = [18440 + i * 512 for i in range(6)]
    B_HB = [1024 + i * 512 for i in range(4)]
    B_QV = [(3072 + i * 1024, 3584 + i * 1024) for i in range(4)]
    B_QG = [7168 + i * 512 for i in range(4)]
    B_S = [9216 + i * 512 for i in range(4)]
    C_XN = 1024
    C_MB = 5120
    C_SS = 9216
    C_Y = 13312
    C_S = [17408 + i * 512 for i in range(6)]

    WQB = [0, 2048]
    AB_K = 0
    AB_XN = 0
    AB_LOW = 4096
    AB_HIGH = 8192
    AB_GEL = 12288
    AB_MAM = 20480
    AB_SSD = 24576
    AB_HQ = 4096

    def TT(e, out, a, b, op, mark=False):
        f = lambda: P.eng[e].tensor_tensor(out, a, b, op)
        return P.mark(e, f) if mark else P.emit(e, f)

    def TS(e, out, a, s1, s2, op0, op1=None, mark=False):
        if op1 is None:
            f = lambda: P.eng[e].tensor_scalar(out, a, s1, None, op0)
        else:
            f = lambda: P.eng[e].tensor_scalar(out, a, s1, s2, op0, op1)
        return P.mark(e, f) if mark else P.emit(e, f)

    def STT(e, out, a, s, b, op0, op1, accum=None, mark=False):
        if accum is None:
            f = lambda: P.eng[e].scalar_tensor_tensor(out, a, s, b, op0, op1)
        else:
            f = lambda: P.eng[e].scalar_tensor_tensor(out, a, s, b, op0, op1,
                                                      accum_out=accum)
        return P.mark(e, f) if mark else P.emit(e, f)

    def CP(e, out, a, mark=False):
        f = lambda: P.eng[e].tensor_copy(out, a)
        return P.mark(e, f) if mark else P.emit(e, f)

    def AV(out, a, func, bias=0.0, scale=1.0, mark=False):
        f = lambda: P.eng["sc"].activation(out=out, in_=a, func=func,
                                           bias=bias, scale=scale)
        return P.mark("sc", f) if mark else P.emit("sc", f)

    def AVA(out, a, func, accum, mark=False):
        f = lambda: P.eng["sc"].activation(out=out, in_=a, func=func,
                                           accum_out=accum)
        return P.mark("sc", f) if mark else P.emit("sc", f)

    def RED(out, a, op="sum", absolute=False, mark=False):
        if op == "max":
            f = lambda: P.eng["ve"].reduce_max(out, a, axis=AX.X,
                                               apply_absolute_value=absolute)
        else:
            f = lambda: P.eng["ve"].reduce_sum(out, a, axis=AX.X,
                                               apply_absolute_value=absolute)
        return P.mark("ve", f) if mark else P.emit("ve", f)

    def RECIP(out, a, mark=False):
        f = lambda: P.eng["ve"].reciprocal(out, a)
        return P.mark("ve", f) if mark else P.emit("ve", f)

    def SCAN(out, d0, d1, init, op0, op1, mark=False):
        f = lambda: P.eng["ve"].tensor_tensor_scan(out=out, data0=d0, data1=d1,
                                                   initial=init, op0=op0, op1=op1)
        return P.mark("ve", f) if mark else P.emit("ve", f)

    def SEL(out, mask, t_, f_, mark=False):
        f = lambda: P.eng["ve"].select(out, mask, t_, f_)
        return P.mark("ve", f) if mark else P.emit("ve", f)

    def MM(psum, lhsT, rhs, start, stop, mark=False):
        f = lambda: P.eng["te"].matmul(psum, lhsT, rhs, start=start, stop=stop)
        return P.mark("te", f) if mark else P.emit("te", f)

    def TR(psum, a, mark=False):
        f = lambda: P.eng["te"].transpose(psum, a, T["ident"][:])
        return P.mark("te", f) if mark else P.emit("te", f)

    def MS(e, out, val, mark=False):
        f = lambda: P.eng[e].memset(out, val)
        return P.mark(e, f) if mark else P.emit(e, f)

    psum_free = {}
    rot = {"pm": 0, "pt": 0}

    def next_bank(kind):
        names = (["pm0", "pm1", "pm2", "pm3", "pm4", "pm5"] if kind == "pm"
                 else ["pt0", "pt1"])
        b = names[rot[kind] % len(names)]
        rot[kind] += 1
        if b in psum_free:
            P.wait("te", psum_free[b])
        return b

    def free_bank(b, dep):
        psum_free[b] = dep

    wbuf_free = [None, None]

    def quant_chain(dst, src, n, man, clamp, maxv, pos, sv, sg, i32_off):
        sa, sb_ = af(sv[0], n), af(sv[1], n)
        sG = af(sg, n)
        si = T["stri"][:, i32_off:i32_off + n]
        if pos:
            t1 = AV(sb_, src, ACT.Ln, bias=T["cb"][:, 0:1], mark=True)
            c1 = INV_LN2
        else:
            AV(sa, src, ACT.Square)
            t1 = AV(sb_, sa, ACT.Ln, bias=T["cb"][:, 0:1], mark=True)
            c1 = 0.5 * INV_LN2
        narrow = n < 128
        if narrow:
            si2 = T["stri"][:, i32_off + n:i32_off + 2 * n]
            sa2 = af(sg + n, n)
        P.wait("ve", t1)
        if narrow:
            tn = TS("ve", si, sb_, -c1, 0.5, ALU.mult, ALU.add, mark=True)
            P.wait("gp", tn)
            tg0 = CP("gp", si2, si, mark=True)
            P.wait("ve", tg0)
            tv1 = TS("ve", sa, si2, float(clamp), float(man), ALU.min, ALU.add,
                     mark=True)
        else:
            TS("ve", si, sb_, -c1, 0.5, ALU.mult, ALU.add)
            tv1 = TS("ve", sa, si, float(clamp), float(man), ALU.min, ALU.add,
                     mark=True)
        P.wait("sc", tv1)
        AV(sb_, sa, ACT.Exp, scale=LN2)
        t2 = AV(sa, sa, ACT.Exp, scale=-LN2, mark=True)
        P.wait("ve", t2)
        if narrow:
            src_done = TT("ve", si, src, sb_, ALU.mult, mark=True)
            P.wait("gp", src_done)
            CP("gp", si2, si)
            tg1 = CP("gp", sa2, sa, mark=True)
            P.wait("ve", tg1)
            tv2 = TT("ve", sb_, si2, sa2, ALU.mult, mark=True)
            P.wait("gp", tv2)
            TT("gp", sG, sb_, sb_, ALU.mult)
            TS("gp", sG, sG, float(maxv) * float(maxv), None, ALU.is_gt)
            TS("gp", sG, sG, BIG, 1.0, ALU.mult, ALU.add)
            tg = TT("gp", dst, sb_, sG, ALU.mult, mark=True)
            return src_done, tg
        src_done = TT("ve", si, src, sb_, ALU.mult, mark=True)
        tv2 = TT("ve", sb_, si, sa, ALU.mult, mark=True)
        if pos:
            P.wait("sc", tv2)
            ts = AV(sG, sb_, ACT.Relu, bias=T["cb"][:, 3:4], mark=True)
            P.wait("ve", ts)
            tg = STT("ve", dst, sG, BIG, sb_, ALU.mult, ALU.add, mark=True)
        else:
            TS("ve", sG, sb_, float(maxv), -float(maxv), ALU.min, ALU.max)
            TT("ve", sG, sb_, sG, ALU.subtract)
            tg = STT("ve", dst, sG, BIG, sb_, ALU.mult, ALU.add, mark=True)
        return src_done, tg

    stg_free = {0: None, 1: None}
    qb_free = {0: None, 1: None}
    prep_w_done = {}
    qpar = [0]
    qprev = {0: None, 1: None}

    def prep_stream(w, batch_cb):
        K, O, _ = W_SPECS[w]
        Wg = wg_of(K)
        npc = (K // 128) * (O // Wg)
        per = 2048 // Wg
        for bi, i0 in enumerate(range(0, npc, per)):
            n = min(per, npc - i0)
            sl = bi % 2
            tok = P.dma("sq",
                        lambda _s=sl, _n=n, _Wg=Wg: af(WSTG[_s], _n * _Wg),
                        lambda _w=w, _i=i0, _n=n:
                            dram[_w][_i:_i + _n].rearrange("n p w -> p n w"),
                        pre=stg_free[sl])
            P.wait("ve", tok)
            P.wait("sc", tok)
            stg_free[sl] = batch_cb(WSTG[sl], n * Wg, i0, n, sl)

    def wscale_pass(w, wtile):
        K, O, _ = W_SPECS[w]
        nb = [0]
        def cb(off, ncols, i0, n, sl):
            tv = RED(T["wacc"][:, nb[0]:nb[0] + 1], af(off, ncols), "sum",
                     absolute=True, mark=True)
            nb[0] += 1
            return tv
        prep_stream(w, cb)
        assert nb[0] <= 8
        tv = P.tok("ve")
        P.wait("gp", tv)
        tg = CP("gp", T["st_sum2"][:, 0:nb[0]], T["wacc"][:, 0:nb[0]], mark=True)
        P.wait("ve", tg)
        t1 = RED(T["wtmp"][:, 1:2], T["st_sum2"][:, 0:nb[0]], "sum", mark=True)
        P.wait("gp", t1)
        tg = CP("gp", T["wtmp"][:, 0:1], T["wtmp"][:, 1:2], mark=True)
        P.wait("te", tg)
        bank = next_bank("pt")
        tt = TR(PSUM[bank][0:1, 0:128], T["wtmp"][:, 0:1], mark=True)
        P.wait("ve", tt)
        CP("ve", T["rowbuf"][0:1, 0:128], PSUM[bank][0:1, 0:128])
        tv = RED(T["wtmp"][0:1, 2:3], T["rowbuf"][0:1, 0:128], "sum", mark=True)
        free_bank(bank, tv)
        P.wait("gp", tv)
        tg = TS("gp", T["wtmp"][0:1, 3:4], T["wtmp"][0:1, 2:3],
                1.0 / (K * O), 1e-5, ALU.mult, ALU.add, mark=True)
        bank2 = next_bank("pt")
        P.wait("te", tg)
        tm = MM(PSUM[bank2][:, 0:1], T["ones1"][0:1, 0:128],
                T["wtmp"][0:1, 3:4], True, True, mark=True)
        P.wait("ve", tm)
        tv = CP("ve", T[wtile][:, 0:1], PSUM[bank2][:, 0:1], mark=True)
        P.wait("gp", tv)
        tv = TS("gp", T[wtile][:, 1:2], T[wtile][:, 0:1], 0.5, None, ALU.mult,
                mark=True)
        free_bank(bank2, tv)

    def prep_weight(w, K, O, q):
        wt = "wsdt" if w == "wdt" else "wsx"
        def cb(off, ncols, i0, n, sl):
            qb = ab(WQB[sl], ncols)
            if qb_free[sl] is not None:
                P.wait("ve", qb_free[sl])
                P.wait("gp", qb_free[sl])
            src = af(off, ncols)
            if q == "cast":
                done = CP("ve", qb, src, mark=True)
                wdep = done
            elif q == "sign":
                par = qpar[0] % 2
                qpar[0] += 1
                sa = af(QV[par][0], ncols)
                sb_ = af(QV[par][1], ncols)
                if qprev[par] is not None:
                    P.wait("ve", qprev[par])
                STT("ve", sa, src, -1.0, src, ALU.mult, ALU.max)
                tv0 = TS("ve", sa, sa, T[wt][:, 1:2], None, ALU.is_gt, mark=True)
                P.wait("sc", tv0)
                ts = AV(sb_, src, ACT.Sign, mark=True)
                P.wait("ve", ts)
                done = TT("ve", qb, sa, sb_, ALU.mult, mark=True)
                qprev[par] = done
                wdep = done
            else:
                par = qpar[0] % 2
                qpar[0] += 1
                if qprev[par] is not None:
                    P.wait("ve", qprev[par])
                    P.wait("sc", qprev[par])
                done, tg = quant_chain(qb, src, ncols, man=3, clamp=6,
                                       maxv=240.0, pos=False, sv=QV[par],
                                       sg=QG[par], i32_off=0 if par == 0 else 0)
                qprev[par] = tg
                wdep = tg
            wtok = P.dma("pw",
                         lambda _w=w, _i=i0, _n=n:
                             dram["scr_" + _w][_i:_i + _n].rearrange("n p w -> p n w"),
                         lambda _q=qb: _q,
                         pre=wdep)
            qb_free[sl] = wtok
            prep_w_done[w] = wtok
            return done
        prep_stream(w, cb)

    def prep():
        toks = []
        for b, _ in BIAS_TILES:
            toks.append(P.dma("bq", lambda _b=b: T[_b][:], lambda _b=b: dram[_b]))
        toks.append(P.dma("bq", lambda: T["ident"][:], lambda: dram["ident"]))
        last = toks[-1]
        prev = toks[-2]
        for e in ("ve", "sc", "te"):
            P.wait(e, last)
            P.wait(e, prev)
        MS("ve", T["ones1"][:, :], 1.0)
        MS("ve", T["zeros"][:], 0.0)
        MS("ve", T["cb"][:, 0:1], 1e-38)
        MS("ve", T["cb"][:, 1:2], 1e-5)
        MS("ve", T["cb"][:, 2:3], 1.0)
        MS("ve", T["cb"][:, 3:4], -240.0)
        MS("ve", T["cb"][:, 5:6], 3.0 * LN2)
        MS("ve", T["cb"][:, 6:7], -3.0 * LN2)
        cb_tok = MS("ve", T["cb"][:, 4:5], 0.5, mark=True)
        P.wait("sc", cb_tok)
        TS("ve", T["frbh"][:], T["frb"][:], 0.5, None, ALU.mult)
        TS("ve", T["dpbh"][:], T["dpb"][:], 0.5, None, ALU.mult)
        TS("ve", T["gabh"][:], T["gab"][:], 0.5, None, ALU.mult)
        AV(T["st_sum"][:], T["alog"][:], ACT.Abs)
        AV(T["st_ssq"][:], T["st_sum"][:], ACT.Exp, scale=-1.0)
        t = AV(T["st_var"][:], T["st_ssq"][:], ACT.Ln, bias=T["cb"][:, 2:3], mark=True)
        P.wait("gp", t)
        TS("gp", T["st_mneg"][:], T["alog"][:], 0.0, None, ALU.max)
        TT("gp", T["st_mneg"][:], T["st_mneg"][:], T["st_var"][:], ALU.add)
        tv = TS("gp", T["acol"][:], T["st_mneg"][:], -1.0, None, ALU.mult,
                mark=True)
        P.wait("sc", tv)
        P.wait("ve", tv)
        quant_chain(T["b2q"][:], T["b2"][:], 8, man=10, clamp=14, maxv=65504.0,
                    pos=False, sv=QV[0], sg=QG[0], i32_off=0)
        wscale_pass("wdt", "wsdt")
        wscale_pass("wx", "wsx")
        for w in ["wdt", "wx", "w1", "w2"]:
            K, O, q = W_SPECS[w]
            prep_weight(w, K, O, q)

    def prep2():
        stg2 = [ab(4096, 4096).bitcast(F32), ab(8192, 4096).bitcast(F32)]
        out2 = [ab(12288, 2048), ab(14336, 2048)]
        sfree = {0: None, 1: None}
        ofree = {0: None, 1: None}
        for w in ["fr", "dp", "bp", "cp", "m1", "m2", "gate", "f1", "f2"]:
            K, O, _ = W_SPECS[w]
            Wg = wg_of(K)
            npc = (K // 128) * (O // Wg)
            per = 2048 // Wg
            for bi, i0 in enumerate(range(0, npc, per)):
                n = min(per, npc - i0)
                sl = bi % 2
                tok = P.dma("sq",
                            lambda _s=sl, _n=n, _Wg=Wg: stg2[_s][:, 0:_n * _Wg],
                            lambda _w=w, _i=i0, _n=n:
                                dram[_w][_i:_i + _n].rearrange("n p w -> p n w"),
                            pre=sfree[sl])
                P.wait("sc", tok)
                if ofree[sl] is not None:
                    P.wait("sc", ofree[sl])
                ts = AV(out2[sl][:, 0:n * Wg], stg2[sl][:, 0:n * Wg],
                        ACT.Identity, mark=True)
                sfree[sl] = ts
                wtok = P.dma("pw",
                             lambda _w=w, _i=i0, _n=n:
                                 dram["scr_" + _w][_i:_i + _n].rearrange("n p w -> p n w"),
                             lambda _sl=sl, _n=n, _Wg=Wg: out2[_sl][:, 0:_n * _Wg],
                             pre=ts)
                ofree[sl] = wtok
                prep_w_done[w] = wtok
        return prep_w_done["f2"]

    def ln_stats_all(x_ap_fn, js=range(NTT)):
        trs = {}
        P.wait("sc", P.tok("ve"))
        for j in js:
            sl = slice(j, j + 1)
            tv = RED(T["st_sum"][:, sl], x_ap_fn(j), "sum", mark=True)
            ts = AVA(af(6144, 1024), x_ap_fn(j), ACT.Square,
                     T["st_ssq"][:, sl], mark=True)
            trs[j] = (tv, ts)
        deps = {}
        for j in js:
            sl = slice(j, j + 1)
            P.wait("gp", trs[j][0])
            P.wait("gp", trs[j][1])
            TS("gp", T["st_mneg"][:, sl], T["st_sum"][:, sl], -1.0 / D, None,
               ALU.mult)
            TT("gp", T["st_var"][:, sl], T["st_mneg"][:, sl],
               T["st_mneg"][:, sl], ALU.mult)
            TS("gp", T["st_ssq2"][:, sl], T["st_ssq"][:, sl], 1.0 / D, None,
               ALU.mult)
            tg = TT("gp", T["st_var"][:, sl], T["st_ssq2"][:, sl],
                    T["st_var"][:, sl], ALU.subtract, mark=True)
            P.wait("sc", tg)
            AV(T["st_r"][:, sl], T["st_var"][:, sl], ACT.Ln, bias=T["cb"][:, 1:2])
            t = AV(T["st_r"][:, sl], T["st_r"][:, sl], ACT.Exp, scale=-0.5,
                   mark=True)
            P.wait("gp", t)
            CP("gp", T["st_rv"][:, sl], T["st_r"][:, sl])
            deps[j] = TT("gp", T["st_mrneg"][:, sl], T["st_mneg"][:, sl],
                         T["st_rv"][:, sl], ALU.mult, mark=True)
        return deps

    def ln_stats(x_ap, j):
        sl = slice(j, j + 1)
        P.wait("sc", P.tok("ve"))
        tv = RED(T["st_sum"][:, sl], x_ap, "sum", mark=True)
        tr = AVA(af(6144, 1024), x_ap, ACT.Square, T["st_ssq"][:, sl],
                 mark=True)
        P.wait("gp", tv)
        P.wait("gp", tr)
        TS("gp", T["st_mneg"][:, sl], T["st_sum"][:, sl], -1.0 / D, None,
           ALU.mult)
        TT("gp", T["st_var"][:, sl], T["st_mneg"][:, sl], T["st_mneg"][:, sl],
           ALU.mult)
        TS("gp", T["st_ssq2"][:, sl], T["st_ssq"][:, sl], 1.0 / D, None,
           ALU.mult)
        tg = TT("gp", T["st_var"][:, sl], T["st_ssq2"][:, sl],
                T["st_var"][:, sl], ALU.subtract, mark=True)
        P.wait("sc", tg)
        AV(T["st_r"][:, sl], T["st_var"][:, sl], ACT.Ln, bias=T["cb"][:, 1:2])
        t = AV(T["st_r"][:, sl], T["st_r"][:, sl], ACT.Exp, scale=-0.5,
               mark=True)
        P.wait("gp", t)
        CP("gp", T["st_rv"][:, sl], T["st_r"][:, sl])
        return TT("gp", T["st_mrneg"][:, sl], T["st_mneg"][:, sl],
                  T["st_rv"][:, sl], ALU.mult, mark=True)

    def normalize_zt(x_ap, j, dep):
        P.wait("sc", dep)
        return AV(af(ZT, 1024), x_ap, ACT.Identity,
                  bias=T["st_mrneg"][:, j:j + 1], scale=T["st_rv"][:, j:j + 1],
                  mark=True)

    def tm_to_fm(src_off, j, evac, src_tensor="AF"):
        tt = None
        for half in range(2):
            bank = next_bank("pt")
            for m in range(4):
                c = half * 4 + m
                src = af(src_off + c * 128, 128)
                tt = TR(PSUM[bank][:, m * 128:(m + 1) * 128], src, mark=(m == 3))
            evac(PSUM[bank][:, 0:512], half * 4, j, tt, bank)
        return tt

    def fm_to_tm_add(src_off, c, dst_tensor="x_tm"):
        bank = next_bank("pt")
        tt = None
        for j in range(NTT):
            tt = TR(PSUM[bank][:, j * 128:(j + 1) * 128],
                    af(src_off + j * 128, 128), mark=(j == NTT - 1))
        P.wait("ve", tt)
        dst = ap3(dst_tensor, c * 128, 1024, NTT, 128)
        tv = TT("ve", dst, dst, psum3(bank, NTT, 128), ALU.add, mark=True)
        free_bank(bank, tv)

    def mm_pass(w, rhs_fn, epi, rhs_dep, rhs_dep_fn=None):
        K, O, _ = W_SPECS[w]
        Wg = wg_of(K)
        nk = K // 128
        nog = O // Wg
        for og in range(nog):
            sl = og % 2
            wb = T["wbuf0" if sl == 0 else "wbuf1"]
            tok = P.dma("wq",
                        lambda _wb=wb, _n=nk * Wg: _wb[:, 0:_n],
                        lambda _w=w, _i=og * nk, _n=nk:
                            dram["scr_" + _w][_i:_i + _n].rearrange("n p w -> p n w"),
                        pre=[wbuf_free[sl], prep_w_done.get(w)])
            P.wait("te", tok)
            if rhs_dep_fn is None:
                if isinstance(rhs_dep, list):
                    for d in rhs_dep:
                        P.wait("te", d)
                else:
                    P.wait("te", rhs_dep)
            for o in range(Wg // 128):
                oc = og * (Wg // 128) + o
                bank = next_bank("pm")
                for kc in range(nk):
                    if rhs_dep_fn is not None:
                        P.wait("te", rhs_dep_fn(kc))
                    MM(PSUM[bank][:, 0:512],
                       wb[:, kc * Wg + o * 128: kc * Wg + (o + 1) * 128],
                       rhs_fn(kc), start=(kc == 0), stop=(kc == nk - 1),
                       mark=(kc == nk - 1))
                epi(bank, oc, P.tok("te"))
            wbuf_free[sl] = P.tok("te")

    x_store_dep = [None]
    final_toks = []

    def load_x(blk):
        seq, t0 = blk // (T_FULL // TB), (blk % (T_FULL // TB)) * TB
        toks = []
        for j in range(NTT):
            toks.append(P.dma("xq",
                lambda _j=j: T["x_tm"][:, _j * 1024:(_j + 1) * 1024],
                lambda _j=j, _s=seq, _t=t0:
                    dram["x"][_s, _t + _j * 128:_t + (_j + 1) * 128, :],
                pre=x_store_dep[0]))
        return toks[-2:]

    def store_x(blk):
        seq, t0 = blk // (T_FULL // TB), (blk % (T_FULL // TB)) * TB
        dep = P.tok("ve")
        toks = []
        for j in range(NTT):
            toks.append(P.dma("oq",
                lambda _j=j, _s=seq, _t=t0:
                    dram["out"][_s, _t + _j * 128:_t + (_j + 1) * 128, :],
                lambda _j=j: T["x_tm"][:, _j * 1024:(_j + 1) * 1024],
                pre=dep))
        x_store_dep[0] = toks[-1]
        final_toks.extend(toks[-2:])

    def tap_dump(blk, phase):
        if (blk, phase) not in taps:
            return
        tap = taps[(blk, phase)]
        dep = P.tok("ve")
        toks = []
        for j in range(NTT):
            toks.append(P.dma("oq",
                lambda _j=j, _t=tap: _t[_j * 128:(_j + 1) * 128, :],
                lambda _j=j: T["x_tm"][:, _j * 1024:(_j + 1) * 1024],
                pre=dep))
        for t_ in toks[-2:]:
            P.wait("ve", t_)
        final_toks.extend(toks[-2:])

    def phase_A(blk, x_dep):
        seq_start = blk % (T_FULL // TB) == 0
        if seq_start:
            for c in range(NCD):
                MS("gp", af(A_PB + c * 513, 513)[:, 0:1], 1.0)
            MS("gp", T["cscarry"][:], 0.0, mark=True)
        kb_done = [None]

        def kb_evac(ps, c_lo, j, tt, bank):
            P.wait("ve", tt)
            dst = bass.AP(T["AB"][:, 0:1].tensor, AB_K + c_lo * 512 + j * 128,
                          [[28672, 128], [512, 4], [1, 128]])
            last = CP("ve", dst, psum3(bank, 4, 128), mark=True)
            free_bank(bank, last)
            kb_done[0] = last

        last_ktr = [None]
        for d in x_dep:
            P.wait("ve", d)
        deps = ln_stats_all(lambda j: T["x_tm"][:, j * 1024:(j + 1) * 1024])
        for j in range(NTT):
            xt = T["x_tm"][:, j * 1024:(j + 1) * 1024]
            tz = normalize_zt(xt, j, deps[j])
            P.wait("ve", tz)
            zt = af(ZT, 1024)
            ta = RED(T["st_am"][:, j:j + 1], zt, "max", absolute=True, mark=True)
            P.wait("gp", ta)
            tg = TS("gp", T["st_am2"][:, j:j + 1], T["st_am"][:, j:j + 1], 1e-5,
                    None, ALU.max, mark=True)
            P.wait("ve", tg)
            tv = RECIP(T["st_scf"][:, j:j + 1], T["st_am2"][:, j:j + 1],
                       mark=True)
            P.wait("gp", tv)
            tg2 = TS("gp", T["st_scf"][:, j:j + 1], T["st_scf"][:, j:j + 1],
                     127.0, None, ALU.mult, mark=True)
            P.wait("sc", tg2)
            if last_ktr[0] is not None:
                P.wait("sc", last_ktr[0])
            t = AV(af(A_KF, 1024), zt, ACT.Identity,
                   scale=T["st_scf"][:, j:j + 1], mark=True)
            P.wait("ve", t)
            CP("ve", T["stri"][:, 0:1024], af(A_KF, 1024))
            tv = TS("ve", af(A_KF, 1024), T["stri"][:, 0:1024], -128.0, 127.0,
                    ALU.max, ALU.min, mark=True)
            P.wait("te", tv)
            last_ktr[0] = tm_to_fm(A_KF, j, kb_evac)

        bank2 = next_bank("pt")
        tprev = None
        P.wait("gp", P.tok("gp"))
        for j in range(NTT):
            sl = j % 2
            rhs = T["rowbuf"][:, sl * 128:(sl + 1) * 128]
            if tprev is not None:
                P.wait("gp", tprev)
            tg = TS("gp", rhs, T["ident"][:], T["st_am2"][:, j:j + 1], None,
                    ALU.mult, mark=True)
            P.wait("te", tg)
            tprev = MM(PSUM[bank2][:, j * 128:(j + 1) * 128], T["ones1"][:],
                       rhs, True, True, mark=True)
        P.wait("ve", tprev)
        tv = TS("ve", T["R"][:], PSUM[bank2][:, 0:512], 1.0 / 127.0, None,
                ALU.mult, mark=True)
        free_bank(bank2, tv)

        kb_dep = kb_done[0]
        epar = [0]

        def epi_dt(bank, oc, te_tok):
            par = epar[0] % 2
            epar[0] += 1
            s0 = af(A_S[par * 2], 512)
            s1 = af(A_S[par * 2 + 1], 512)
            P.wait("ve", te_tok)
            e1 = TT("ve", s0, PSUM[bank][:, 0:512], T["R"][:], ALU.mult,
                    mark=True)
            free_bank(bank, e1)
            tv = TS("ve", s0, s0, T["wsdt"][:, 0:1], T["bdt"][:, oc:oc + 1],
                    ALU.mult, ALU.add, mark=True)
            P.wait("sc", tv)
            AV(s1, s0, ACT.Abs)
            AV(s1, s1, ACT.Exp, scale=-1.0)
            t = AV(s1, s1, ACT.Ln, bias=T["cb"][:, 2:3], mark=True)
            P.wait("ve", t)
            STT("ve", s0, s0, 0.0, s1, ALU.max, ALU.add)
            TS("ve", af(A_DT + oc * 512, 512), s0, 1e-3, 1.0, ALU.max, ALU.min,
               mark=True)

        mm_pass("wdt", lambda kc: ab(AB_K + kc * 512, 512), epi_dt, kb_dep)
        fm_dump(blk, "afterdt")

        last_tanh = {0: None, 1: None}

        def epi_xp(bank, oc, te_tok):
            par = epar[0] % 2
            epar[0] += 1
            s0 = af(A_S[par * 2], 512)
            s1 = af(A_S[par * 2 + 1], 512)
            P.wait("ve", te_tok)
            if last_tanh[par] is not None:
                P.wait("ve", last_tanh[par])
            e1 = TT("ve", s0, PSUM[bank][:, 0:512], T["R"][:], ALU.mult,
                    mark=True)
            free_bank(bank, e1)
            TS("ve", s0, s0, T["wsx"][:, 0:1], T["bx"][:, oc:oc + 1],
               ALU.mult, ALU.add)
            tv = TS("ve", s0, s0, -5.0, 5.0, ALU.max, ALU.min, mark=True)
            if oc < 8:
                TT("ve", af(A_GN + oc * 512, 512), s0,
                   af(A_DT + oc * 512, 512), ALU.mult, mark=True)
            else:
                P.wait("sc", tv)
                last_tanh[par] = AV(af(A_TC + (oc - 8) * 512, 512), s0,
                                    ACT.Tanh, mark=True)

        mm_pass("wx", lambda kc: ab(AB_K + kc * 512, 512), epi_xp, kb_dep)
        fm_dump(blk, "afterxp")

        P.wait("ve", P.tok("sc"))
        last_carry = [None]
        for c in range(NCD):
            if last_carry[0] is not None:
                P.wait("ve", last_carry[0])
            pb = T["AF"][:, A_PB + c * 513: A_PB + (c + 1) * 513]
            dtc = af(A_DT + c * 512, 512)
            tv = TS("ve", af(A_S[0], 512), dtc, T["acol"][:, c:c + 1], -10.0,
                    ALU.mult, ALU.max, mark=True)
            P.wait("sc", tv)
            if c == 0:
                fm_dump(blk, "c0_ua")
            t = AV(af(A_S[1], 512), af(A_S[0], 512), ACT.Exp, mark=True)
            P.wait("ve", t)
            if c == 0:
                fm_dump(blk, "c0_exp")
            SCAN(pb[:, 1:513], af(A_S[1], 512), af(A_S[1], 512), pb[:, 0:1],
                 ALU.mult, ALU.bypass)
            if c == 0:
                fm_dump(blk, "c0_scan")
            RECIP(af(A_S[2], 512), pb[:, 0:512])
            TT("ve", af(A_S[2], 512), af(A_GN + c * 512, 512), af(A_S[2], 512),
               ALU.mult)
            if c == 0:
                fm_dump(blk, "c0_g")
            tsc = SCAN(af(A_S[3], 512), af(A_S[2], 512), af(A_S[2], 512),
                       T["cscarry"][:, c:c + 1], ALU.add, ALU.bypass, mark=True)
            if c == 0:
                fm_dump(blk, "c0_cs")
            tv = TT("ve", af(A_S[4], 512), pb[:, 1:513], af(A_S[3], 512),
                    ALU.mult, mark=True)
            P.wait("gp", tv)
            CP("gp", T["cscarry"][:, c:c + 1], af(A_S[3] + 511, 1))
            last_carry[0] = CP("gp", pb[:, 0:1], pb[:, 512:513], mark=True)
            TT("ve", af(A_S[4], 512), af(A_S[4], 512), af(A_TC + c * 512, 512),
               ALU.mult)
            TT("ve", af(A_S[5], 512), af(A_S[4], 512), af(A_S[4], 512),
               ALU.subtract)
            TS("ve", T["strm"][:], af(A_S[5], 512), 0.0, None, ALU.is_equal)
            tv = SEL(af(A_S[5], 512), T["strm"][:], af(A_S[4], 512),
                     T["zeros"][:], mark=True)
            if c == 0:
                fm_dump(blk, "c0_sel")
            P.wait("te", tv)
            fm_to_tm_add(A_S[5], c)

    def fm_dump(blk, point="endA"):
        if blk != 0 or not fmtaps:
            return
        sel = {t: v for t, v in fmtaps.items() if v[3] == point}
        if not sel:
            return
        dep = P.tok("ve")
        toks = []
        for tag, (tap, off, cols, _pt) in sel.items():
            if isinstance(off, str) and off.startswith("ab:"):
                src_fn = (lambda _o=int(off[3:]), _c=cols: ab(_o, _c))
            elif isinstance(off, str):
                src_fn = (lambda _n=off, _c=cols: T[_n][:, 0:_c])
            else:
                src_fn = (lambda _o=off, _c=cols: af(_o, _c))
            toks.append(P.dma("oq", lambda _t=tap: _t, src_fn, pre=dep))
        for t_ in toks[-2:]:
            P.wait("ve", t_)
        final_toks.extend(toks[-2:])

    def phase_B(blk):
        xn_done = [None]

        def xn_evac(ps, c_lo, j, tt, bank):
            P.wait("ve", tt)
            dst = bass.AP(T["AB"][:, 0:1].tensor, AB_XN + c_lo * 512 + j * 128,
                          [[28672, 128], [512, 4], [1, 128]])
            last = CP("ve", dst, psum3(bank, 4, 128), mark=True)
            free_bank(bank, last)
            xn_done[0] = last

        last_tr = [None]
        deps = ln_stats_all(lambda j: T["x_tm"][:, j * 1024:(j + 1) * 1024])
        for j in range(NTT):
            xt = T["x_tm"][:, j * 1024:(j + 1) * 1024]
            if last_tr[0] is not None:
                P.wait("sc", last_tr[0])
            tz = normalize_zt(xt, j, deps[j])
            P.wait("te", tz)
            last_tr[0] = tm_to_fm(ZT, j, xn_evac)

        qp = [0]
        qdeps = {0: None, 1: None, 2: None, 3: None}
        last_g = [None]
        hq_tok_by_kc = {}

        def epi_h(bank, oc, te_tok):
            par = qp[0] % 4
            qp[0] += 1
            hln = af(B_HB[par], 512)
            sa, sb_ = af(B_QV[par][0], 512), af(B_QV[par][1], 512)
            sG = af(B_QG[par], 512)
            si = T["stri"][:, par * 512:(par + 1) * 512]
            dst = ab(AB_HQ + oc * 512, 512)
            P.wait("ve", te_tok)
            if qdeps[par] is not None:
                P.wait("sc", qdeps[par])
                P.wait("ve", qdeps[par])
            tv0 = TS("ve", hln, PSUM[bank][:, 0:512], T["b1"][:, oc:oc + 1],
                     2.0 ** -6, ALU.add, ALU.max, mark=True)
            P.wait("sc", tv0)
            AV(sa, hln, ACT.Ln)
            AV(si, sa, ACT.Identity, scale=-INV_LN2, bias=T["cb"][:, 4:5])
            AV(sa, si, ACT.Exp, scale=LN2, bias=T["cb"][:, 5:6])
            t2 = AV(sb_, si, ACT.Exp, scale=-LN2, bias=T["cb"][:, 6:7],
                    mark=True)
            P.wait("ve", t2)
            tk = STT("ve", si, PSUM[bank][:, 0:512], T["b1"][:, oc:oc + 1],
                     sa, ALU.add, ALU.mult, mark=True)
            free_bank(bank, tk)
            tq = STT("ve", sG, si, 0.0, sb_, ALU.max, ALU.mult, mark=True)
            P.wait("sc", tq)
            to = AV(sb_, sG, ACT.Relu, bias=T["cb"][:, 3:4], mark=True)
            P.wait("ve", to)
            tg = STT("ve", dst, sb_, BIG, sG, ALU.mult, ALU.add, mark=True)
            qdeps[par] = tg
            last_g[0] = tg
            hq_tok_by_kc[oc] = tg

        mm_pass("w1", lambda kc: ab(AB_XN + kc * 512, 512), epi_h, xn_done[0])

        op = [0]
        hq_toks = dict(hq_tok_by_kc)

        def epi_o(bank, oc, te_tok):
            par = op[0] % 4
            op[0] += 1
            hb = af(B_HB[par], 512)
            P.wait("sc", te_tok)
            if qdeps[par] is not None:
                P.wait("ve", qdeps[par])
                P.wait("sc", qdeps[par])
            e1 = AV(hb, PSUM[bank][:, 0:512], ACT.Identity,
                    bias=T["b2q"][:, oc:oc + 1], mark=True)
            free_bank(bank, e1)
            _, tg = quant_chain(af(B_S[par], 512), hb, 512,
                                man=10, clamp=14, maxv=65504.0, pos=False,
                                sv=B_QV[par], sg=B_QG[par], i32_off=par * 512)
            qdeps[par] = tg
            P.wait("te", tg)
            fm_to_tm_add(B_S[par], oc)

        mm_pass("w2", lambda kc: ab(AB_HQ + kc * 512, 512), epi_o, None,
                rhs_dep_fn=lambda kc: hq_toks[kc])

    def phase_C(blk):
        seq_start = blk % (T_FULL // TB) == 0
        if seq_start:
            MS("gp", T["ccarry"][:], 0.0, mark=True)
        xn_done = [None]

        def xn_evac(ps, c_lo, j, tt, bank):
            P.wait("ve", tt)
            for m in range(4):
                c = c_lo + m
                dst32 = af(C_XN + c * 512 + j * 128, 128)
                TS("ve", dst32, ps[:, m * 128:(m + 1) * 128],
                   T["ng"][:, c:c + 1], T["nb"][:, c:c + 1], ALU.mult, ALU.add,
                   mark=(m == 3))
            free_bank(bank, P.tok("ve"))
            src32 = bass.AP(T["AF"][:, 0:1].tensor, C_XN + c_lo * 512 + j * 128,
                            [[22016, 128], [512, 4], [1, 128]])
            dstb = bass.AP(T["AB"][:, 0:1].tensor, AB_XN + c_lo * 512 + j * 128,
                           [[28672, 128], [512, 4], [1, 128]])
            last = CP("ve", dstb, src32, mark=True)
            xn_done[0] = last

        last_tr = [None]
        deps = ln_stats_all(lambda j: T["x_tm"][:, j * 1024:(j + 1) * 1024])
        for j in range(NTT):
            xt = T["x_tm"][:, j * 1024:(j + 1) * 1024]
            if last_tr[0] is not None:
                P.wait("sc", last_tr[0])
            tz = normalize_zt(xt, j, deps[j])
            P.wait("te", tz)
            last_tr[0] = tm_to_fm(ZT, j, xn_evac)
        CP("gp", T["wtmp"][:, 4:8], T["st_rv"][:, 0:4])
        sdep = CP("gp", T["wtmp"][:, 0:4], T["st_mrneg"][:, 0:4], mark=True)

        lh_done = [None]

        def epi_fr(bank, oc, te_tok):
            s0 = af(C_S[0], 512)
            s1 = af(C_S[1], 512)
            P.wait("sc", te_tok)
            P.wait("sc", lh_done[0])
            th = AV(s0, PSUM[bank][:, 0:512], ACT.Tanh, scale=0.5,
                    bias=T["frbh"][:, oc:oc + 1])
            tl = AV(s1, s0, ACT.Identity, scale=0.5, bias=T["cb"][:, 4:5], mark=True)
            free_bank(bank, tl)
            P.wait("ve", tl)
            xnc = af(C_XN + oc * 512, 512)
            TT("ve", ab(AB_LOW + oc * 512, 512), xnc, s1, ALU.mult)
            TS("ve", s1, s1, -1.0, 1.0, ALU.mult, ALU.add)
            lh_done[0] = TT("ve", ab(AB_HIGH + oc * 512, 512), xnc, s1,
                            ALU.mult, mark=True)

        fm_dump(blk, "C_xn")
        mm_pass("fr", lambda kc: ab(AB_XN + kc * 512, 512), epi_fr, xn_done[0])
        fm_dump(blk, "C_fr")

        ssd_done = [None]
        ssd_toks = {}
        mam_toks = {}
        nk = NCD
        cpbuf_free = [lh_done[0]]

        def cp_wslice(kc, o):
            return ab(AB_XN + kc * 512 + o * 128, 128)

        def scan_block():
            specs = [("dp", 0), ("bp", 1), ("cp", 2)]
            for og in range(2):
                dtoks = {}
                for w, sl in specs:
                    if sl == 2:
                        dtoks[w] = P.dma("wq",
                            lambda: ab(AB_XN, nk * 512),
                            lambda _w=w, _i=og * nk: dram["scr_" + _w][_i:_i + nk]
                                .rearrange("n p w -> p n w"),
                            pre=[cpbuf_free[0], prep_w_done.get(w)])
                    else:
                        dtoks[w] = P.dma("wq",
                            lambda _wb="wbuf0" if sl == 0 else "wbuf1":
                                T[_wb][:, 0:nk * 512],
                            lambda _w=w, _i=og * nk: dram["scr_" + _w][_i:_i + nk]
                                .rearrange("n p w -> p n w"),
                            pre=[wbuf_free[sl], prep_w_done.get(w)])
                for o in range(4):
                    c = og * 4 + o
                    banks = {}
                    for w, sl in specs:
                        P.wait("te", dtoks[w])
                        bank = next_bank("pm")
                        for kc in range(nk):
                            if sl == 2:
                                lhsT = cp_wslice(kc, o)
                                rhs = ab(AB_HIGH + kc * 512, 512)
                            else:
                                wb = T["wbuf0" if sl == 0 else "wbuf1"]
                                lhsT = wb[:, kc * 512 + o * 128: kc * 512 + (o + 1) * 128]
                                rhs = ab(AB_LOW + kc * 512, 512)
                            MM(PSUM[bank][:, 0:512], lhsT, rhs,
                               start=(kc == 0), stop=(kc == nk - 1),
                               mark=(kc == nk - 1))
                        banks[w] = (bank, P.tok("te"))
                    scan_chunk(c, banks)
                for w, sl in specs:
                    if sl == 2:
                        cpbuf_free[0] = P.tok("te")
                    else:
                        wbuf_free[sl] = P.tok("te")

        carry_guard = [None]

        def scan_chunk(c, banks):
            s0, s1, s2, s3, s4 = (af(C_S[i], 512) for i in range(5))
            if carry_guard[0] is not None:
                P.wait("ve", carry_guard[0])
            xnc = af(C_XN + c * 512, 512)
            bank_a, ta = banks["dp"]
            bank_b, tb = banks["bp"]
            bank_c, tc = banks["cp"]
            P.wait("sc", tc)
            P.wait("sc", ssd_done[0])
            AV(s0, PSUM[bank_a][:, 0:512], ACT.Tanh, scale=0.5,
               bias=T["dpbh"][:, c:c + 1])
            AV(s0, s0, ACT.Identity, scale=0.5, bias=T["cb"][:, 4:5])
            AV(s1, PSUM[bank_b][:, 0:512], ACT.Tanh, bias=T["bpb"][:, c:c + 1])
            t3 = AV(s2, PSUM[bank_c][:, 0:512], ACT.Tanh,
                    bias=T["cpb"][:, c:c + 1], mark=True)
            free_bank(bank_a, t3)
            free_bank(bank_b, t3)
            free_bank(bank_c, t3)
            P.wait("ve", t3)
            TT("ve", s1, s1, xnc, ALU.mult)
            tsc = SCAN(s3, s0, s1, T["ccarry"][:, c:c + 1], ALU.mult, ALU.add,
                       mark=True)
            P.wait("gp", tsc)
            carry_guard[0] = CP("gp", T["ccarry"][:, c:c + 1],
                                af(C_S[3] + 511, 1), mark=True)
            TT("ve", s4, s2, s3, ALU.mult)
            ssd32 = af(C_SS + c * 512, 512)
            STT("ve", ssd32, xnc, T["skp"][:, c:c + 1], s4, ALU.mult, ALU.add)
            ssd_done[0] = CP("ve", ab(AB_SSD + c * 512, 512), ssd32, mark=True)
            ssd_toks[c] = ssd_done[0]

        scan_block()
        fm_dump(blk, "C_scan")

        gel_done = [None]
        gel_toks = {}

        def epi_m1(bank, oc, te_tok):
            P.wait("sc", te_tok)
            t = AV(ab(AB_GEL + oc * 512, 512), PSUM[bank][:, 0:512], ACT.Gelu,
                   bias=T["m1b"][:, oc:oc + 1], mark=True)
            free_bank(bank, t)
            gel_done[0] = t
            gel_toks[oc] = t

        mm_pass("m1", lambda kc: ab(AB_LOW + kc * 512, 512), epi_m1, lh_done[0])

        mam_done = [None]

        def epi_m2(bank, oc, te_tok):
            P.wait("sc", te_tok)
            e1 = AV(af(C_MB + oc * 512, 512), PSUM[bank][:, 0:512],
                    ACT.Identity, bias=T["m2b"][:, oc:oc + 1], mark=True)
            free_bank(bank, e1)
            mam_done[0] = AV(ab(AB_MAM + oc * 512, 512),
                             af(C_MB + oc * 512, 512), ACT.Identity, mark=True)
            mam_toks[oc] = mam_done[0]

        mm_pass("m2", lambda kc: ab(AB_GEL + kc * 512, 512), epi_m2, None,
                rhs_dep_fn=lambda kc: gel_toks[kc])
        fm_dump(blk, "C_m2")


        gdep = [None]

        def epi_gate(bank, oc, te_tok):
            s0 = af(C_S[0], 512)
            s1 = af(C_S[1], 512)
            P.wait("sc", te_tok)
            P.wait("sc", gdep[0])
            th = AV(s0, PSUM[bank][:, 0:512], ACT.Tanh, scale=0.5,
                    bias=T["gabh"][:, oc:oc + 1])
            th = AV(s0, s0, ACT.Identity, scale=0.5, bias=T["cb"][:, 4:5], mark=True)
            free_bank(bank, th)
            P.wait("ve", th)
            TT("ve", s1, af(C_SS + oc * 512, 512), af(C_MB + oc * 512, 512),
               ALU.subtract)
            TT("ve", s1, s0, s1, ALU.mult)
            tv = TT("ve", s1, s1, af(C_MB + oc * 512, 512), ALU.add, mark=True)
            gdep[0] = tv
            P.wait("te", tv)
            for j in range(NTT):
                bk = next_bank("pt")
                tt = TR(PSUM[bk][:, 0:128], af(C_S[1] + j * 128, 128), mark=True)
                P.wait("ve", tt)
                tv2 = CP("ve", af(C_Y + j * 1024 + oc * 128, 128),
                         PSUM[bk][:, 0:128], mark=True)
                free_bank(bk, tv2)
                gdep[0] = tv2

        def gate_rhs(kc):
            if kc < 8:
                return ab(AB_SSD + kc * 512, 512)
            return ab(AB_MAM + (kc - 8) * 512, 512)

        mm_pass("gate", gate_rhs, epi_gate, None,
                rhs_dep_fn=lambda kc: ssd_toks[kc] if kc < 8 else mam_toks[kc - 8])
        fm_dump(blk, "C_gate")

        P.wait("sc", sdep)
        prev_use = None
        for j in range(NTT):
            xt = T["x_tm"][:, j * 1024:(j + 1) * 1024]
            yt = af(C_Y + j * 1024, 1024)
            if prev_use is not None:
                P.wait("sc", prev_use)
            t = AV(af(ZT, 1024), xt, ACT.Identity,
                   scale=T["wtmp"][:, 4 + j:5 + j], bias=T["wtmp"][:, j:j + 1],
                   mark=True)
            P.wait("ve", t)
            TT("ve", yt, yt, af(ZT, 1024), ALU.add)
            prev_use = TT("ve", xt, xt, yt, ALU.add, mark=True)

        yln_done = [None]

        def yln_evac(ps, c_lo, j, tt, bank):
            P.wait("ve", tt)
            last = None
            for m in range(4):
                c = c_lo + m
                last = TS("ve", ab(AB_XN + c * 512 + j * 128, 128),
                          ps[:, m * 128:(m + 1) * 128],
                          T["fng"][:, c:c + 1], T["fnb"][:, c:c + 1],
                          ALU.mult, ALU.add, mark=(m == 3))
            free_bank(bank, last)
            yln_done[0] = last

        last_tr2 = [None]
        deps = ln_stats_all(lambda j: af(C_Y + j * 1024, 1024))
        for j in range(NTT):
            yt_off = C_Y + j * 1024
            if last_tr2[0] is not None:
                P.wait("sc", last_tr2[0])
            tz = normalize_zt(af(yt_off, 1024), j, deps[j])
            P.wait("te", tz)
            last_tr2[0] = tm_to_fm(ZT, j, yln_evac)

        hgel_done = [None]
        hgel_toks = {}

        def epi_f1(bank, oc, te_tok):
            P.wait("sc", te_tok)
            t = AV(ab(AB_HQ + oc * 512, 512), PSUM[bank][:, 0:512], ACT.Gelu,
                   bias=T["f1b"][:, oc:oc + 1], mark=True)
            free_bank(bank, t)
            hgel_done[0] = t
            hgel_toks[oc] = t

        fm_dump(blk, "C_y")
        mm_pass("f1", lambda kc: ab(AB_XN + kc * 512, 512), epi_f1, yln_done[0])

        def epi_f2(bank, oc, te_tok):
            P.wait("sc", te_tok)
            e1 = AV(af(C_S[5], 512), PSUM[bank][:, 0:512], ACT.Identity,
                    bias=T["f2b"][:, oc:oc + 1], mark=True)
            free_bank(bank, e1)
            P.wait("te", e1)
            fm_to_tm_add(C_S[5], oc)

        mm_pass("f2", lambda kc: ab(AB_HQ + kc * 512, 512), epi_f2, None,
                rhs_dep_fn=lambda kc: hgel_toks[kc])

    prep()
    prep2_done = [None]
    for blk in range(N_BLK):
        x_dep = load_x(blk)
        tap_dump(blk, "load")
        phase_A(blk, x_dep)
        if blk == 0:
            prep2_done[0] = prep2()
        tap_dump(blk, "A")
        if blk == 0:
            P.wait("ve", prep2_done[0])
            P.wait("sc", prep2_done[0])
        phase_B(blk)
        tap_dump(blk, "B")
        phase_C(blk)
        tap_dump(blk, "C")
        phase_B(blk)
        tap_dump(blk, "D")
        store_x(blk)

    n_oq = P.q["oq"]
    if n_oq:
        P.wait("sy", ("oq", 0, ((n_oq + 1) // 2) * 16))
        P.wait("sy", ("oq", 1, (n_oq // 2) * 16))

    with stack:
        sems = {}
        with contextlib.ExitStack() as semstack:
            for e in ENGINES:
                sems[e] = semstack.enter_context(nc.semaphore("sem_" + e))
            dsems = {}
            for c in DMA_CLASSES:
                for i in (0, 1):
                    dsems[(c, i)] = semstack.enter_context(
                        nc.semaphore(f"dsem_{c}{i}"))
            P.sem = sems
            P.dsem = dsems
            with nc.Block() as block:
                P.eng = {"sy": nc.sync, "ve": nc.vector, "sc": nc.scalar,
                         "te": nc.tensor, "gp": nc.gpsimd}

                @block.sync
                def _(eng):
                    for f in P.ops["sy"]:
                        f()

                @block.vector
                def _(eng):
                    for f in P.ops["ve"]:
                        f()

                @block.scalar
                def _(eng):
                    for f in P.ops["sc"]:
                        f()

                @block.tensor
                def _(eng):
                    for f in P.ops["te"]:
                        f()

                @block.gpsimd
                def _(eng):
                    for f in P.ops["gp"]:
                        f()
    return nc



def kernel(**inputs):
    g = host_prepare(inputs)
    x = np.asarray(inputs["L0_attn_a_x"], np.float32)
    nc = build_program()
    in_maps = []
    for core in range(N_CORES):
        m = dict(g)
        m["x"] = np.ascontiguousarray(x[core * B_LOC:(core + 1) * B_LOC])
        in_maps.append(m)
    res = run_bass_kernel_spmd(nc, in_maps, core_ids=list(range(N_CORES)))
    out = np.concatenate([np.asarray(r["out"]) for r in res.results], axis=0)
    return out.reshape(B_FULL, T_FULL, D)


# version 08
# speedup vs baseline: 1.0203x; 1.0169x over previous
import contextlib
import numpy as np
import ml_dtypes
import concourse.bass as bass
import concourse.mybir as mybir
from concourse.bass_utils import run_bass_kernel_spmd

F32 = mybir.dt.float32
BF16 = mybir.dt.bfloat16
I32 = mybir.dt.int32
U8 = mybir.dt.uint8
ALU = mybir.AluOpType
ACT = mybir.ActivationFunctionType
AX = mybir.AxisListType
bf16 = ml_dtypes.bfloat16

D = 1024
T_FULL = 2048
B_FULL = 16
N_CORES = 8
B_LOC = B_FULL // N_CORES
TB = 512
N_BLK = B_LOC * (T_FULL // TB)
NTT = TB // 128
NCD = D // 128

LN2 = float(np.log(2.0))
INV_LN2 = float(1.0 / np.log(2.0))
BIG = 3.0e38

W_SPECS = {
    "wdt":  (1024, 1024, "sign"),
    "wx":   (1024, 2048, "sign"),
    "w1":   (1024, 4096, "cast"),
    "w2":   (4096, 1024, "e4m3"),
    "fr":   (1024, 1024, "cast"),
    "dp":   (1024, 1024, "cast"),
    "bp":   (1024, 1024, "cast"),
    "cp":   (1024, 1024, "cast"),
    "m1":   (1024, 2048, "cast"),
    "m2":   (2048, 1024, "cast"),
    "gate": (2048, 1024, "cast"),
    "f1":   (1024, 4096, "cast"),
    "f2":   (4096, 1024, "cast"),
}

BIAS_TILES = [("bdt", 8), ("bx", 16), ("alog", 8), ("ng", 8), ("nb", 8),
              ("frb", 8), ("dpb", 8), ("bpb", 8), ("cpb", 8), ("skp", 8),
              ("m1b", 16), ("m2b", 8), ("gab", 8), ("fng", 8), ("fnb", 8),
              ("f1b", 32), ("f2b", 8), ("b1", 32), ("b2", 8)]


def wg_of(K):
    return 512 if K <= 2048 else 256



def _pieces(w_ko, Wg):
    K, O = w_ko.shape
    nk = K // 128
    nog = O // Wg
    out = np.empty((nog * nk, 128, Wg), np.float32)
    i = 0
    for og in range(nog):
        for kc in range(nk):
            out[i] = w_ko[kc * 128:(kc + 1) * 128, og * Wg:(og + 1) * Wg]
            i += 1
    return np.ascontiguousarray(out)


def _coltile(v, n):
    return np.ascontiguousarray(np.asarray(v, np.float32).reshape(n, 128).T)


def host_prepare(inputs):
    f = np.float32
    g = {}
    g["wdt"] = _pieces(np.asarray(inputs["L0_attn_a_W_dt"], f).T, wg_of(1024))
    g["wx"] = _pieces(np.asarray(inputs["L0_attn_a_W_x"], f).T, wg_of(1024))
    g["w1"] = _pieces(np.asarray(inputs["L0_ffn_w1"], f).T, wg_of(1024))
    g["w2"] = _pieces(np.asarray(inputs["L0_ffn_w2"], f).T, wg_of(4096))
    g["fr"] = _pieces(np.asarray(inputs["L0_attn_b_fr_w"], f), wg_of(1024))
    g["dp"] = _pieces(np.asarray(inputs["L0_attn_b_dp_w"], f), wg_of(1024))
    g["bp"] = _pieces(np.asarray(inputs["L0_attn_b_bp_w"], f), wg_of(1024))
    g["cp"] = _pieces(np.asarray(inputs["L0_attn_b_cp_w"], f), wg_of(1024))
    g["m1"] = _pieces(np.asarray(inputs["L0_attn_b_m1_w"], f), wg_of(1024))
    g["m2"] = _pieces(np.asarray(inputs["L0_attn_b_m2_w"], f), wg_of(2048))
    g["gate"] = _pieces(np.asarray(inputs["L0_attn_b_gate_w"], f), wg_of(2048))
    g["f1"] = _pieces(np.asarray(inputs["L0_attn_b_ffn1_w"], f), wg_of(1024))
    g["f2"] = _pieces(np.asarray(inputs["L0_attn_b_ffn2_w"], f), wg_of(4096))
    g["bdt"] = _coltile(inputs["L0_attn_a_b_dt"], 8)
    g["bx"] = _coltile(inputs["L0_attn_a_b_x"], 16)
    g["alog"] = _coltile(inputs["L0_attn_a_A_log"], 8)
    g["ng"] = _coltile(inputs["L0_attn_b_norm_g"], 8)
    g["nb"] = _coltile(inputs["L0_attn_b_norm_b"], 8)
    g["frb"] = _coltile(inputs["L0_attn_b_fr_b"], 8)
    g["dpb"] = _coltile(inputs["L0_attn_b_dp_b"], 8)
    g["bpb"] = _coltile(inputs["L0_attn_b_bp_b"], 8)
    g["cpb"] = _coltile(inputs["L0_attn_b_cp_b"], 8)
    g["skp"] = _coltile(inputs["L0_attn_b_skip"], 8)
    g["m1b"] = _coltile(inputs["L0_attn_b_m1_b"], 16)
    g["m2b"] = _coltile(inputs["L0_attn_b_m2_b"], 8)
    g["gab"] = _coltile(inputs["L0_attn_b_gate_b"], 8)
    g["fng"] = _coltile(inputs["L0_attn_b_ffn_g"], 8)
    g["fnb"] = _coltile(inputs["L0_attn_b_ffn_bn"], 8)
    g["f1b"] = _coltile(inputs["L0_attn_b_ffn1_b"], 32)
    g["f2b"] = _coltile(inputs["L0_attn_b_ffn2_b"], 8)
    g["b1"] = _coltile(inputs["L0_ffn_b1"], 32)
    g["b2"] = _coltile(inputs["L0_ffn_b2"], 8)
    g["ident"] = np.eye(128, dtype=np.float32)
    return g



ENGINES = ("sy", "ve", "sc", "te", "gp")
DEBUG_POINT = ["endA"]
DMA_CLASSES = ("bq", "sq", "wq", "xq", "pw", "oq")
DMA_ENGINE = {"bq": "sy", "sq": "sy", "wq": "sy", "xq": "sy",
              "pw": "gp", "oq": "gp"}


class Prog:
    def __init__(self):
        self.ops = {e: [] for e in ENGINES}
        self.n = {e: 0 for e in ENGINES}
        self.waited = {e: {} for e in ENGINES}
        self.q = {c: 0 for c in DMA_CLASSES}
        self.eng = None
        self.sem = None
        self.dsem = None

    def emit(self, e, fn):
        self.ops[e].append(fn)
        return None

    def mark(self, e, fn):
        self.n[e] += 1
        tok = self.n[e]
        def wrapped(_fn=fn, _e=e):
            _fn().then_inc(self.sem[_e], 1)
        self.ops[e].append(wrapped)
        return (e, tok)

    def wait(self, e, dep):
        if dep is None:
            return
        if dep[0] in ENGINES:
            kind, tok = dep
            assert tok <= self.n[kind], \
                f"deadlock: {e} waits {kind}@{tok} > {self.n[kind]}"
            if kind == e:
                return
            if self.waited[e].get(kind, 0) >= tok:
                return
            self.waited[e][kind] = tok
            def w(_k=kind, _t=tok, _e=e):
                self.eng[_e].wait_ge(self.sem[_k], _t)
            self.ops[e].append(w)
        else:
            cls, idx, val = dep
            key = (cls, idx)
            if self.waited[e].get(key, 0) >= val:
                return
            self.waited[e][key] = val
            def w(_key=key, _v=val, _e=e):
                self.eng[_e].wait_ge(self.dsem[_key], _v)
            self.ops[e].append(w)

    def dma(self, cls, out_fn, in_fn, pre=None):
        eng = DMA_ENGINE[cls]
        if isinstance(pre, list):
            for p in pre:
                self.wait(eng, p)
        else:
            self.wait(eng, pre)
        i = self.q[cls]
        self.q[cls] += 1
        idx = i % 2
        val = ((i // 2) + 1) * 16
        def d(_o=out_fn, _i=in_fn, _k=(cls, idx), _e=eng):
            self.eng[_e].dma_start(out=_o(), in_=_i()).then_inc(self.dsem[_k], 16)
        self.ops[eng].append(d)
        return (cls, idx, val)

    def tok(self, e):
        return (e, self.n[e])


def build_program(debug_taps=(), debug_fm=()):
    nc = bass.Bass("TRN2", target_bir_lowering=False)
    P = Prog()
    stack = contextlib.ExitStack()

    dram = {}
    dram["x"] = nc.dram_tensor("x", [B_LOC, T_FULL, D], F32, kind="ExternalInput").ap()
    dram["out"] = nc.dram_tensor("out", [B_LOC, T_FULL, D], F32,
                                 kind="ExternalOutput").ap()
    for w, (K, O, _) in W_SPECS.items():
        Wg = wg_of(K)
        npc = (K // 128) * (O // Wg)
        dram[w] = nc.dram_tensor(w, [npc, 128, Wg], F32, kind="ExternalInput").ap()
        dram["scr_" + w] = nc.dram_tensor("scr_" + w, [npc, 128, Wg], BF16,
                                          kind="Internal").ap()
    for b, n in BIAS_TILES:
        dram[b] = nc.dram_tensor(b, [128, n], F32, kind="ExternalInput").ap()
    dram["ident"] = nc.dram_tensor("ident", [128, 128], F32,
                                   kind="ExternalInput").ap()
    fmtaps = {}
    for ent in debug_fm:
        tag, off, cols = ent[0], ent[1], ent[2]
        point = ent[3] if len(ent) > 3 else "endA"
        dt_ = BF16 if (isinstance(off, str) and off.startswith("ab:")) else F32
        fmtaps[tag] = (nc.dram_tensor("fmtap_" + tag, [128, cols], dt_,
                                      kind="ExternalOutput").ap(), off, cols, point)
    taps = {}
    for blk, phase in debug_taps:
        taps[(blk, phase)] = nc.dram_tensor(f"tap_{blk}_{phase}", [TB, D], F32,
                                            kind="ExternalOutput").ap()

    T = {}
    def sbuf(name, cols, dtype=F32):
        T[name] = stack.enter_context(nc.sbuf_tensor("sb_" + name, [128, cols], dtype))

    sbuf("ident", 128)
    sbuf("ones1", 128)
    sbuf("zeros", 512)
    sbuf("R", 512)
    sbuf("rowbuf", 640)
    for b, n in BIAS_TILES:
        sbuf(b, n)
    sbuf("acol", 8)
    sbuf("frbh", 8); sbuf("dpbh", 8); sbuf("gabh", 8)
    sbuf("b2q", 8)
    sbuf("wsdt", 2); sbuf("wsx", 2)
    sbuf("wtmp", 8)
    sbuf("cb", 8)
    for s in ["st_sum", "st_ssq", "st_mneg", "st_var", "st_r", "st_rv",
              "st_mrneg", "st_am", "st_scf", "st_sum2", "st_ssq2", "st_am2"]:
        sbuf(s, 8)
    sbuf("wacc", 16)
    sbuf("ccarry", 8)
    sbuf("cscarry", 8)
    sbuf("x_tm", NTT * 1024)
    sbuf("AF", 22016)
    sbuf("AB", 28672, BF16)
    sbuf("wbuf0", 8192, BF16)
    sbuf("wbuf1", 8192, BF16)
    sbuf("stri", 2048, I32)
    sbuf("strm", 512, U8)

    PSUM = {}
    for p in ["pm0", "pm1", "pm2", "pm3", "pm4", "pm5", "pt0", "pt1"]:
        PSUM[p] = stack.enter_context(nc.psum_tensor(p, [128, 512], F32))

    def af(off, n):
        return T["AF"][:, off:off + n]

    def ab(off, n):
        return T["AB"][:, off:off + n]

    def ap3(tname, off, stride, n, w):
        h = T[tname]
        row = h.shape[1] if hasattr(h, "shape") else None
        hh = h[:, 0:1]
        rowlen = hh.ap[0][0]
        return bass.AP(hh.tensor, off, [[rowlen, 128], [stride, n], [1, w]])

    def psum3(bank, n, w):
        hh = PSUM[bank][:, 0:1]
        rowlen = hh.ap[0][0]
        return bass.AP(hh.tensor, 0, [[rowlen, 128], [w, n], [1, w]])

    ZT = 0
    WSTG = [1024, 3072]
    QV = [(5120, 7168), (9216, 11264)]
    QG = [13312, 15360]
    A_KF = 1024
    A_DT = 2048
    A_GN = 6144
    A_TC = 10240
    A_PB = 14336
    A_S = [18440 + i * 512 for i in range(6)]
    B_HB = [1024 + i * 512 for i in range(4)]
    B_QV = [(3072 + i * 1024, 3584 + i * 1024) for i in range(4)]
    B_QG = [7168 + i * 512 for i in range(4)]
    B_S = [9216 + i * 512 for i in range(4)]
    C_XN = 1024
    C_MB = 5120
    C_SS = 9216
    C_Y = 13312
    C_S = [17408 + i * 512 for i in range(6)]

    WQB = [0, 2048]
    AB_K = 0
    AB_XN = 0
    AB_LOW = 4096
    AB_HIGH = 8192
    AB_GEL = 12288
    AB_MAM = 20480
    AB_SSD = 24576
    AB_HQ = 4096

    def TT(e, out, a, b, op, mark=False):
        f = lambda: P.eng[e].tensor_tensor(out, a, b, op)
        return P.mark(e, f) if mark else P.emit(e, f)

    def TS(e, out, a, s1, s2, op0, op1=None, mark=False):
        if op1 is None:
            f = lambda: P.eng[e].tensor_scalar(out, a, s1, None, op0)
        else:
            f = lambda: P.eng[e].tensor_scalar(out, a, s1, s2, op0, op1)
        return P.mark(e, f) if mark else P.emit(e, f)

    def STT(e, out, a, s, b, op0, op1, accum=None, mark=False):
        if accum is None:
            f = lambda: P.eng[e].scalar_tensor_tensor(out, a, s, b, op0, op1)
        else:
            f = lambda: P.eng[e].scalar_tensor_tensor(out, a, s, b, op0, op1,
                                                      accum_out=accum)
        return P.mark(e, f) if mark else P.emit(e, f)

    def CP(e, out, a, mark=False):
        f = lambda: P.eng[e].tensor_copy(out, a)
        return P.mark(e, f) if mark else P.emit(e, f)

    def AV(out, a, func, bias=0.0, scale=1.0, mark=False):
        f = lambda: P.eng["sc"].activation(out=out, in_=a, func=func,
                                           bias=bias, scale=scale)
        return P.mark("sc", f) if mark else P.emit("sc", f)

    def AVA(out, a, func, accum, mark=False):
        f = lambda: P.eng["sc"].activation(out=out, in_=a, func=func,
                                           accum_out=accum)
        return P.mark("sc", f) if mark else P.emit("sc", f)

    def RED(out, a, op="sum", absolute=False, mark=False):
        if op == "max":
            f = lambda: P.eng["ve"].reduce_max(out, a, axis=AX.X,
                                               apply_absolute_value=absolute)
        else:
            f = lambda: P.eng["ve"].reduce_sum(out, a, axis=AX.X,
                                               apply_absolute_value=absolute)
        return P.mark("ve", f) if mark else P.emit("ve", f)

    def RECIP(out, a, mark=False):
        f = lambda: P.eng["ve"].reciprocal(out, a)
        return P.mark("ve", f) if mark else P.emit("ve", f)

    def SCAN(out, d0, d1, init, op0, op1, mark=False):
        f = lambda: P.eng["ve"].tensor_tensor_scan(out=out, data0=d0, data1=d1,
                                                   initial=init, op0=op0, op1=op1)
        return P.mark("ve", f) if mark else P.emit("ve", f)

    def SEL(out, mask, t_, f_, mark=False):
        f = lambda: P.eng["ve"].select(out, mask, t_, f_)
        return P.mark("ve", f) if mark else P.emit("ve", f)

    def MM(psum, lhsT, rhs, start, stop, mark=False):
        f = lambda: P.eng["te"].matmul(psum, lhsT, rhs, start=start, stop=stop)
        return P.mark("te", f) if mark else P.emit("te", f)

    def TR(psum, a, mark=False):
        f = lambda: P.eng["te"].transpose(psum, a, T["ident"][:])
        return P.mark("te", f) if mark else P.emit("te", f)

    def MS(e, out, val, mark=False):
        f = lambda: P.eng[e].memset(out, val)
        return P.mark(e, f) if mark else P.emit(e, f)

    psum_free = {}
    rot = {"pm": 0, "pt": 0}

    def next_bank(kind):
        names = (["pm0", "pm1", "pm2", "pm3", "pm4", "pm5"] if kind == "pm"
                 else ["pt0", "pt1"])
        b = names[rot[kind] % len(names)]
        rot[kind] += 1
        if b in psum_free:
            P.wait("te", psum_free[b])
        return b

    def free_bank(b, dep):
        psum_free[b] = dep

    wbuf_free = [None, None]

    def quant_chain(dst, src, n, man, clamp, maxv, pos, sv, sg, i32_off):
        sa, sb_ = af(sv[0], n), af(sv[1], n)
        sG = af(sg, n)
        si = T["stri"][:, i32_off:i32_off + n]
        if pos:
            t1 = AV(sb_, src, ACT.Ln, bias=T["cb"][:, 0:1], mark=True)
            c1 = INV_LN2
        else:
            AV(sa, src, ACT.Square)
            t1 = AV(sb_, sa, ACT.Ln, bias=T["cb"][:, 0:1], mark=True)
            c1 = 0.5 * INV_LN2
        narrow = n < 128
        if narrow:
            si2 = T["stri"][:, i32_off + n:i32_off + 2 * n]
            sa2 = af(sg + n, n)
        P.wait("ve", t1)
        if narrow:
            tn = TS("ve", si, sb_, -c1, 0.5, ALU.mult, ALU.add, mark=True)
            P.wait("gp", tn)
            tg0 = CP("gp", si2, si, mark=True)
            P.wait("ve", tg0)
            tv1 = TS("ve", sa, si2, float(clamp), float(man), ALU.min, ALU.add,
                     mark=True)
        else:
            TS("ve", si, sb_, -c1, 0.5, ALU.mult, ALU.add)
            tv1 = TS("ve", sa, si, float(clamp), float(man), ALU.min, ALU.add,
                     mark=True)
        P.wait("sc", tv1)
        AV(sb_, sa, ACT.Exp, scale=LN2)
        t2 = AV(sa, sa, ACT.Exp, scale=-LN2, mark=True)
        P.wait("ve", t2)
        if narrow:
            src_done = TT("ve", si, src, sb_, ALU.mult, mark=True)
            P.wait("gp", src_done)
            CP("gp", si2, si)
            tg1 = CP("gp", sa2, sa, mark=True)
            P.wait("ve", tg1)
            tv2 = TT("ve", sb_, si2, sa2, ALU.mult, mark=True)
            P.wait("gp", tv2)
            TT("gp", sG, sb_, sb_, ALU.mult)
            TS("gp", sG, sG, float(maxv) * float(maxv), None, ALU.is_gt)
            TS("gp", sG, sG, BIG, 1.0, ALU.mult, ALU.add)
            tg = TT("gp", dst, sb_, sG, ALU.mult, mark=True)
            return src_done, tg
        src_done = TT("ve", si, src, sb_, ALU.mult, mark=True)
        tv2 = TT("ve", sb_, si, sa, ALU.mult, mark=True)
        if pos:
            P.wait("sc", tv2)
            ts = AV(sG, sb_, ACT.Relu, bias=T["cb"][:, 3:4], mark=True)
            P.wait("ve", ts)
            tg = STT("ve", dst, sG, BIG, sb_, ALU.mult, ALU.add, mark=True)
        else:
            TS("ve", sG, sb_, float(maxv), -float(maxv), ALU.min, ALU.max)
            TT("ve", sG, sb_, sG, ALU.subtract)
            tg = STT("ve", dst, sG, BIG, sb_, ALU.mult, ALU.add, mark=True)
        return src_done, tg

    stg_free = {0: None, 1: None}
    qb_free = {0: None, 1: None}
    prep_w_done = {}
    qpar = [0]
    qprev = {0: None, 1: None}

    def prep_stream(w, batch_cb):
        K, O, _ = W_SPECS[w]
        Wg = wg_of(K)
        npc = (K // 128) * (O // Wg)
        per = 2048 // Wg
        for bi, i0 in enumerate(range(0, npc, per)):
            n = min(per, npc - i0)
            sl = bi % 2
            tok = P.dma("sq",
                        lambda _s=sl, _n=n, _Wg=Wg: af(WSTG[_s], _n * _Wg),
                        lambda _w=w, _i=i0, _n=n:
                            dram[_w][_i:_i + _n].rearrange("n p w -> p n w"),
                        pre=stg_free[sl])
            P.wait("ve", tok)
            P.wait("sc", tok)
            stg_free[sl] = batch_cb(WSTG[sl], n * Wg, i0, n, sl)

    def wscale_pass(w, wtile):
        K, O, _ = W_SPECS[w]
        nb = [0]
        def cb(off, ncols, i0, n, sl):
            tv = RED(T["wacc"][:, nb[0]:nb[0] + 1], af(off, ncols), "sum",
                     absolute=True, mark=True)
            nb[0] += 1
            return tv
        prep_stream(w, cb)
        assert nb[0] <= 8
        tv = P.tok("ve")
        P.wait("gp", tv)
        tg = CP("gp", T["st_sum2"][:, 0:nb[0]], T["wacc"][:, 0:nb[0]], mark=True)
        P.wait("ve", tg)
        t1 = RED(T["wtmp"][:, 1:2], T["st_sum2"][:, 0:nb[0]], "sum", mark=True)
        P.wait("gp", t1)
        tg = CP("gp", T["wtmp"][:, 0:1], T["wtmp"][:, 1:2], mark=True)
        P.wait("te", tg)
        bank = next_bank("pt")
        tt = TR(PSUM[bank][0:1, 0:128], T["wtmp"][:, 0:1], mark=True)
        P.wait("ve", tt)
        CP("ve", T["rowbuf"][0:1, 0:128], PSUM[bank][0:1, 0:128])
        tv = RED(T["wtmp"][0:1, 2:3], T["rowbuf"][0:1, 0:128], "sum", mark=True)
        free_bank(bank, tv)
        P.wait("gp", tv)
        tg = TS("gp", T["wtmp"][0:1, 3:4], T["wtmp"][0:1, 2:3],
                1.0 / (K * O), 1e-5, ALU.mult, ALU.add, mark=True)
        bank2 = next_bank("pt")
        P.wait("te", tg)
        tm = MM(PSUM[bank2][:, 0:1], T["ones1"][0:1, 0:128],
                T["wtmp"][0:1, 3:4], True, True, mark=True)
        P.wait("ve", tm)
        tv = CP("ve", T[wtile][:, 0:1], PSUM[bank2][:, 0:1], mark=True)
        P.wait("gp", tv)
        tv = TS("gp", T[wtile][:, 1:2], T[wtile][:, 0:1], 0.5, None, ALU.mult,
                mark=True)
        free_bank(bank2, tv)

    def prep_weight(w, K, O, q):
        wt = "wsdt" if w == "wdt" else "wsx"
        def cb(off, ncols, i0, n, sl):
            qb = ab(WQB[sl], ncols)
            if qb_free[sl] is not None:
                P.wait("ve", qb_free[sl])
                P.wait("gp", qb_free[sl])
            src = af(off, ncols)
            if q == "cast":
                done = CP("ve", qb, src, mark=True)
                wdep = done
            elif q == "sign":
                par = qpar[0] % 2
                qpar[0] += 1
                sa = af(QV[par][0], ncols)
                sb_ = af(QV[par][1], ncols)
                if qprev[par] is not None:
                    P.wait("ve", qprev[par])
                STT("ve", sa, src, -1.0, src, ALU.mult, ALU.max)
                tv0 = TS("ve", sa, sa, T[wt][:, 1:2], None, ALU.is_gt, mark=True)
                P.wait("sc", tv0)
                ts = AV(sb_, src, ACT.Sign, mark=True)
                P.wait("ve", ts)
                done = TT("ve", qb, sa, sb_, ALU.mult, mark=True)
                qprev[par] = done
                wdep = done
            else:
                par = qpar[0] % 2
                qpar[0] += 1
                if qprev[par] is not None:
                    P.wait("ve", qprev[par])
                    P.wait("sc", qprev[par])
                done, tg = quant_chain(qb, src, ncols, man=3, clamp=6,
                                       maxv=240.0, pos=False, sv=QV[par],
                                       sg=QG[par], i32_off=0 if par == 0 else 0)
                qprev[par] = tg
                wdep = tg
            wtok = P.dma("pw",
                         lambda _w=w, _i=i0, _n=n:
                             dram["scr_" + _w][_i:_i + _n].rearrange("n p w -> p n w"),
                         lambda _q=qb: _q,
                         pre=wdep)
            qb_free[sl] = wtok
            prep_w_done[w] = wtok
            return done
        prep_stream(w, cb)

    def prep():
        toks = []
        for b, _ in BIAS_TILES:
            toks.append(P.dma("bq", lambda _b=b: T[_b][:], lambda _b=b: dram[_b]))
        toks.append(P.dma("bq", lambda: T["ident"][:], lambda: dram["ident"]))
        last = toks[-1]
        prev = toks[-2]
        for e in ("ve", "sc", "te"):
            P.wait(e, last)
            P.wait(e, prev)
        MS("ve", T["ones1"][:, :], 1.0)
        MS("ve", T["zeros"][:], 0.0)
        MS("ve", T["cb"][:, 0:1], 1e-38)
        MS("ve", T["cb"][:, 1:2], 1e-5)
        MS("ve", T["cb"][:, 2:3], 1.0)
        MS("ve", T["cb"][:, 3:4], -240.0)
        MS("ve", T["cb"][:, 5:6], 3.0 * LN2)
        MS("ve", T["cb"][:, 6:7], -3.0 * LN2)
        cb_tok = MS("ve", T["cb"][:, 4:5], 0.5, mark=True)
        P.wait("sc", cb_tok)
        TS("ve", T["frbh"][:], T["frb"][:], 0.5, None, ALU.mult)
        TS("ve", T["dpbh"][:], T["dpb"][:], 0.5, None, ALU.mult)
        TS("ve", T["gabh"][:], T["gab"][:], 0.5, None, ALU.mult)
        AV(T["st_sum"][:], T["alog"][:], ACT.Abs)
        AV(T["st_ssq"][:], T["st_sum"][:], ACT.Exp, scale=-1.0)
        t = AV(T["st_var"][:], T["st_ssq"][:], ACT.Ln, bias=T["cb"][:, 2:3], mark=True)
        P.wait("gp", t)
        TS("gp", T["st_mneg"][:], T["alog"][:], 0.0, None, ALU.max)
        TT("gp", T["st_mneg"][:], T["st_mneg"][:], T["st_var"][:], ALU.add)
        tv = TS("gp", T["acol"][:], T["st_mneg"][:], -1.0, None, ALU.mult,
                mark=True)
        P.wait("sc", tv)
        P.wait("ve", tv)
        quant_chain(T["b2q"][:], T["b2"][:], 8, man=10, clamp=14, maxv=65504.0,
                    pos=False, sv=QV[0], sg=QG[0], i32_off=0)
        wscale_pass("wdt", "wsdt")
        wscale_pass("wx", "wsx")
        for w in ["wdt", "wx", "w1", "w2"]:
            K, O, q = W_SPECS[w]
            prep_weight(w, K, O, q)

    def prep2():
        stg2 = [ab(4096, 4096).bitcast(F32), ab(8192, 4096).bitcast(F32)]
        out2 = [ab(12288, 2048), ab(14336, 2048)]
        sfree = {0: None, 1: None}
        ofree = {0: None, 1: None}
        for w in ["fr", "dp", "bp", "cp", "m1", "m2", "gate", "f1", "f2"]:
            K, O, _ = W_SPECS[w]
            Wg = wg_of(K)
            npc = (K // 128) * (O // Wg)
            per = 2048 // Wg
            for bi, i0 in enumerate(range(0, npc, per)):
                n = min(per, npc - i0)
                sl = bi % 2
                tok = P.dma("sq",
                            lambda _s=sl, _n=n, _Wg=Wg: stg2[_s][:, 0:_n * _Wg],
                            lambda _w=w, _i=i0, _n=n:
                                dram[_w][_i:_i + _n].rearrange("n p w -> p n w"),
                            pre=sfree[sl])
                P.wait("sc", tok)
                if ofree[sl] is not None:
                    P.wait("sc", ofree[sl])
                ts = AV(out2[sl][:, 0:n * Wg], stg2[sl][:, 0:n * Wg],
                        ACT.Identity, mark=True)
                sfree[sl] = ts
                wtok = P.dma("pw",
                             lambda _w=w, _i=i0, _n=n:
                                 dram["scr_" + _w][_i:_i + _n].rearrange("n p w -> p n w"),
                             lambda _sl=sl, _n=n, _Wg=Wg: out2[_sl][:, 0:_n * _Wg],
                             pre=ts)
                ofree[sl] = wtok
                prep_w_done[w] = wtok
        return prep_w_done["f2"]

    def ln_stats_all(x_ap_fn, js=range(NTT)):
        trs = {}
        for j in js:
            sl = slice(j, j + 1)
            RED(T["st_sum"][:, sl], x_ap_fn(j), "sum")
            trs[j] = STT("ve", af(6144, 1024), x_ap_fn(j), 1.0, x_ap_fn(j),
                         ALU.bypass, ALU.mult, accum=T["st_ssq"][:, sl],
                         mark=True)
        deps = {}
        for j in js:
            sl = slice(j, j + 1)
            P.wait("gp", trs[j])
            TS("gp", T["st_mneg"][:, sl], T["st_sum"][:, sl], -1.0 / D, None,
               ALU.mult)
            TT("gp", T["st_var"][:, sl], T["st_mneg"][:, sl],
               T["st_mneg"][:, sl], ALU.mult)
            TS("gp", T["st_ssq2"][:, sl], T["st_ssq"][:, sl], 1.0 / D, None,
               ALU.mult)
            tg = TT("gp", T["st_var"][:, sl], T["st_ssq2"][:, sl],
                    T["st_var"][:, sl], ALU.subtract, mark=True)
            P.wait("sc", tg)
            AV(T["st_r"][:, sl], T["st_var"][:, sl], ACT.Ln, bias=T["cb"][:, 1:2])
            t = AV(T["st_r"][:, sl], T["st_r"][:, sl], ACT.Exp, scale=-0.5,
                   mark=True)
            P.wait("gp", t)
            CP("gp", T["st_rv"][:, sl], T["st_r"][:, sl])
            deps[j] = TT("gp", T["st_mrneg"][:, sl], T["st_mneg"][:, sl],
                         T["st_rv"][:, sl], ALU.mult, mark=True)
        return deps

    def ln_stats(x_ap, j):
        sl = slice(j, j + 1)
        RED(T["st_sum"][:, sl], x_ap, "sum")
        tr = STT("ve", af(6144, 1024), x_ap, 1.0, x_ap, ALU.bypass, ALU.mult,
                 accum=T["st_ssq"][:, sl], mark=True)
        P.wait("gp", tr)
        TS("gp", T["st_mneg"][:, sl], T["st_sum"][:, sl], -1.0 / D, None,
           ALU.mult)
        TT("gp", T["st_var"][:, sl], T["st_mneg"][:, sl], T["st_mneg"][:, sl],
           ALU.mult)
        TS("gp", T["st_ssq2"][:, sl], T["st_ssq"][:, sl], 1.0 / D, None,
           ALU.mult)
        tg = TT("gp", T["st_var"][:, sl], T["st_ssq2"][:, sl],
                T["st_var"][:, sl], ALU.subtract, mark=True)
        P.wait("sc", tg)
        AV(T["st_r"][:, sl], T["st_var"][:, sl], ACT.Ln, bias=T["cb"][:, 1:2])
        t = AV(T["st_r"][:, sl], T["st_r"][:, sl], ACT.Exp, scale=-0.5,
               mark=True)
        P.wait("gp", t)
        CP("gp", T["st_rv"][:, sl], T["st_r"][:, sl])
        return TT("gp", T["st_mrneg"][:, sl], T["st_mneg"][:, sl],
                  T["st_rv"][:, sl], ALU.mult, mark=True)

    def normalize_zt(x_ap, j, dep):
        P.wait("sc", dep)
        return AV(af(ZT, 1024), x_ap, ACT.Identity,
                  bias=T["st_mrneg"][:, j:j + 1], scale=T["st_rv"][:, j:j + 1],
                  mark=True)

    def tm_to_fm(src_off, j, evac, src_tensor="AF"):
        tt = None
        for half in range(2):
            bank = next_bank("pt")
            for m in range(4):
                c = half * 4 + m
                src = af(src_off + c * 128, 128)
                tt = TR(PSUM[bank][:, m * 128:(m + 1) * 128], src, mark=(m == 3))
            evac(PSUM[bank][:, 0:512], half * 4, j, tt, bank)
        return tt

    def fm_to_tm_add(src_off, c, dst_tensor="x_tm"):
        bank = next_bank("pt")
        tt = None
        for j in range(NTT):
            tt = TR(PSUM[bank][:, j * 128:(j + 1) * 128],
                    af(src_off + j * 128, 128), mark=(j == NTT - 1))
        P.wait("ve", tt)
        dst = ap3(dst_tensor, c * 128, 1024, NTT, 128)
        tv = TT("ve", dst, dst, psum3(bank, NTT, 128), ALU.add, mark=True)
        free_bank(bank, tv)

    def mm_pass(w, rhs_fn, epi, rhs_dep, rhs_dep_fn=None):
        K, O, _ = W_SPECS[w]
        Wg = wg_of(K)
        nk = K // 128
        nog = O // Wg
        for og in range(nog):
            sl = og % 2
            wb = T["wbuf0" if sl == 0 else "wbuf1"]
            tok = P.dma("wq",
                        lambda _wb=wb, _n=nk * Wg: _wb[:, 0:_n],
                        lambda _w=w, _i=og * nk, _n=nk:
                            dram["scr_" + _w][_i:_i + _n].rearrange("n p w -> p n w"),
                        pre=[wbuf_free[sl], prep_w_done.get(w)])
            P.wait("te", tok)
            if rhs_dep_fn is None:
                if isinstance(rhs_dep, list):
                    for d in rhs_dep:
                        P.wait("te", d)
                else:
                    P.wait("te", rhs_dep)
            for o in range(Wg // 128):
                oc = og * (Wg // 128) + o
                bank = next_bank("pm")
                for kc in range(nk):
                    if rhs_dep_fn is not None:
                        P.wait("te", rhs_dep_fn(kc))
                    MM(PSUM[bank][:, 0:512],
                       wb[:, kc * Wg + o * 128: kc * Wg + (o + 1) * 128],
                       rhs_fn(kc), start=(kc == 0), stop=(kc == nk - 1),
                       mark=(kc == nk - 1))
                epi(bank, oc, P.tok("te"))
            wbuf_free[sl] = P.tok("te")

    x_store_dep = [None]
    final_toks = []

    def load_x(blk):
        seq, t0 = blk // (T_FULL // TB), (blk % (T_FULL // TB)) * TB
        toks = []
        for j in range(NTT):
            toks.append(P.dma("xq",
                lambda _j=j: T["x_tm"][:, _j * 1024:(_j + 1) * 1024],
                lambda _j=j, _s=seq, _t=t0:
                    dram["x"][_s, _t + _j * 128:_t + (_j + 1) * 128, :],
                pre=x_store_dep[0]))
        return toks[-2:]

    def store_x(blk):
        seq, t0 = blk // (T_FULL // TB), (blk % (T_FULL // TB)) * TB
        dep = P.tok("ve")
        toks = []
        for j in range(NTT):
            toks.append(P.dma("oq",
                lambda _j=j, _s=seq, _t=t0:
                    dram["out"][_s, _t + _j * 128:_t + (_j + 1) * 128, :],
                lambda _j=j: T["x_tm"][:, _j * 1024:(_j + 1) * 1024],
                pre=dep))
        x_store_dep[0] = toks[-1]
        final_toks.extend(toks[-2:])

    def tap_dump(blk, phase):
        if (blk, phase) not in taps:
            return
        tap = taps[(blk, phase)]
        dep = P.tok("ve")
        toks = []
        for j in range(NTT):
            toks.append(P.dma("oq",
                lambda _j=j, _t=tap: _t[_j * 128:(_j + 1) * 128, :],
                lambda _j=j: T["x_tm"][:, _j * 1024:(_j + 1) * 1024],
                pre=dep))
        for t_ in toks[-2:]:
            P.wait("ve", t_)
        final_toks.extend(toks[-2:])

    def phase_A(blk, x_dep):
        seq_start = blk % (T_FULL // TB) == 0
        if seq_start:
            for c in range(NCD):
                MS("gp", af(A_PB + c * 513, 513)[:, 0:1], 1.0)
            MS("gp", T["cscarry"][:], 0.0, mark=True)
        kb_done = [None]

        def kb_evac(ps, c_lo, j, tt, bank):
            P.wait("ve", tt)
            dst = bass.AP(T["AB"][:, 0:1].tensor, AB_K + c_lo * 512 + j * 128,
                          [[28672, 128], [512, 4], [1, 128]])
            last = CP("ve", dst, psum3(bank, 4, 128), mark=True)
            free_bank(bank, last)
            kb_done[0] = last

        last_ktr = [None]
        for d in x_dep:
            P.wait("ve", d)
        deps = ln_stats_all(lambda j: T["x_tm"][:, j * 1024:(j + 1) * 1024])
        for j in range(NTT):
            xt = T["x_tm"][:, j * 1024:(j + 1) * 1024]
            tz = normalize_zt(xt, j, deps[j])
            P.wait("ve", tz)
            zt = af(ZT, 1024)
            ta = RED(T["st_am"][:, j:j + 1], zt, "max", absolute=True, mark=True)
            P.wait("gp", ta)
            tg = TS("gp", T["st_am2"][:, j:j + 1], T["st_am"][:, j:j + 1], 1e-5,
                    None, ALU.max, mark=True)
            P.wait("ve", tg)
            tv = RECIP(T["st_scf"][:, j:j + 1], T["st_am2"][:, j:j + 1],
                       mark=True)
            P.wait("gp", tv)
            tg2 = TS("gp", T["st_scf"][:, j:j + 1], T["st_scf"][:, j:j + 1],
                     127.0, None, ALU.mult, mark=True)
            P.wait("sc", tg2)
            if last_ktr[0] is not None:
                P.wait("sc", last_ktr[0])
            t = AV(af(A_KF, 1024), zt, ACT.Identity,
                   scale=T["st_scf"][:, j:j + 1], mark=True)
            P.wait("ve", t)
            CP("ve", T["stri"][:, 0:1024], af(A_KF, 1024))
            tv = TS("ve", af(A_KF, 1024), T["stri"][:, 0:1024], -128.0, 127.0,
                    ALU.max, ALU.min, mark=True)
            P.wait("te", tv)
            last_ktr[0] = tm_to_fm(A_KF, j, kb_evac)

        bank2 = next_bank("pt")
        tprev = None
        P.wait("gp", P.tok("gp"))
        for j in range(NTT):
            sl = j % 2
            rhs = T["rowbuf"][:, sl * 128:(sl + 1) * 128]
            if tprev is not None:
                P.wait("gp", tprev)
            tg = TS("gp", rhs, T["ident"][:], T["st_am2"][:, j:j + 1], None,
                    ALU.mult, mark=True)
            P.wait("te", tg)
            tprev = MM(PSUM[bank2][:, j * 128:(j + 1) * 128], T["ones1"][:],
                       rhs, True, True, mark=True)
        P.wait("ve", tprev)
        tv = TS("ve", T["R"][:], PSUM[bank2][:, 0:512], 1.0 / 127.0, None,
                ALU.mult, mark=True)
        free_bank(bank2, tv)

        kb_dep = kb_done[0]
        epar = [0]

        def epi_dt(bank, oc, te_tok):
            par = epar[0] % 2
            epar[0] += 1
            s0 = af(A_S[par * 2], 512)
            s1 = af(A_S[par * 2 + 1], 512)
            P.wait("ve", te_tok)
            e1 = TT("ve", s0, PSUM[bank][:, 0:512], T["R"][:], ALU.mult,
                    mark=True)
            free_bank(bank, e1)
            tv = TS("ve", s0, s0, T["wsdt"][:, 0:1], T["bdt"][:, oc:oc + 1],
                    ALU.mult, ALU.add, mark=True)
            P.wait("sc", tv)
            AV(s1, s0, ACT.Abs)
            AV(s1, s1, ACT.Exp, scale=-1.0)
            t = AV(s1, s1, ACT.Ln, bias=T["cb"][:, 2:3], mark=True)
            P.wait("ve", t)
            STT("ve", s0, s0, 0.0, s1, ALU.max, ALU.add)
            TS("ve", af(A_DT + oc * 512, 512), s0, 1e-3, 1.0, ALU.max, ALU.min,
               mark=True)

        mm_pass("wdt", lambda kc: ab(AB_K + kc * 512, 512), epi_dt, kb_dep)
        fm_dump(blk, "afterdt")

        last_tanh = {0: None, 1: None}

        def epi_xp(bank, oc, te_tok):
            par = epar[0] % 2
            epar[0] += 1
            s0 = af(A_S[par * 2], 512)
            s1 = af(A_S[par * 2 + 1], 512)
            P.wait("ve", te_tok)
            if last_tanh[par] is not None:
                P.wait("ve", last_tanh[par])
            e1 = TT("ve", s0, PSUM[bank][:, 0:512], T["R"][:], ALU.mult,
                    mark=True)
            free_bank(bank, e1)
            TS("ve", s0, s0, T["wsx"][:, 0:1], T["bx"][:, oc:oc + 1],
               ALU.mult, ALU.add)
            tv = TS("ve", s0, s0, -5.0, 5.0, ALU.max, ALU.min, mark=True)
            if oc < 8:
                TT("ve", af(A_GN + oc * 512, 512), s0,
                   af(A_DT + oc * 512, 512), ALU.mult, mark=True)
            else:
                P.wait("sc", tv)
                last_tanh[par] = AV(af(A_TC + (oc - 8) * 512, 512), s0,
                                    ACT.Tanh, mark=True)

        mm_pass("wx", lambda kc: ab(AB_K + kc * 512, 512), epi_xp, kb_dep)
        fm_dump(blk, "afterxp")

        P.wait("ve", P.tok("sc"))
        last_carry = [None]
        for c in range(NCD):
            if last_carry[0] is not None:
                P.wait("ve", last_carry[0])
            pb = T["AF"][:, A_PB + c * 513: A_PB + (c + 1) * 513]
            dtc = af(A_DT + c * 512, 512)
            tv = TS("ve", af(A_S[0], 512), dtc, T["acol"][:, c:c + 1], -10.0,
                    ALU.mult, ALU.max, mark=True)
            P.wait("sc", tv)
            if c == 0:
                fm_dump(blk, "c0_ua")
            t = AV(af(A_S[1], 512), af(A_S[0], 512), ACT.Exp, mark=True)
            P.wait("ve", t)
            if c == 0:
                fm_dump(blk, "c0_exp")
            SCAN(pb[:, 1:513], af(A_S[1], 512), af(A_S[1], 512), pb[:, 0:1],
                 ALU.mult, ALU.bypass)
            if c == 0:
                fm_dump(blk, "c0_scan")
            RECIP(af(A_S[2], 512), pb[:, 0:512])
            TT("ve", af(A_S[2], 512), af(A_GN + c * 512, 512), af(A_S[2], 512),
               ALU.mult)
            if c == 0:
                fm_dump(blk, "c0_g")
            tsc = SCAN(af(A_S[3], 512), af(A_S[2], 512), af(A_S[2], 512),
                       T["cscarry"][:, c:c + 1], ALU.add, ALU.bypass, mark=True)
            if c == 0:
                fm_dump(blk, "c0_cs")
            tv = TT("ve", af(A_S[4], 512), pb[:, 1:513], af(A_S[3], 512),
                    ALU.mult, mark=True)
            P.wait("gp", tv)
            CP("gp", T["cscarry"][:, c:c + 1], af(A_S[3] + 511, 1))
            last_carry[0] = CP("gp", pb[:, 0:1], pb[:, 512:513], mark=True)
            TT("ve", af(A_S[4], 512), af(A_S[4], 512), af(A_TC + c * 512, 512),
               ALU.mult)
            TT("ve", af(A_S[5], 512), af(A_S[4], 512), af(A_S[4], 512),
               ALU.subtract)
            TS("ve", T["strm"][:], af(A_S[5], 512), 0.0, None, ALU.is_equal)
            tv = SEL(af(A_S[5], 512), T["strm"][:], af(A_S[4], 512),
                     T["zeros"][:], mark=True)
            if c == 0:
                fm_dump(blk, "c0_sel")
            P.wait("te", tv)
            fm_to_tm_add(A_S[5], c)

    def fm_dump(blk, point="endA"):
        if blk != 0 or not fmtaps:
            return
        sel = {t: v for t, v in fmtaps.items() if v[3] == point}
        if not sel:
            return
        dep = P.tok("ve")
        toks = []
        for tag, (tap, off, cols, _pt) in sel.items():
            if isinstance(off, str) and off.startswith("ab:"):
                src_fn = (lambda _o=int(off[3:]), _c=cols: ab(_o, _c))
            elif isinstance(off, str):
                src_fn = (lambda _n=off, _c=cols: T[_n][:, 0:_c])
            else:
                src_fn = (lambda _o=off, _c=cols: af(_o, _c))
            toks.append(P.dma("oq", lambda _t=tap: _t, src_fn, pre=dep))
        for t_ in toks[-2:]:
            P.wait("ve", t_)
        final_toks.extend(toks[-2:])

    def phase_B(blk):
        xn_done = [None]

        def xn_evac(ps, c_lo, j, tt, bank):
            P.wait("ve", tt)
            dst = bass.AP(T["AB"][:, 0:1].tensor, AB_XN + c_lo * 512 + j * 128,
                          [[28672, 128], [512, 4], [1, 128]])
            last = CP("ve", dst, psum3(bank, 4, 128), mark=True)
            free_bank(bank, last)
            xn_done[0] = last

        last_tr = [None]
        deps = ln_stats_all(lambda j: T["x_tm"][:, j * 1024:(j + 1) * 1024])
        for j in range(NTT):
            xt = T["x_tm"][:, j * 1024:(j + 1) * 1024]
            if last_tr[0] is not None:
                P.wait("sc", last_tr[0])
            tz = normalize_zt(xt, j, deps[j])
            P.wait("te", tz)
            last_tr[0] = tm_to_fm(ZT, j, xn_evac)

        qp = [0]
        qdeps = {0: None, 1: None, 2: None, 3: None}
        last_g = [None]
        hq_tok_by_kc = {}

        def epi_h(bank, oc, te_tok):
            par = qp[0] % 4
            qp[0] += 1
            hln = af(B_HB[par], 512)
            sa, sb_ = af(B_QV[par][0], 512), af(B_QV[par][1], 512)
            sG = af(B_QG[par], 512)
            si = T["stri"][:, par * 512:(par + 1) * 512]
            dst = ab(AB_HQ + oc * 512, 512)
            P.wait("ve", te_tok)
            if qdeps[par] is not None:
                P.wait("sc", qdeps[par])
                P.wait("ve", qdeps[par])
            tv0 = TS("ve", hln, PSUM[bank][:, 0:512], T["b1"][:, oc:oc + 1],
                     2.0 ** -6, ALU.add, ALU.max, mark=True)
            P.wait("sc", tv0)
            AV(sa, hln, ACT.Ln)
            AV(si, sa, ACT.Identity, scale=-INV_LN2, bias=T["cb"][:, 4:5])
            AV(sa, si, ACT.Exp, scale=LN2, bias=T["cb"][:, 5:6])
            t2 = AV(sb_, si, ACT.Exp, scale=-LN2, bias=T["cb"][:, 6:7],
                    mark=True)
            P.wait("ve", t2)
            tk = STT("ve", si, PSUM[bank][:, 0:512], T["b1"][:, oc:oc + 1],
                     sa, ALU.add, ALU.mult, mark=True)
            free_bank(bank, tk)
            tq = STT("ve", sG, si, 0.0, sb_, ALU.max, ALU.mult, mark=True)
            P.wait("sc", tq)
            to = AV(sb_, sG, ACT.Relu, bias=T["cb"][:, 3:4], mark=True)
            P.wait("ve", to)
            tg = STT("ve", dst, sb_, BIG, sG, ALU.mult, ALU.add, mark=True)
            qdeps[par] = tg
            last_g[0] = tg
            hq_tok_by_kc[oc] = tg

        mm_pass("w1", lambda kc: ab(AB_XN + kc * 512, 512), epi_h, xn_done[0])

        op = [0]
        hq_toks = dict(hq_tok_by_kc)

        def epi_o(bank, oc, te_tok):
            par = op[0] % 4
            op[0] += 1
            hb = af(B_HB[par], 512)
            P.wait("sc", te_tok)
            if qdeps[par] is not None:
                P.wait("ve", qdeps[par])
                P.wait("sc", qdeps[par])
            e1 = AV(hb, PSUM[bank][:, 0:512], ACT.Identity,
                    bias=T["b2q"][:, oc:oc + 1], mark=True)
            free_bank(bank, e1)
            _, tg = quant_chain(af(B_S[par], 512), hb, 512,
                                man=10, clamp=14, maxv=65504.0, pos=False,
                                sv=B_QV[par], sg=B_QG[par], i32_off=par * 512)
            qdeps[par] = tg
            P.wait("te", tg)
            fm_to_tm_add(B_S[par], oc)

        mm_pass("w2", lambda kc: ab(AB_HQ + kc * 512, 512), epi_o, None,
                rhs_dep_fn=lambda kc: hq_toks[kc])

    def phase_C(blk):
        seq_start = blk % (T_FULL // TB) == 0
        if seq_start:
            MS("gp", T["ccarry"][:], 0.0, mark=True)
        xn_done = [None]

        def xn_evac(ps, c_lo, j, tt, bank):
            P.wait("ve", tt)
            for m in range(4):
                c = c_lo + m
                dst32 = af(C_XN + c * 512 + j * 128, 128)
                TS("ve", dst32, ps[:, m * 128:(m + 1) * 128],
                   T["ng"][:, c:c + 1], T["nb"][:, c:c + 1], ALU.mult, ALU.add,
                   mark=(m == 3))
            free_bank(bank, P.tok("ve"))
            src32 = bass.AP(T["AF"][:, 0:1].tensor, C_XN + c_lo * 512 + j * 128,
                            [[22016, 128], [512, 4], [1, 128]])
            dstb = bass.AP(T["AB"][:, 0:1].tensor, AB_XN + c_lo * 512 + j * 128,
                           [[28672, 128], [512, 4], [1, 128]])
            last = CP("ve", dstb, src32, mark=True)
            xn_done[0] = last

        last_tr = [None]
        deps = ln_stats_all(lambda j: T["x_tm"][:, j * 1024:(j + 1) * 1024])
        for j in range(NTT):
            xt = T["x_tm"][:, j * 1024:(j + 1) * 1024]
            if last_tr[0] is not None:
                P.wait("sc", last_tr[0])
            tz = normalize_zt(xt, j, deps[j])
            P.wait("te", tz)
            last_tr[0] = tm_to_fm(ZT, j, xn_evac)
        CP("gp", T["wtmp"][:, 4:8], T["st_rv"][:, 0:4])
        sdep = CP("gp", T["wtmp"][:, 0:4], T["st_mrneg"][:, 0:4], mark=True)

        lh_done = [None]

        def epi_fr(bank, oc, te_tok):
            s0 = af(C_S[0], 512)
            s1 = af(C_S[1], 512)
            P.wait("sc", te_tok)
            P.wait("sc", lh_done[0])
            th = AV(s0, PSUM[bank][:, 0:512], ACT.Tanh, scale=0.5,
                    bias=T["frbh"][:, oc:oc + 1])
            tl = AV(s1, s0, ACT.Identity, scale=0.5, bias=T["cb"][:, 4:5], mark=True)
            free_bank(bank, tl)
            P.wait("ve", tl)
            xnc = af(C_XN + oc * 512, 512)
            TT("ve", ab(AB_LOW + oc * 512, 512), xnc, s1, ALU.mult)
            TS("ve", s1, s1, -1.0, 1.0, ALU.mult, ALU.add)
            lh_done[0] = TT("ve", ab(AB_HIGH + oc * 512, 512), xnc, s1,
                            ALU.mult, mark=True)

        fm_dump(blk, "C_xn")
        mm_pass("fr", lambda kc: ab(AB_XN + kc * 512, 512), epi_fr, xn_done[0])
        fm_dump(blk, "C_fr")

        ssd_done = [None]
        ssd_toks = {}
        mam_toks = {}
        nk = NCD
        cpbuf_free = [lh_done[0]]

        def cp_wslice(kc, o):
            return ab(AB_XN + kc * 512 + o * 128, 128)

        def scan_block():
            specs = [("dp", 0), ("bp", 1), ("cp", 2)]
            for og in range(2):
                dtoks = {}
                for w, sl in specs:
                    if sl == 2:
                        dtoks[w] = P.dma("wq",
                            lambda: ab(AB_XN, nk * 512),
                            lambda _w=w, _i=og * nk: dram["scr_" + _w][_i:_i + nk]
                                .rearrange("n p w -> p n w"),
                            pre=[cpbuf_free[0], prep_w_done.get(w)])
                    else:
                        dtoks[w] = P.dma("wq",
                            lambda _wb="wbuf0" if sl == 0 else "wbuf1":
                                T[_wb][:, 0:nk * 512],
                            lambda _w=w, _i=og * nk: dram["scr_" + _w][_i:_i + nk]
                                .rearrange("n p w -> p n w"),
                            pre=[wbuf_free[sl], prep_w_done.get(w)])
                for o in range(4):
                    c = og * 4 + o
                    banks = {}
                    for w, sl in specs:
                        P.wait("te", dtoks[w])
                        bank = next_bank("pm")
                        for kc in range(nk):
                            if sl == 2:
                                lhsT = cp_wslice(kc, o)
                                rhs = ab(AB_HIGH + kc * 512, 512)
                            else:
                                wb = T["wbuf0" if sl == 0 else "wbuf1"]
                                lhsT = wb[:, kc * 512 + o * 128: kc * 512 + (o + 1) * 128]
                                rhs = ab(AB_LOW + kc * 512, 512)
                            MM(PSUM[bank][:, 0:512], lhsT, rhs,
                               start=(kc == 0), stop=(kc == nk - 1),
                               mark=(kc == nk - 1))
                        banks[w] = (bank, P.tok("te"))
                    scan_chunk(c, banks)
                for w, sl in specs:
                    if sl == 2:
                        cpbuf_free[0] = P.tok("te")
                    else:
                        wbuf_free[sl] = P.tok("te")

        carry_guard = [None]

        def scan_chunk(c, banks):
            s0, s1, s2, s3, s4 = (af(C_S[i], 512) for i in range(5))
            if carry_guard[0] is not None:
                P.wait("ve", carry_guard[0])
            xnc = af(C_XN + c * 512, 512)
            bank_a, ta = banks["dp"]
            bank_b, tb = banks["bp"]
            bank_c, tc = banks["cp"]
            P.wait("sc", tc)
            P.wait("sc", ssd_done[0])
            AV(s0, PSUM[bank_a][:, 0:512], ACT.Tanh, scale=0.5,
               bias=T["dpbh"][:, c:c + 1])
            AV(s0, s0, ACT.Identity, scale=0.5, bias=T["cb"][:, 4:5])
            AV(s1, PSUM[bank_b][:, 0:512], ACT.Tanh, bias=T["bpb"][:, c:c + 1])
            t3 = AV(s2, PSUM[bank_c][:, 0:512], ACT.Tanh,
                    bias=T["cpb"][:, c:c + 1], mark=True)
            free_bank(bank_a, t3)
            free_bank(bank_b, t3)
            free_bank(bank_c, t3)
            P.wait("ve", t3)
            TT("ve", s1, s1, xnc, ALU.mult)
            tsc = SCAN(s3, s0, s1, T["ccarry"][:, c:c + 1], ALU.mult, ALU.add,
                       mark=True)
            P.wait("gp", tsc)
            carry_guard[0] = CP("gp", T["ccarry"][:, c:c + 1],
                                af(C_S[3] + 511, 1), mark=True)
            TT("ve", s4, s2, s3, ALU.mult)
            ssd32 = af(C_SS + c * 512, 512)
            STT("ve", ssd32, xnc, T["skp"][:, c:c + 1], s4, ALU.mult, ALU.add)
            ssd_done[0] = CP("ve", ab(AB_SSD + c * 512, 512), ssd32, mark=True)
            ssd_toks[c] = ssd_done[0]

        scan_block()
        fm_dump(blk, "C_scan")

        gel_done = [None]
        gel_toks = {}

        def epi_m1(bank, oc, te_tok):
            P.wait("sc", te_tok)
            t = AV(ab(AB_GEL + oc * 512, 512), PSUM[bank][:, 0:512], ACT.Gelu,
                   bias=T["m1b"][:, oc:oc + 1], mark=True)
            free_bank(bank, t)
            gel_done[0] = t
            gel_toks[oc] = t

        mm_pass("m1", lambda kc: ab(AB_LOW + kc * 512, 512), epi_m1, lh_done[0])

        mam_done = [None]

        def epi_m2(bank, oc, te_tok):
            P.wait("sc", te_tok)
            e1 = AV(af(C_MB + oc * 512, 512), PSUM[bank][:, 0:512],
                    ACT.Identity, bias=T["m2b"][:, oc:oc + 1], mark=True)
            free_bank(bank, e1)
            mam_done[0] = AV(ab(AB_MAM + oc * 512, 512),
                             af(C_MB + oc * 512, 512), ACT.Identity, mark=True)
            mam_toks[oc] = mam_done[0]

        mm_pass("m2", lambda kc: ab(AB_GEL + kc * 512, 512), epi_m2, None,
                rhs_dep_fn=lambda kc: gel_toks[kc])
        fm_dump(blk, "C_m2")


        gdep = [None]

        def epi_gate(bank, oc, te_tok):
            s0 = af(C_S[0], 512)
            s1 = af(C_S[1], 512)
            P.wait("sc", te_tok)
            P.wait("sc", gdep[0])
            th = AV(s0, PSUM[bank][:, 0:512], ACT.Tanh, scale=0.5,
                    bias=T["gabh"][:, oc:oc + 1])
            th = AV(s0, s0, ACT.Identity, scale=0.5, bias=T["cb"][:, 4:5], mark=True)
            free_bank(bank, th)
            P.wait("ve", th)
            TT("ve", s1, af(C_SS + oc * 512, 512), af(C_MB + oc * 512, 512),
               ALU.subtract)
            TT("ve", s1, s0, s1, ALU.mult)
            tv = TT("ve", s1, s1, af(C_MB + oc * 512, 512), ALU.add, mark=True)
            gdep[0] = tv
            P.wait("te", tv)
            for j in range(NTT):
                bk = next_bank("pt")
                tt = TR(PSUM[bk][:, 0:128], af(C_S[1] + j * 128, 128), mark=True)
                P.wait("ve", tt)
                tv2 = CP("ve", af(C_Y + j * 1024 + oc * 128, 128),
                         PSUM[bk][:, 0:128], mark=True)
                free_bank(bk, tv2)
                gdep[0] = tv2

        def gate_rhs(kc):
            if kc < 8:
                return ab(AB_SSD + kc * 512, 512)
            return ab(AB_MAM + (kc - 8) * 512, 512)

        mm_pass("gate", gate_rhs, epi_gate, None,
                rhs_dep_fn=lambda kc: ssd_toks[kc] if kc < 8 else mam_toks[kc - 8])
        fm_dump(blk, "C_gate")

        P.wait("sc", sdep)
        prev_use = None
        for j in range(NTT):
            xt = T["x_tm"][:, j * 1024:(j + 1) * 1024]
            yt = af(C_Y + j * 1024, 1024)
            if prev_use is not None:
                P.wait("sc", prev_use)
            t = AV(af(ZT, 1024), xt, ACT.Identity,
                   scale=T["wtmp"][:, 4 + j:5 + j], bias=T["wtmp"][:, j:j + 1],
                   mark=True)
            P.wait("ve", t)
            TT("ve", yt, yt, af(ZT, 1024), ALU.add)
            prev_use = TT("ve", xt, xt, yt, ALU.add, mark=True)

        yln_done = [None]

        def yln_evac(ps, c_lo, j, tt, bank):
            P.wait("ve", tt)
            last = None
            for m in range(4):
                c = c_lo + m
                last = TS("ve", ab(AB_XN + c * 512 + j * 128, 128),
                          ps[:, m * 128:(m + 1) * 128],
                          T["fng"][:, c:c + 1], T["fnb"][:, c:c + 1],
                          ALU.mult, ALU.add, mark=(m == 3))
            free_bank(bank, last)
            yln_done[0] = last

        last_tr2 = [None]
        deps = ln_stats_all(lambda j: af(C_Y + j * 1024, 1024))
        for j in range(NTT):
            yt_off = C_Y + j * 1024
            if last_tr2[0] is not None:
                P.wait("sc", last_tr2[0])
            tz = normalize_zt(af(yt_off, 1024), j, deps[j])
            P.wait("te", tz)
            last_tr2[0] = tm_to_fm(ZT, j, yln_evac)

        hgel_done = [None]
        hgel_toks = {}

        def epi_f1(bank, oc, te_tok):
            P.wait("sc", te_tok)
            t = AV(ab(AB_HQ + oc * 512, 512), PSUM[bank][:, 0:512], ACT.Gelu,
                   bias=T["f1b"][:, oc:oc + 1], mark=True)
            free_bank(bank, t)
            hgel_done[0] = t
            hgel_toks[oc] = t

        fm_dump(blk, "C_y")
        mm_pass("f1", lambda kc: ab(AB_XN + kc * 512, 512), epi_f1, yln_done[0])

        def epi_f2(bank, oc, te_tok):
            P.wait("sc", te_tok)
            e1 = AV(af(C_S[5], 512), PSUM[bank][:, 0:512], ACT.Identity,
                    bias=T["f2b"][:, oc:oc + 1], mark=True)
            free_bank(bank, e1)
            P.wait("te", e1)
            fm_to_tm_add(C_S[5], oc)

        mm_pass("f2", lambda kc: ab(AB_HQ + kc * 512, 512), epi_f2, None,
                rhs_dep_fn=lambda kc: hgel_toks[kc])

    prep()
    prep2_done = [None]
    for blk in range(N_BLK):
        x_dep = load_x(blk)
        tap_dump(blk, "load")
        phase_A(blk, x_dep)
        if blk == 0:
            prep2_done[0] = prep2()
        tap_dump(blk, "A")
        if blk == 0:
            P.wait("ve", prep2_done[0])
            P.wait("sc", prep2_done[0])
        phase_B(blk)
        tap_dump(blk, "B")
        phase_C(blk)
        tap_dump(blk, "C")
        phase_B(blk)
        tap_dump(blk, "D")
        store_x(blk)

    n_oq = P.q["oq"]
    if n_oq:
        P.wait("sy", ("oq", 0, ((n_oq + 1) // 2) * 16))
        P.wait("sy", ("oq", 1, (n_oq // 2) * 16))

    with stack:
        sems = {}
        with contextlib.ExitStack() as semstack:
            for e in ENGINES:
                sems[e] = semstack.enter_context(nc.semaphore("sem_" + e))
            dsems = {}
            for c in DMA_CLASSES:
                for i in (0, 1):
                    dsems[(c, i)] = semstack.enter_context(
                        nc.semaphore(f"dsem_{c}{i}"))
            P.sem = sems
            P.dsem = dsems
            with nc.Block() as block:
                P.eng = {"sy": nc.sync, "ve": nc.vector, "sc": nc.scalar,
                         "te": nc.tensor, "gp": nc.gpsimd}

                @block.sync
                def _(eng):
                    for f in P.ops["sy"]:
                        f()

                @block.vector
                def _(eng):
                    for f in P.ops["ve"]:
                        f()

                @block.scalar
                def _(eng):
                    for f in P.ops["sc"]:
                        f()

                @block.tensor
                def _(eng):
                    for f in P.ops["te"]:
                        f()

                @block.gpsimd
                def _(eng):
                    for f in P.ops["gp"]:
                        f()
    return nc



def kernel(**inputs):
    g = host_prepare(inputs)
    x = np.asarray(inputs["L0_attn_a_x"], np.float32)
    nc = build_program()
    in_maps = []
    for core in range(N_CORES):
        m = dict(g)
        m["x"] = np.ascontiguousarray(x[core * B_LOC:(core + 1) * B_LOC])
        in_maps.append(m)
    res = run_bass_kernel_spmd(nc, in_maps, core_ids=list(range(N_CORES)))
    out = np.concatenate([np.asarray(r["out"]) for r in res.results], axis=0)
    return out.reshape(B_FULL, T_FULL, D)


# version 09
# speedup vs baseline: 1.0236x; 1.0032x over previous
import contextlib
import numpy as np
import ml_dtypes
import concourse.bass as bass
import concourse.mybir as mybir
from concourse.bass_utils import run_bass_kernel_spmd

F32 = mybir.dt.float32
BF16 = mybir.dt.bfloat16
I32 = mybir.dt.int32
U8 = mybir.dt.uint8
ALU = mybir.AluOpType
ACT = mybir.ActivationFunctionType
AX = mybir.AxisListType
bf16 = ml_dtypes.bfloat16

D = 1024
T_FULL = 2048
B_FULL = 16
N_CORES = 8
B_LOC = B_FULL // N_CORES
TB = 512
N_BLK = B_LOC * (T_FULL // TB)
NTT = TB // 128
NCD = D // 128

LN2 = float(np.log(2.0))
INV_LN2 = float(1.0 / np.log(2.0))
BIG = 3.0e38

W_SPECS = {
    "wdt":  (1024, 1024, "sign"),
    "wx":   (1024, 2048, "sign"),
    "w1":   (1024, 4096, "cast"),
    "w2":   (4096, 1024, "e4m3"),
    "fr":   (1024, 1024, "cast"),
    "dp":   (1024, 1024, "cast"),
    "bp":   (1024, 1024, "cast"),
    "cp":   (1024, 1024, "cast"),
    "m1":   (1024, 2048, "cast"),
    "m2":   (2048, 1024, "cast"),
    "gate": (2048, 1024, "cast"),
    "f1":   (1024, 4096, "cast"),
    "f2":   (4096, 1024, "cast"),
}

BIAS_TILES = [("bdt", 8), ("bx", 16), ("alog", 8), ("ng", 8), ("nb", 8),
              ("frb", 8), ("dpb", 8), ("bpb", 8), ("cpb", 8), ("skp", 8),
              ("m1b", 16), ("m2b", 8), ("gab", 8), ("fng", 8), ("fnb", 8),
              ("f1b", 32), ("f2b", 8), ("b1", 32), ("b2", 8)]


def wg_of(K):
    return 512 if K <= 2048 else 256



def _pieces(w_ko, Wg):
    K, O = w_ko.shape
    nk = K // 128
    nog = O // Wg
    out = np.empty((nog * nk, 128, Wg), np.float32)
    i = 0
    for og in range(nog):
        for kc in range(nk):
            out[i] = w_ko[kc * 128:(kc + 1) * 128, og * Wg:(og + 1) * Wg]
            i += 1
    return np.ascontiguousarray(out)


def _coltile(v, n):
    return np.ascontiguousarray(np.asarray(v, np.float32).reshape(n, 128).T)


def host_prepare(inputs):
    f = np.float32
    g = {}
    g["wdt"] = _pieces(np.asarray(inputs["L0_attn_a_W_dt"], f).T, wg_of(1024))
    g["wx"] = _pieces(np.asarray(inputs["L0_attn_a_W_x"], f).T, wg_of(1024))
    g["w1"] = _pieces(np.asarray(inputs["L0_ffn_w1"], f).T, wg_of(1024))
    g["w2"] = _pieces(np.asarray(inputs["L0_ffn_w2"], f).T, wg_of(4096))
    g["fr"] = _pieces(np.asarray(inputs["L0_attn_b_fr_w"], f), wg_of(1024))
    g["dp"] = _pieces(np.asarray(inputs["L0_attn_b_dp_w"], f), wg_of(1024))
    g["bp"] = _pieces(np.asarray(inputs["L0_attn_b_bp_w"], f), wg_of(1024))
    g["cp"] = _pieces(np.asarray(inputs["L0_attn_b_cp_w"], f), wg_of(1024))
    g["m1"] = _pieces(np.asarray(inputs["L0_attn_b_m1_w"], f), wg_of(1024))
    g["m2"] = _pieces(np.asarray(inputs["L0_attn_b_m2_w"], f), wg_of(2048))
    g["gate"] = _pieces(np.asarray(inputs["L0_attn_b_gate_w"], f), wg_of(2048))
    g["f1"] = _pieces(np.asarray(inputs["L0_attn_b_ffn1_w"], f), wg_of(1024))
    g["f2"] = _pieces(np.asarray(inputs["L0_attn_b_ffn2_w"], f), wg_of(4096))
    g["bdt"] = _coltile(inputs["L0_attn_a_b_dt"], 8)
    g["bx"] = _coltile(inputs["L0_attn_a_b_x"], 16)
    g["alog"] = _coltile(inputs["L0_attn_a_A_log"], 8)
    g["ng"] = _coltile(inputs["L0_attn_b_norm_g"], 8)
    g["nb"] = _coltile(inputs["L0_attn_b_norm_b"], 8)
    g["frb"] = _coltile(inputs["L0_attn_b_fr_b"], 8)
    g["dpb"] = _coltile(inputs["L0_attn_b_dp_b"], 8)
    g["bpb"] = _coltile(inputs["L0_attn_b_bp_b"], 8)
    g["cpb"] = _coltile(inputs["L0_attn_b_cp_b"], 8)
    g["skp"] = _coltile(inputs["L0_attn_b_skip"], 8)
    g["m1b"] = _coltile(inputs["L0_attn_b_m1_b"], 16)
    g["m2b"] = _coltile(inputs["L0_attn_b_m2_b"], 8)
    g["gab"] = _coltile(inputs["L0_attn_b_gate_b"], 8)
    g["fng"] = _coltile(inputs["L0_attn_b_ffn_g"], 8)
    g["fnb"] = _coltile(inputs["L0_attn_b_ffn_bn"], 8)
    g["f1b"] = _coltile(inputs["L0_attn_b_ffn1_b"], 32)
    g["f2b"] = _coltile(inputs["L0_attn_b_ffn2_b"], 8)
    g["b1"] = _coltile(inputs["L0_ffn_b1"], 32)
    g["b2"] = _coltile(inputs["L0_ffn_b2"], 8)
    g["ident"] = np.eye(128, dtype=np.float32)
    return g



ENGINES = ("sy", "ve", "sc", "te", "gp")
DEBUG_POINT = ["endA"]
DMA_CLASSES = ("bq", "sq", "wq", "xq", "pw", "oq")
DMA_ENGINE = {"bq": "sy", "sq": "sy", "wq": "sy", "xq": "sy",
              "pw": "gp", "oq": "gp"}


class Prog:
    def __init__(self):
        self.ops = {e: [] for e in ENGINES}
        self.n = {e: 0 for e in ENGINES}
        self.waited = {e: {} for e in ENGINES}
        self.q = {c: 0 for c in DMA_CLASSES}
        self.eng = None
        self.sem = None
        self.dsem = None

    def emit(self, e, fn):
        self.ops[e].append(fn)
        return None

    def mark(self, e, fn):
        self.n[e] += 1
        tok = self.n[e]
        def wrapped(_fn=fn, _e=e):
            _fn().then_inc(self.sem[_e], 1)
        self.ops[e].append(wrapped)
        return (e, tok)

    def wait(self, e, dep):
        if dep is None:
            return
        if dep[0] in ENGINES:
            kind, tok = dep
            assert tok <= self.n[kind], \
                f"deadlock: {e} waits {kind}@{tok} > {self.n[kind]}"
            if kind == e:
                return
            if self.waited[e].get(kind, 0) >= tok:
                return
            self.waited[e][kind] = tok
            def w(_k=kind, _t=tok, _e=e):
                self.eng[_e].wait_ge(self.sem[_k], _t)
            self.ops[e].append(w)
        else:
            cls, idx, val = dep
            key = (cls, idx)
            if self.waited[e].get(key, 0) >= val:
                return
            self.waited[e][key] = val
            def w(_key=key, _v=val, _e=e):
                self.eng[_e].wait_ge(self.dsem[_key], _v)
            self.ops[e].append(w)

    def dma(self, cls, out_fn, in_fn, pre=None):
        eng = DMA_ENGINE[cls]
        if isinstance(pre, list):
            for p in pre:
                self.wait(eng, p)
        else:
            self.wait(eng, pre)
        i = self.q[cls]
        self.q[cls] += 1
        idx = i % 2
        val = ((i // 2) + 1) * 16
        def d(_o=out_fn, _i=in_fn, _k=(cls, idx), _e=eng):
            self.eng[_e].dma_start(out=_o(), in_=_i()).then_inc(self.dsem[_k], 16)
        self.ops[eng].append(d)
        return (cls, idx, val)

    def tok(self, e):
        return (e, self.n[e])


def build_program(debug_taps=(), debug_fm=()):
    nc = bass.Bass("TRN2", target_bir_lowering=False)
    P = Prog()
    stack = contextlib.ExitStack()

    dram = {}
    dram["x"] = nc.dram_tensor("x", [B_LOC, T_FULL, D], F32, kind="ExternalInput").ap()
    dram["out"] = nc.dram_tensor("out", [B_LOC, T_FULL, D], F32,
                                 kind="ExternalOutput").ap()
    for w, (K, O, _) in W_SPECS.items():
        Wg = wg_of(K)
        npc = (K // 128) * (O // Wg)
        dram[w] = nc.dram_tensor(w, [npc, 128, Wg], F32, kind="ExternalInput").ap()
        dram["scr_" + w] = nc.dram_tensor("scr_" + w, [npc, 128, Wg], BF16,
                                          kind="Internal").ap()
    for b, n in BIAS_TILES:
        dram[b] = nc.dram_tensor(b, [128, n], F32, kind="ExternalInput").ap()
    dram["ident"] = nc.dram_tensor("ident", [128, 128], F32,
                                   kind="ExternalInput").ap()
    fmtaps = {}
    for ent in debug_fm:
        tag, off, cols = ent[0], ent[1], ent[2]
        point = ent[3] if len(ent) > 3 else "endA"
        dt_ = BF16 if (isinstance(off, str) and off.startswith("ab:")) else F32
        fmtaps[tag] = (nc.dram_tensor("fmtap_" + tag, [128, cols], dt_,
                                      kind="ExternalOutput").ap(), off, cols, point)
    taps = {}
    for blk, phase in debug_taps:
        taps[(blk, phase)] = nc.dram_tensor(f"tap_{blk}_{phase}", [TB, D], F32,
                                            kind="ExternalOutput").ap()

    T = {}
    def sbuf(name, cols, dtype=F32):
        T[name] = stack.enter_context(nc.sbuf_tensor("sb_" + name, [128, cols], dtype))

    sbuf("ident", 128)
    sbuf("ones1", 128)
    sbuf("zeros", 512)
    sbuf("R", 512)
    sbuf("rowbuf", 640)
    for b, n in BIAS_TILES:
        sbuf(b, n)
    sbuf("acol", 8)
    sbuf("frbh", 8); sbuf("dpbh", 8); sbuf("gabh", 8)
    sbuf("b2q", 8)
    sbuf("wsdt", 2); sbuf("wsx", 2)
    sbuf("wtmp", 8)
    sbuf("cb", 8)
    for s in ["st_sum", "st_ssq", "st_mneg", "st_var", "st_r", "st_rv",
              "st_mrneg", "st_am", "st_scf", "st_sum2", "st_ssq2", "st_am2"]:
        sbuf(s, 8)
    sbuf("wacc", 16)
    sbuf("ccarry", 8)
    sbuf("cscarry", 8)
    sbuf("x_tm", NTT * 1024)
    sbuf("AF", 22016)
    sbuf("AB", 28672, BF16)
    sbuf("wbuf0", 8192, BF16)
    sbuf("wbuf1", 8192, BF16)
    sbuf("stri", 2048, I32)
    sbuf("strm", 512, U8)

    PSUM = {}
    for p in ["pm0", "pm1", "pm2", "pm3", "pm4", "pm5", "pt0", "pt1"]:
        PSUM[p] = stack.enter_context(nc.psum_tensor(p, [128, 512], F32))

    def af(off, n):
        return T["AF"][:, off:off + n]

    def ab(off, n):
        return T["AB"][:, off:off + n]

    def afb(off, n):
        return T["AF"].bitcast(BF16)[:, 2 * off:2 * off + n]

    def ap3(tname, off, stride, n, w):
        h = T[tname]
        row = h.shape[1] if hasattr(h, "shape") else None
        hh = h[:, 0:1]
        rowlen = hh.ap[0][0]
        return bass.AP(hh.tensor, off, [[rowlen, 128], [stride, n], [1, w]])

    def psum3(bank, n, w):
        hh = PSUM[bank][:, 0:1]
        rowlen = hh.ap[0][0]
        return bass.AP(hh.tensor, 0, [[rowlen, 128], [w, n], [1, w]])

    ZT = 0
    WSTG = [1024, 3072]
    QV = [(5120, 7168), (9216, 11264)]
    QG = [13312, 15360]
    A_KF = 1024
    A_DT = 2048
    A_GN = 6144
    A_TC = 10240
    A_PB = 14336
    A_S = [18440 + i * 512 for i in range(6)]
    B_HB = [1024 + i * 512 for i in range(4)]
    B_QV = [(3072 + i * 1024, 3584 + i * 1024) for i in range(4)]
    B_QG = [7168 + i * 512 for i in range(4)]
    B_S = [9216 + i * 512 for i in range(4)]
    C_XN = 1024
    C_MB = 5120
    C_SS = 9216
    C_Y = 13312
    C_S = [17408 + i * 512 for i in range(6)]

    WQB = [0, 2048]
    AB_K = 0
    AB_XN = 0
    AB_LOW = 4096
    AB_HIGH = 8192
    AB_GEL = 12288
    AB_MAM = 20480
    AB_SSD = 24576
    AB_HQ = 4096

    def TT(e, out, a, b, op, mark=False):
        f = lambda: P.eng[e].tensor_tensor(out, a, b, op)
        return P.mark(e, f) if mark else P.emit(e, f)

    def TS(e, out, a, s1, s2, op0, op1=None, mark=False):
        if op1 is None:
            f = lambda: P.eng[e].tensor_scalar(out, a, s1, None, op0)
        else:
            f = lambda: P.eng[e].tensor_scalar(out, a, s1, s2, op0, op1)
        return P.mark(e, f) if mark else P.emit(e, f)

    def STT(e, out, a, s, b, op0, op1, accum=None, mark=False):
        if accum is None:
            f = lambda: P.eng[e].scalar_tensor_tensor(out, a, s, b, op0, op1)
        else:
            f = lambda: P.eng[e].scalar_tensor_tensor(out, a, s, b, op0, op1,
                                                      accum_out=accum)
        return P.mark(e, f) if mark else P.emit(e, f)

    def CP(e, out, a, mark=False):
        f = lambda: P.eng[e].tensor_copy(out, a)
        return P.mark(e, f) if mark else P.emit(e, f)

    def AV(out, a, func, bias=0.0, scale=1.0, mark=False):
        f = lambda: P.eng["sc"].activation(out=out, in_=a, func=func,
                                           bias=bias, scale=scale)
        return P.mark("sc", f) if mark else P.emit("sc", f)

    def AVA(out, a, func, accum, mark=False):
        f = lambda: P.eng["sc"].activation(out=out, in_=a, func=func,
                                           accum_out=accum)
        return P.mark("sc", f) if mark else P.emit("sc", f)

    def RED(out, a, op="sum", absolute=False, mark=False):
        if op == "max":
            f = lambda: P.eng["ve"].reduce_max(out, a, axis=AX.X,
                                               apply_absolute_value=absolute)
        else:
            f = lambda: P.eng["ve"].reduce_sum(out, a, axis=AX.X,
                                               apply_absolute_value=absolute)
        return P.mark("ve", f) if mark else P.emit("ve", f)

    def RECIP(out, a, mark=False):
        f = lambda: P.eng["ve"].reciprocal(out, a)
        return P.mark("ve", f) if mark else P.emit("ve", f)

    def SCAN(out, d0, d1, init, op0, op1, mark=False):
        f = lambda: P.eng["ve"].tensor_tensor_scan(out=out, data0=d0, data1=d1,
                                                   initial=init, op0=op0, op1=op1)
        return P.mark("ve", f) if mark else P.emit("ve", f)

    def SEL(out, mask, t_, f_, mark=False):
        f = lambda: P.eng["ve"].select(out, mask, t_, f_)
        return P.mark("ve", f) if mark else P.emit("ve", f)

    def MM(psum, lhsT, rhs, start, stop, mark=False):
        f = lambda: P.eng["te"].matmul(psum, lhsT, rhs, start=start, stop=stop)
        return P.mark("te", f) if mark else P.emit("te", f)

    def TR(psum, a, mark=False):
        f = lambda: P.eng["te"].transpose(psum, a, T["ident"][:])
        return P.mark("te", f) if mark else P.emit("te", f)

    def MS(e, out, val, mark=False):
        f = lambda: P.eng[e].memset(out, val)
        return P.mark(e, f) if mark else P.emit(e, f)

    psum_free = {}
    rot = {"pm": 0, "pt": 0}

    def next_bank(kind):
        names = (["pm0", "pm1", "pm2", "pm3", "pm4", "pm5"] if kind == "pm"
                 else ["pt0", "pt1"])
        b = names[rot[kind] % len(names)]
        rot[kind] += 1
        if b in psum_free:
            P.wait("te", psum_free[b])
        return b

    def free_bank(b, dep):
        psum_free[b] = dep

    wbuf_free = [None, None]

    def quant_chain(dst, src, n, man, clamp, maxv, pos, sv, sg, i32_off):
        sa, sb_ = af(sv[0], n), af(sv[1], n)
        sG = af(sg, n)
        si = T["stri"][:, i32_off:i32_off + n]
        if pos:
            t1 = AV(sb_, src, ACT.Ln, bias=T["cb"][:, 0:1], mark=True)
            c1 = INV_LN2
        else:
            AV(sa, src, ACT.Square)
            t1 = AV(sb_, sa, ACT.Ln, bias=T["cb"][:, 0:1], mark=True)
            c1 = 0.5 * INV_LN2
        narrow = n < 128
        if narrow:
            si2 = T["stri"][:, i32_off + n:i32_off + 2 * n]
            sa2 = af(sg + n, n)
        P.wait("ve", t1)
        if narrow:
            tn = TS("ve", si, sb_, -c1, 0.5, ALU.mult, ALU.add, mark=True)
            P.wait("gp", tn)
            tg0 = CP("gp", si2, si, mark=True)
            P.wait("ve", tg0)
            tv1 = TS("ve", sa, si2, float(clamp), float(man), ALU.min, ALU.add,
                     mark=True)
        else:
            TS("ve", si, sb_, -c1, 0.5, ALU.mult, ALU.add)
            tv1 = TS("ve", sa, si, float(clamp), float(man), ALU.min, ALU.add,
                     mark=True)
        P.wait("sc", tv1)
        if not narrow:
            sb16 = afb(sv[1], n)
            sg16 = afb(sg, n)
            AV(sb16, sa, ACT.Exp, scale=LN2)
            t2 = AV(sg16, sa, ACT.Exp, scale=-LN2, mark=True)
            P.wait("ve", t2)
            src_done = TT("ve", si, src, sb16, ALU.mult, mark=True)
            tv2 = TT("ve", sb_, si, sg16, ALU.mult, mark=True)
            if pos:
                P.wait("sc", tv2)
                ts = AV(sG, sb_, ACT.Relu, bias=T["cb"][:, 3:4], mark=True)
                P.wait("ve", ts)
                tg = STT("ve", dst, sG, BIG, sb_, ALU.mult, ALU.add, mark=True)
            else:
                TS("ve", sG, sb_, float(maxv), -float(maxv), ALU.min, ALU.max)
                TT("ve", sG, sb_, sG, ALU.subtract)
                tg = STT("ve", dst, sG, BIG, sb_, ALU.mult, ALU.add, mark=True)
            return src_done, tg
        AV(sb_, sa, ACT.Exp, scale=LN2)
        t2 = AV(sa, sa, ACT.Exp, scale=-LN2, mark=True)
        P.wait("ve", t2)
        if narrow:
            src_done = TT("ve", si, src, sb_, ALU.mult, mark=True)
            P.wait("gp", src_done)
            CP("gp", si2, si)
            tg1 = CP("gp", sa2, sa, mark=True)
            P.wait("ve", tg1)
            tv2 = TT("ve", sb_, si2, sa2, ALU.mult, mark=True)
            P.wait("gp", tv2)
            TT("gp", sG, sb_, sb_, ALU.mult)
            TS("gp", sG, sG, float(maxv) * float(maxv), None, ALU.is_gt)
            TS("gp", sG, sG, BIG, 1.0, ALU.mult, ALU.add)
            tg = TT("gp", dst, sb_, sG, ALU.mult, mark=True)
            return src_done, tg
        raise AssertionError("wide path handled above")

    stg_free = {0: None, 1: None}
    qb_free = {0: None, 1: None}
    prep_w_done = {}
    qpar = [0]
    qprev = {0: None, 1: None}

    def prep_stream(w, batch_cb):
        K, O, _ = W_SPECS[w]
        Wg = wg_of(K)
        npc = (K // 128) * (O // Wg)
        per = 2048 // Wg
        for bi, i0 in enumerate(range(0, npc, per)):
            n = min(per, npc - i0)
            sl = bi % 2
            tok = P.dma("sq",
                        lambda _s=sl, _n=n, _Wg=Wg: af(WSTG[_s], _n * _Wg),
                        lambda _w=w, _i=i0, _n=n:
                            dram[_w][_i:_i + _n].rearrange("n p w -> p n w"),
                        pre=stg_free[sl])
            P.wait("ve", tok)
            P.wait("sc", tok)
            stg_free[sl] = batch_cb(WSTG[sl], n * Wg, i0, n, sl)

    def wscale_pass(w, wtile):
        K, O, _ = W_SPECS[w]
        nb = [0]
        def cb(off, ncols, i0, n, sl):
            tv = RED(T["wacc"][:, nb[0]:nb[0] + 1], af(off, ncols), "sum",
                     absolute=True, mark=True)
            nb[0] += 1
            return tv
        prep_stream(w, cb)
        assert nb[0] <= 8
        tv = P.tok("ve")
        P.wait("gp", tv)
        tg = CP("gp", T["st_sum2"][:, 0:nb[0]], T["wacc"][:, 0:nb[0]], mark=True)
        P.wait("ve", tg)
        t1 = RED(T["wtmp"][:, 1:2], T["st_sum2"][:, 0:nb[0]], "sum", mark=True)
        P.wait("gp", t1)
        tg = CP("gp", T["wtmp"][:, 0:1], T["wtmp"][:, 1:2], mark=True)
        P.wait("te", tg)
        bank = next_bank("pt")
        tt = TR(PSUM[bank][0:1, 0:128], T["wtmp"][:, 0:1], mark=True)
        P.wait("ve", tt)
        CP("ve", T["rowbuf"][0:1, 0:128], PSUM[bank][0:1, 0:128])
        tv = RED(T["wtmp"][0:1, 2:3], T["rowbuf"][0:1, 0:128], "sum", mark=True)
        free_bank(bank, tv)
        P.wait("gp", tv)
        tg = TS("gp", T["wtmp"][0:1, 3:4], T["wtmp"][0:1, 2:3],
                1.0 / (K * O), 1e-5, ALU.mult, ALU.add, mark=True)
        bank2 = next_bank("pt")
        P.wait("te", tg)
        tm = MM(PSUM[bank2][:, 0:1], T["ones1"][0:1, 0:128],
                T["wtmp"][0:1, 3:4], True, True, mark=True)
        P.wait("ve", tm)
        tv = CP("ve", T[wtile][:, 0:1], PSUM[bank2][:, 0:1], mark=True)
        P.wait("gp", tv)
        tv = TS("gp", T[wtile][:, 1:2], T[wtile][:, 0:1], 0.5, None, ALU.mult,
                mark=True)
        free_bank(bank2, tv)

    def prep_weight(w, K, O, q):
        wt = "wsdt" if w == "wdt" else "wsx"
        def cb(off, ncols, i0, n, sl):
            qb = ab(WQB[sl], ncols)
            if qb_free[sl] is not None:
                P.wait("ve", qb_free[sl])
                P.wait("gp", qb_free[sl])
            src = af(off, ncols)
            if q == "cast":
                done = CP("ve", qb, src, mark=True)
                wdep = done
            elif q == "sign":
                par = qpar[0] % 2
                qpar[0] += 1
                sa = af(QV[par][0], ncols)
                sb_ = af(QV[par][1], ncols)
                if qprev[par] is not None:
                    P.wait("ve", qprev[par])
                STT("ve", sa, src, -1.0, src, ALU.mult, ALU.max)
                tv0 = TS("ve", sa, sa, T[wt][:, 1:2], None, ALU.is_gt, mark=True)
                P.wait("sc", tv0)
                ts = AV(sb_, src, ACT.Sign, mark=True)
                P.wait("ve", ts)
                done = TT("ve", qb, sa, sb_, ALU.mult, mark=True)
                qprev[par] = done
                wdep = done
            else:
                par = qpar[0] % 2
                qpar[0] += 1
                if qprev[par] is not None:
                    P.wait("ve", qprev[par])
                    P.wait("sc", qprev[par])
                done, tg = quant_chain(qb, src, ncols, man=3, clamp=6,
                                       maxv=240.0, pos=False, sv=QV[par],
                                       sg=QG[par], i32_off=0 if par == 0 else 0)
                qprev[par] = tg
                wdep = tg
            wtok = P.dma("pw",
                         lambda _w=w, _i=i0, _n=n:
                             dram["scr_" + _w][_i:_i + _n].rearrange("n p w -> p n w"),
                         lambda _q=qb: _q,
                         pre=wdep)
            qb_free[sl] = wtok
            prep_w_done[w] = wtok
            return done
        prep_stream(w, cb)

    def prep():
        toks = []
        for b, _ in BIAS_TILES:
            toks.append(P.dma("bq", lambda _b=b: T[_b][:], lambda _b=b: dram[_b]))
        toks.append(P.dma("bq", lambda: T["ident"][:], lambda: dram["ident"]))
        last = toks[-1]
        prev = toks[-2]
        for e in ("ve", "sc", "te"):
            P.wait(e, last)
            P.wait(e, prev)
        MS("ve", T["ones1"][:, :], 1.0)
        MS("ve", T["zeros"][:], 0.0)
        MS("ve", T["cb"][:, 0:1], 1e-38)
        MS("ve", T["cb"][:, 1:2], 1e-5)
        MS("ve", T["cb"][:, 2:3], 1.0)
        MS("ve", T["cb"][:, 3:4], -240.0)
        MS("ve", T["cb"][:, 5:6], 3.0 * LN2)
        MS("ve", T["cb"][:, 6:7], -3.0 * LN2)
        cb_tok = MS("ve", T["cb"][:, 4:5], 0.5, mark=True)
        P.wait("sc", cb_tok)
        TS("ve", T["frbh"][:], T["frb"][:], 0.5, None, ALU.mult)
        TS("ve", T["dpbh"][:], T["dpb"][:], 0.5, None, ALU.mult)
        TS("ve", T["gabh"][:], T["gab"][:], 0.5, None, ALU.mult)
        AV(T["st_sum"][:], T["alog"][:], ACT.Abs)
        AV(T["st_ssq"][:], T["st_sum"][:], ACT.Exp, scale=-1.0)
        t = AV(T["st_var"][:], T["st_ssq"][:], ACT.Ln, bias=T["cb"][:, 2:3], mark=True)
        P.wait("gp", t)
        TS("gp", T["st_mneg"][:], T["alog"][:], 0.0, None, ALU.max)
        TT("gp", T["st_mneg"][:], T["st_mneg"][:], T["st_var"][:], ALU.add)
        tv = TS("gp", T["acol"][:], T["st_mneg"][:], -1.0, None, ALU.mult,
                mark=True)
        P.wait("sc", tv)
        P.wait("ve", tv)
        quant_chain(T["b2q"][:], T["b2"][:], 8, man=10, clamp=14, maxv=65504.0,
                    pos=False, sv=QV[0], sg=QG[0], i32_off=0)
        wscale_pass("wdt", "wsdt")
        wscale_pass("wx", "wsx")
        for w in ["wdt", "wx", "w1", "w2"]:
            K, O, q = W_SPECS[w]
            prep_weight(w, K, O, q)

    def prep2():
        stg2 = [ab(4096, 4096).bitcast(F32), ab(8192, 4096).bitcast(F32)]
        out2 = [ab(12288, 2048), ab(14336, 2048)]
        sfree = {0: None, 1: None}
        ofree = {0: None, 1: None}
        for w in ["fr", "dp", "bp", "cp", "m1", "m2", "gate", "f1", "f2"]:
            K, O, _ = W_SPECS[w]
            Wg = wg_of(K)
            npc = (K // 128) * (O // Wg)
            per = 2048 // Wg
            for bi, i0 in enumerate(range(0, npc, per)):
                n = min(per, npc - i0)
                sl = bi % 2
                tok = P.dma("sq",
                            lambda _s=sl, _n=n, _Wg=Wg: stg2[_s][:, 0:_n * _Wg],
                            lambda _w=w, _i=i0, _n=n:
                                dram[_w][_i:_i + _n].rearrange("n p w -> p n w"),
                            pre=sfree[sl])
                P.wait("sc", tok)
                if ofree[sl] is not None:
                    P.wait("sc", ofree[sl])
                ts = AV(out2[sl][:, 0:n * Wg], stg2[sl][:, 0:n * Wg],
                        ACT.Identity, mark=True)
                sfree[sl] = ts
                wtok = P.dma("pw",
                             lambda _w=w, _i=i0, _n=n:
                                 dram["scr_" + _w][_i:_i + _n].rearrange("n p w -> p n w"),
                             lambda _sl=sl, _n=n, _Wg=Wg: out2[_sl][:, 0:_n * _Wg],
                             pre=ts)
                ofree[sl] = wtok
                prep_w_done[w] = wtok
        return prep_w_done["f2"]

    def ln_stats_all(x_ap_fn, js=range(NTT)):
        trs = {}
        for j in js:
            sl = slice(j, j + 1)
            RED(T["st_sum"][:, sl], x_ap_fn(j), "sum")
            trs[j] = STT("ve", af(6144, 1024), x_ap_fn(j), 1.0, x_ap_fn(j),
                         ALU.bypass, ALU.mult, accum=T["st_ssq"][:, sl],
                         mark=True)
        deps = {}
        for j in js:
            sl = slice(j, j + 1)
            P.wait("gp", trs[j])
            TS("gp", T["st_mneg"][:, sl], T["st_sum"][:, sl], -1.0 / D, None,
               ALU.mult)
            TT("gp", T["st_var"][:, sl], T["st_mneg"][:, sl],
               T["st_mneg"][:, sl], ALU.mult)
            TS("gp", T["st_ssq2"][:, sl], T["st_ssq"][:, sl], 1.0 / D, None,
               ALU.mult)
            tg = TT("gp", T["st_var"][:, sl], T["st_ssq2"][:, sl],
                    T["st_var"][:, sl], ALU.subtract, mark=True)
            P.wait("sc", tg)
            AV(T["st_r"][:, sl], T["st_var"][:, sl], ACT.Ln, bias=T["cb"][:, 1:2])
            t = AV(T["st_r"][:, sl], T["st_r"][:, sl], ACT.Exp, scale=-0.5,
                   mark=True)
            P.wait("gp", t)
            CP("gp", T["st_rv"][:, sl], T["st_r"][:, sl])
            deps[j] = TT("gp", T["st_mrneg"][:, sl], T["st_mneg"][:, sl],
                         T["st_rv"][:, sl], ALU.mult, mark=True)
        return deps

    def ln_stats(x_ap, j):
        sl = slice(j, j + 1)
        RED(T["st_sum"][:, sl], x_ap, "sum")
        tr = STT("ve", af(6144, 1024), x_ap, 1.0, x_ap, ALU.bypass, ALU.mult,
                 accum=T["st_ssq"][:, sl], mark=True)
        P.wait("gp", tr)
        TS("gp", T["st_mneg"][:, sl], T["st_sum"][:, sl], -1.0 / D, None,
           ALU.mult)
        TT("gp", T["st_var"][:, sl], T["st_mneg"][:, sl], T["st_mneg"][:, sl],
           ALU.mult)
        TS("gp", T["st_ssq2"][:, sl], T["st_ssq"][:, sl], 1.0 / D, None,
           ALU.mult)
        tg = TT("gp", T["st_var"][:, sl], T["st_ssq2"][:, sl],
                T["st_var"][:, sl], ALU.subtract, mark=True)
        P.wait("sc", tg)
        AV(T["st_r"][:, sl], T["st_var"][:, sl], ACT.Ln, bias=T["cb"][:, 1:2])
        t = AV(T["st_r"][:, sl], T["st_r"][:, sl], ACT.Exp, scale=-0.5,
               mark=True)
        P.wait("gp", t)
        CP("gp", T["st_rv"][:, sl], T["st_r"][:, sl])
        return TT("gp", T["st_mrneg"][:, sl], T["st_mneg"][:, sl],
                  T["st_rv"][:, sl], ALU.mult, mark=True)

    def normalize_zt(x_ap, j, dep):
        P.wait("sc", dep)
        return AV(af(ZT, 1024), x_ap, ACT.Identity,
                  bias=T["st_mrneg"][:, j:j + 1], scale=T["st_rv"][:, j:j + 1],
                  mark=True)

    def tm_to_fm(src_off, j, evac, src_tensor="AF"):
        tt = None
        for half in range(2):
            bank = next_bank("pt")
            for m in range(4):
                c = half * 4 + m
                src = af(src_off + c * 128, 128)
                tt = TR(PSUM[bank][:, m * 128:(m + 1) * 128], src, mark=(m == 3))
            evac(PSUM[bank][:, 0:512], half * 4, j, tt, bank)
        return tt

    def fm_to_tm_add(src_off, c, dst_tensor="x_tm"):
        bank = next_bank("pt")
        tt = None
        for j in range(NTT):
            tt = TR(PSUM[bank][:, j * 128:(j + 1) * 128],
                    af(src_off + j * 128, 128), mark=(j == NTT - 1))
        P.wait("ve", tt)
        dst = ap3(dst_tensor, c * 128, 1024, NTT, 128)
        tv = TT("ve", dst, dst, psum3(bank, NTT, 128), ALU.add, mark=True)
        free_bank(bank, tv)

    def mm_pass(w, rhs_fn, epi, rhs_dep, rhs_dep_fn=None):
        K, O, _ = W_SPECS[w]
        Wg = wg_of(K)
        nk = K // 128
        nog = O // Wg
        for og in range(nog):
            sl = og % 2
            wb = T["wbuf0" if sl == 0 else "wbuf1"]
            tok = P.dma("wq",
                        lambda _wb=wb, _n=nk * Wg: _wb[:, 0:_n],
                        lambda _w=w, _i=og * nk, _n=nk:
                            dram["scr_" + _w][_i:_i + _n].rearrange("n p w -> p n w"),
                        pre=[wbuf_free[sl], prep_w_done.get(w)])
            P.wait("te", tok)
            if rhs_dep_fn is None:
                if isinstance(rhs_dep, list):
                    for d in rhs_dep:
                        P.wait("te", d)
                else:
                    P.wait("te", rhs_dep)
            for o in range(Wg // 128):
                oc = og * (Wg // 128) + o
                bank = next_bank("pm")
                for kc in range(nk):
                    if rhs_dep_fn is not None:
                        P.wait("te", rhs_dep_fn(kc))
                    MM(PSUM[bank][:, 0:512],
                       wb[:, kc * Wg + o * 128: kc * Wg + (o + 1) * 128],
                       rhs_fn(kc), start=(kc == 0), stop=(kc == nk - 1),
                       mark=(kc == nk - 1))
                epi(bank, oc, P.tok("te"))
            wbuf_free[sl] = P.tok("te")

    x_store_dep = [None]
    final_toks = []

    def load_x(blk):
        seq, t0 = blk // (T_FULL // TB), (blk % (T_FULL // TB)) * TB
        toks = []
        for j in range(NTT):
            toks.append(P.dma("xq",
                lambda _j=j: T["x_tm"][:, _j * 1024:(_j + 1) * 1024],
                lambda _j=j, _s=seq, _t=t0:
                    dram["x"][_s, _t + _j * 128:_t + (_j + 1) * 128, :],
                pre=x_store_dep[0]))
        return toks[-2:]

    def store_x(blk):
        seq, t0 = blk // (T_FULL // TB), (blk % (T_FULL // TB)) * TB
        dep = P.tok("ve")
        toks = []
        for j in range(NTT):
            toks.append(P.dma("oq",
                lambda _j=j, _s=seq, _t=t0:
                    dram["out"][_s, _t + _j * 128:_t + (_j + 1) * 128, :],
                lambda _j=j: T["x_tm"][:, _j * 1024:(_j + 1) * 1024],
                pre=dep))
        x_store_dep[0] = toks[-1]
        final_toks.extend(toks[-2:])

    def tap_dump(blk, phase):
        if (blk, phase) not in taps:
            return
        tap = taps[(blk, phase)]
        dep = P.tok("ve")
        toks = []
        for j in range(NTT):
            toks.append(P.dma("oq",
                lambda _j=j, _t=tap: _t[_j * 128:(_j + 1) * 128, :],
                lambda _j=j: T["x_tm"][:, _j * 1024:(_j + 1) * 1024],
                pre=dep))
        for t_ in toks[-2:]:
            P.wait("ve", t_)
        final_toks.extend(toks[-2:])

    def phase_A(blk, x_dep):
        seq_start = blk % (T_FULL // TB) == 0
        if seq_start:
            for c in range(NCD):
                MS("gp", af(A_PB + c * 513, 513)[:, 0:1], 1.0)
            MS("gp", T["cscarry"][:], 0.0, mark=True)
        kb_done = [None]

        def kb_evac(ps, c_lo, j, tt, bank):
            P.wait("ve", tt)
            dst = bass.AP(T["AB"][:, 0:1].tensor, AB_K + c_lo * 512 + j * 128,
                          [[28672, 128], [512, 4], [1, 128]])
            last = CP("ve", dst, psum3(bank, 4, 128), mark=True)
            free_bank(bank, last)
            kb_done[0] = last

        last_ktr = [None]
        for d in x_dep:
            P.wait("ve", d)
        deps = ln_stats_all(lambda j: T["x_tm"][:, j * 1024:(j + 1) * 1024])
        for j in range(NTT):
            xt = T["x_tm"][:, j * 1024:(j + 1) * 1024]
            tz = normalize_zt(xt, j, deps[j])
            P.wait("ve", tz)
            zt = af(ZT, 1024)
            ta = RED(T["st_am"][:, j:j + 1], zt, "max", absolute=True, mark=True)
            P.wait("gp", ta)
            tg = TS("gp", T["st_am2"][:, j:j + 1], T["st_am"][:, j:j + 1], 1e-5,
                    None, ALU.max, mark=True)
            P.wait("ve", tg)
            tv = RECIP(T["st_scf"][:, j:j + 1], T["st_am2"][:, j:j + 1],
                       mark=True)
            P.wait("gp", tv)
            tg2 = TS("gp", T["st_scf"][:, j:j + 1], T["st_scf"][:, j:j + 1],
                     127.0, None, ALU.mult, mark=True)
            P.wait("sc", tg2)
            if last_ktr[0] is not None:
                P.wait("sc", last_ktr[0])
            t = AV(af(A_KF, 1024), zt, ACT.Identity,
                   scale=T["st_scf"][:, j:j + 1], mark=True)
            P.wait("ve", t)
            CP("ve", T["stri"][:, 0:1024], af(A_KF, 1024))
            tv = TS("ve", af(A_KF, 1024), T["stri"][:, 0:1024], -128.0, 127.0,
                    ALU.max, ALU.min, mark=True)
            P.wait("te", tv)
            last_ktr[0] = tm_to_fm(A_KF, j, kb_evac)

        bank2 = next_bank("pt")
        tprev = None
        P.wait("gp", P.tok("gp"))
        for j in range(NTT):
            sl = j % 2
            rhs = T["rowbuf"][:, sl * 128:(sl + 1) * 128]
            if tprev is not None:
                P.wait("gp", tprev)
            tg = TS("gp", rhs, T["ident"][:], T["st_am2"][:, j:j + 1], None,
                    ALU.mult, mark=True)
            P.wait("te", tg)
            tprev = MM(PSUM[bank2][:, j * 128:(j + 1) * 128], T["ones1"][:],
                       rhs, True, True, mark=True)
        P.wait("ve", tprev)
        tv = TS("ve", T["R"][:], PSUM[bank2][:, 0:512], 1.0 / 127.0, None,
                ALU.mult, mark=True)
        free_bank(bank2, tv)

        kb_dep = kb_done[0]
        epar = [0]

        def epi_dt(bank, oc, te_tok):
            par = epar[0] % 2
            epar[0] += 1
            s0 = af(A_S[par * 2], 512)
            s1 = af(A_S[par * 2 + 1], 512)
            P.wait("ve", te_tok)
            e1 = TT("ve", s0, PSUM[bank][:, 0:512], T["R"][:], ALU.mult,
                    mark=True)
            free_bank(bank, e1)
            tv = TS("ve", s0, s0, T["wsdt"][:, 0:1], T["bdt"][:, oc:oc + 1],
                    ALU.mult, ALU.add, mark=True)
            P.wait("sc", tv)
            AV(s1, s0, ACT.Abs)
            AV(s1, s1, ACT.Exp, scale=-1.0)
            t = AV(s1, s1, ACT.Ln, bias=T["cb"][:, 2:3], mark=True)
            P.wait("ve", t)
            STT("ve", s0, s0, 0.0, s1, ALU.max, ALU.add)
            TS("ve", af(A_DT + oc * 512, 512), s0, 1e-3, 1.0, ALU.max, ALU.min,
               mark=True)

        mm_pass("wdt", lambda kc: ab(AB_K + kc * 512, 512), epi_dt, kb_dep)
        fm_dump(blk, "afterdt")

        last_tanh = {0: None, 1: None}

        def epi_xp(bank, oc, te_tok):
            par = epar[0] % 2
            epar[0] += 1
            s0 = af(A_S[par * 2], 512)
            s1 = af(A_S[par * 2 + 1], 512)
            P.wait("ve", te_tok)
            if last_tanh[par] is not None:
                P.wait("ve", last_tanh[par])
            e1 = TT("ve", s0, PSUM[bank][:, 0:512], T["R"][:], ALU.mult,
                    mark=True)
            free_bank(bank, e1)
            TS("ve", s0, s0, T["wsx"][:, 0:1], T["bx"][:, oc:oc + 1],
               ALU.mult, ALU.add)
            tv = TS("ve", s0, s0, -5.0, 5.0, ALU.max, ALU.min, mark=True)
            if oc < 8:
                TT("ve", af(A_GN + oc * 512, 512), s0,
                   af(A_DT + oc * 512, 512), ALU.mult, mark=True)
            else:
                P.wait("sc", tv)
                last_tanh[par] = AV(af(A_TC + (oc - 8) * 512, 512), s0,
                                    ACT.Tanh, mark=True)

        mm_pass("wx", lambda kc: ab(AB_K + kc * 512, 512), epi_xp, kb_dep)
        fm_dump(blk, "afterxp")

        P.wait("ve", P.tok("sc"))
        last_carry = [None]
        for c in range(NCD):
            if last_carry[0] is not None:
                P.wait("ve", last_carry[0])
            pb = T["AF"][:, A_PB + c * 513: A_PB + (c + 1) * 513]
            dtc = af(A_DT + c * 512, 512)
            tv = TS("ve", af(A_S[0], 512), dtc, T["acol"][:, c:c + 1], -10.0,
                    ALU.mult, ALU.max, mark=True)
            P.wait("sc", tv)
            if c == 0:
                fm_dump(blk, "c0_ua")
            t = AV(af(A_S[1], 512), af(A_S[0], 512), ACT.Exp, mark=True)
            P.wait("ve", t)
            if c == 0:
                fm_dump(blk, "c0_exp")
            SCAN(pb[:, 1:513], af(A_S[1], 512), af(A_S[1], 512), pb[:, 0:1],
                 ALU.mult, ALU.bypass)
            if c == 0:
                fm_dump(blk, "c0_scan")
            RECIP(af(A_S[2], 512), pb[:, 0:512])
            TT("ve", af(A_S[2], 512), af(A_GN + c * 512, 512), af(A_S[2], 512),
               ALU.mult)
            if c == 0:
                fm_dump(blk, "c0_g")
            tsc = SCAN(af(A_S[3], 512), af(A_S[2], 512), af(A_S[2], 512),
                       T["cscarry"][:, c:c + 1], ALU.add, ALU.bypass, mark=True)
            if c == 0:
                fm_dump(blk, "c0_cs")
            tv = TT("ve", af(A_S[4], 512), pb[:, 1:513], af(A_S[3], 512),
                    ALU.mult, mark=True)
            P.wait("gp", tv)
            CP("gp", T["cscarry"][:, c:c + 1], af(A_S[3] + 511, 1))
            last_carry[0] = CP("gp", pb[:, 0:1], pb[:, 512:513], mark=True)
            TT("ve", af(A_S[4], 512), af(A_S[4], 512), af(A_TC + c * 512, 512),
               ALU.mult)
            TT("ve", af(A_S[5], 512), af(A_S[4], 512), af(A_S[4], 512),
               ALU.subtract)
            TS("ve", T["strm"][:], af(A_S[5], 512), 0.0, None, ALU.is_equal)
            tv = SEL(af(A_S[5], 512), T["strm"][:], af(A_S[4], 512),
                     T["zeros"][:], mark=True)
            if c == 0:
                fm_dump(blk, "c0_sel")
            P.wait("te", tv)
            fm_to_tm_add(A_S[5], c)

    def fm_dump(blk, point="endA"):
        if blk != 0 or not fmtaps:
            return
        sel = {t: v for t, v in fmtaps.items() if v[3] == point}
        if not sel:
            return
        dep = P.tok("ve")
        toks = []
        for tag, (tap, off, cols, _pt) in sel.items():
            if isinstance(off, str) and off.startswith("ab:"):
                src_fn = (lambda _o=int(off[3:]), _c=cols: ab(_o, _c))
            elif isinstance(off, str):
                src_fn = (lambda _n=off, _c=cols: T[_n][:, 0:_c])
            else:
                src_fn = (lambda _o=off, _c=cols: af(_o, _c))
            toks.append(P.dma("oq", lambda _t=tap: _t, src_fn, pre=dep))
        for t_ in toks[-2:]:
            P.wait("ve", t_)
        final_toks.extend(toks[-2:])

    def phase_B(blk):
        xn_done = [None]

        def xn_evac(ps, c_lo, j, tt, bank):
            P.wait("ve", tt)
            dst = bass.AP(T["AB"][:, 0:1].tensor, AB_XN + c_lo * 512 + j * 128,
                          [[28672, 128], [512, 4], [1, 128]])
            last = CP("ve", dst, psum3(bank, 4, 128), mark=True)
            free_bank(bank, last)
            xn_done[0] = last

        last_tr = [None]
        deps = ln_stats_all(lambda j: T["x_tm"][:, j * 1024:(j + 1) * 1024])
        for j in range(NTT):
            xt = T["x_tm"][:, j * 1024:(j + 1) * 1024]
            if last_tr[0] is not None:
                P.wait("sc", last_tr[0])
            tz = normalize_zt(xt, j, deps[j])
            P.wait("te", tz)
            last_tr[0] = tm_to_fm(ZT, j, xn_evac)

        qp = [0]
        qdeps = {0: None, 1: None, 2: None, 3: None}
        last_g = [None]
        hq_tok_by_kc = {}

        def epi_h(bank, oc, te_tok):
            par = qp[0] % 4
            qp[0] += 1
            hln = af(B_HB[par], 512)
            sa, sb_ = af(B_QV[par][0], 512), af(B_QV[par][1], 512)
            sG = af(B_QG[par], 512)
            si = T["stri"][:, par * 512:(par + 1) * 512]
            dst = ab(AB_HQ + oc * 512, 512)
            P.wait("ve", te_tok)
            if qdeps[par] is not None:
                P.wait("sc", qdeps[par])
                P.wait("ve", qdeps[par])
            tv0 = TS("ve", hln, PSUM[bank][:, 0:512], T["b1"][:, oc:oc + 1],
                     2.0 ** -6, ALU.add, ALU.max, mark=True)
            P.wait("sc", tv0)
            AV(sa, hln, ACT.Ln)
            AV(si, sa, ACT.Identity, scale=-INV_LN2, bias=T["cb"][:, 4:5])
            sa16 = afb(B_QV[par][0], 512)
            sb16 = afb(B_QV[par][1], 512)
            AV(sa16, si, ACT.Exp, scale=LN2, bias=T["cb"][:, 5:6])
            t2 = AV(sb16, si, ACT.Exp, scale=-LN2, bias=T["cb"][:, 6:7],
                    mark=True)
            P.wait("ve", t2)
            tk = STT("ve", si, PSUM[bank][:, 0:512], T["b1"][:, oc:oc + 1],
                     sa16, ALU.add, ALU.mult, mark=True)
            free_bank(bank, tk)
            tq = STT("ve", sG, si, 0.0, sb16, ALU.max, ALU.mult, mark=True)
            P.wait("sc", tq)
            to = AV(sb_, sG, ACT.Relu, bias=T["cb"][:, 3:4], mark=True)
            P.wait("ve", to)
            tg = STT("ve", dst, sb_, BIG, sG, ALU.mult, ALU.add, mark=True)
            qdeps[par] = tg
            last_g[0] = tg
            hq_tok_by_kc[oc] = tg

        mm_pass("w1", lambda kc: ab(AB_XN + kc * 512, 512), epi_h, xn_done[0])

        op = [0]
        hq_toks = dict(hq_tok_by_kc)

        def epi_o(bank, oc, te_tok):
            par = op[0] % 4
            op[0] += 1
            hb = af(B_HB[par], 512)
            P.wait("sc", te_tok)
            if qdeps[par] is not None:
                P.wait("ve", qdeps[par])
                P.wait("sc", qdeps[par])
            e1 = AV(hb, PSUM[bank][:, 0:512], ACT.Identity,
                    bias=T["b2q"][:, oc:oc + 1], mark=True)
            free_bank(bank, e1)
            _, tg = quant_chain(af(B_S[par], 512), hb, 512,
                                man=10, clamp=14, maxv=65504.0, pos=False,
                                sv=B_QV[par], sg=B_QG[par], i32_off=par * 512)
            qdeps[par] = tg
            P.wait("te", tg)
            fm_to_tm_add(B_S[par], oc)

        mm_pass("w2", lambda kc: ab(AB_HQ + kc * 512, 512), epi_o, None,
                rhs_dep_fn=lambda kc: hq_toks[kc])

    def phase_C(blk):
        seq_start = blk % (T_FULL // TB) == 0
        if seq_start:
            MS("gp", T["ccarry"][:], 0.0, mark=True)
        xn_done = [None]

        def xn_evac(ps, c_lo, j, tt, bank):
            P.wait("ve", tt)
            for m in range(4):
                c = c_lo + m
                dst32 = af(C_XN + c * 512 + j * 128, 128)
                TS("ve", dst32, ps[:, m * 128:(m + 1) * 128],
                   T["ng"][:, c:c + 1], T["nb"][:, c:c + 1], ALU.mult, ALU.add,
                   mark=(m == 3))
            free_bank(bank, P.tok("ve"))
            src32 = bass.AP(T["AF"][:, 0:1].tensor, C_XN + c_lo * 512 + j * 128,
                            [[22016, 128], [512, 4], [1, 128]])
            dstb = bass.AP(T["AB"][:, 0:1].tensor, AB_XN + c_lo * 512 + j * 128,
                           [[28672, 128], [512, 4], [1, 128]])
            last = CP("ve", dstb, src32, mark=True)
            xn_done[0] = last

        last_tr = [None]
        deps = ln_stats_all(lambda j: T["x_tm"][:, j * 1024:(j + 1) * 1024])
        for j in range(NTT):
            xt = T["x_tm"][:, j * 1024:(j + 1) * 1024]
            if last_tr[0] is not None:
                P.wait("sc", last_tr[0])
            tz = normalize_zt(xt, j, deps[j])
            P.wait("te", tz)
            last_tr[0] = tm_to_fm(ZT, j, xn_evac)
        CP("gp", T["wtmp"][:, 4:8], T["st_rv"][:, 0:4])
        sdep = CP("gp", T["wtmp"][:, 0:4], T["st_mrneg"][:, 0:4], mark=True)

        lh_done = [None]

        def epi_fr(bank, oc, te_tok):
            s0 = af(C_S[0], 512)
            s1 = af(C_S[1], 512)
            P.wait("sc", te_tok)
            P.wait("sc", lh_done[0])
            th = AV(s0, PSUM[bank][:, 0:512], ACT.Tanh, scale=0.5,
                    bias=T["frbh"][:, oc:oc + 1])
            tl = AV(s1, s0, ACT.Identity, scale=0.5, bias=T["cb"][:, 4:5], mark=True)
            free_bank(bank, tl)
            P.wait("ve", tl)
            xnc = af(C_XN + oc * 512, 512)
            TT("ve", ab(AB_LOW + oc * 512, 512), xnc, s1, ALU.mult)
            TS("ve", s1, s1, -1.0, 1.0, ALU.mult, ALU.add)
            lh_done[0] = TT("ve", ab(AB_HIGH + oc * 512, 512), xnc, s1,
                            ALU.mult, mark=True)

        fm_dump(blk, "C_xn")
        mm_pass("fr", lambda kc: ab(AB_XN + kc * 512, 512), epi_fr, xn_done[0])
        fm_dump(blk, "C_fr")

        ssd_done = [None]
        ssd_toks = {}
        mam_toks = {}
        nk = NCD
        cpbuf_free = [lh_done[0]]

        def cp_wslice(kc, o):
            return ab(AB_XN + kc * 512 + o * 128, 128)

        def scan_block():
            specs = [("dp", 0), ("bp", 1), ("cp", 2)]
            for og in range(2):
                dtoks = {}
                for w, sl in specs:
                    if sl == 2:
                        dtoks[w] = P.dma("wq",
                            lambda: ab(AB_XN, nk * 512),
                            lambda _w=w, _i=og * nk: dram["scr_" + _w][_i:_i + nk]
                                .rearrange("n p w -> p n w"),
                            pre=[cpbuf_free[0], prep_w_done.get(w)])
                    else:
                        dtoks[w] = P.dma("wq",
                            lambda _wb="wbuf0" if sl == 0 else "wbuf1":
                                T[_wb][:, 0:nk * 512],
                            lambda _w=w, _i=og * nk: dram["scr_" + _w][_i:_i + nk]
                                .rearrange("n p w -> p n w"),
                            pre=[wbuf_free[sl], prep_w_done.get(w)])
                for o in range(4):
                    c = og * 4 + o
                    banks = {}
                    for w, sl in specs:
                        P.wait("te", dtoks[w])
                        bank = next_bank("pm")
                        for kc in range(nk):
                            if sl == 2:
                                lhsT = cp_wslice(kc, o)
                                rhs = ab(AB_HIGH + kc * 512, 512)
                            else:
                                wb = T["wbuf0" if sl == 0 else "wbuf1"]
                                lhsT = wb[:, kc * 512 + o * 128: kc * 512 + (o + 1) * 128]
                                rhs = ab(AB_LOW + kc * 512, 512)
                            MM(PSUM[bank][:, 0:512], lhsT, rhs,
                               start=(kc == 0), stop=(kc == nk - 1),
                               mark=(kc == nk - 1))
                        banks[w] = (bank, P.tok("te"))
                    scan_chunk(c, banks)
                for w, sl in specs:
                    if sl == 2:
                        cpbuf_free[0] = P.tok("te")
                    else:
                        wbuf_free[sl] = P.tok("te")

        carry_guard = [None]

        def scan_chunk(c, banks):
            s0, s1, s2, s3, s4 = (af(C_S[i], 512) for i in range(5))
            if carry_guard[0] is not None:
                P.wait("ve", carry_guard[0])
            xnc = af(C_XN + c * 512, 512)
            bank_a, ta = banks["dp"]
            bank_b, tb = banks["bp"]
            bank_c, tc = banks["cp"]
            P.wait("sc", tc)
            P.wait("sc", ssd_done[0])
            AV(s0, PSUM[bank_a][:, 0:512], ACT.Tanh, scale=0.5,
               bias=T["dpbh"][:, c:c + 1])
            AV(s0, s0, ACT.Identity, scale=0.5, bias=T["cb"][:, 4:5])
            AV(s1, PSUM[bank_b][:, 0:512], ACT.Tanh, bias=T["bpb"][:, c:c + 1])
            t3 = AV(s2, PSUM[bank_c][:, 0:512], ACT.Tanh,
                    bias=T["cpb"][:, c:c + 1], mark=True)
            free_bank(bank_a, t3)
            free_bank(bank_b, t3)
            free_bank(bank_c, t3)
            P.wait("ve", t3)
            TT("ve", s1, s1, xnc, ALU.mult)
            tsc = SCAN(s3, s0, s1, T["ccarry"][:, c:c + 1], ALU.mult, ALU.add,
                       mark=True)
            P.wait("gp", tsc)
            carry_guard[0] = CP("gp", T["ccarry"][:, c:c + 1],
                                af(C_S[3] + 511, 1), mark=True)
            TT("ve", s4, s2, s3, ALU.mult)
            ssd32 = af(C_SS + c * 512, 512)
            STT("ve", ssd32, xnc, T["skp"][:, c:c + 1], s4, ALU.mult, ALU.add)
            ssd_done[0] = CP("ve", ab(AB_SSD + c * 512, 512), ssd32, mark=True)
            ssd_toks[c] = ssd_done[0]

        scan_block()
        fm_dump(blk, "C_scan")

        gel_done = [None]
        gel_toks = {}

        def epi_m1(bank, oc, te_tok):
            P.wait("sc", te_tok)
            t = AV(ab(AB_GEL + oc * 512, 512), PSUM[bank][:, 0:512], ACT.Gelu,
                   bias=T["m1b"][:, oc:oc + 1], mark=True)
            free_bank(bank, t)
            gel_done[0] = t
            gel_toks[oc] = t

        mm_pass("m1", lambda kc: ab(AB_LOW + kc * 512, 512), epi_m1, lh_done[0])

        mam_done = [None]

        def epi_m2(bank, oc, te_tok):
            P.wait("sc", te_tok)
            e1 = AV(af(C_MB + oc * 512, 512), PSUM[bank][:, 0:512],
                    ACT.Identity, bias=T["m2b"][:, oc:oc + 1], mark=True)
            free_bank(bank, e1)
            mam_done[0] = AV(ab(AB_MAM + oc * 512, 512),
                             af(C_MB + oc * 512, 512), ACT.Identity, mark=True)
            mam_toks[oc] = mam_done[0]

        mm_pass("m2", lambda kc: ab(AB_GEL + kc * 512, 512), epi_m2, None,
                rhs_dep_fn=lambda kc: gel_toks[kc])
        fm_dump(blk, "C_m2")


        gdep = [None]

        def epi_gate(bank, oc, te_tok):
            s0 = af(C_S[0], 512)
            s1 = af(C_S[1], 512)
            P.wait("sc", te_tok)
            P.wait("sc", gdep[0])
            th = AV(s0, PSUM[bank][:, 0:512], ACT.Tanh, scale=0.5,
                    bias=T["gabh"][:, oc:oc + 1])
            th = AV(s0, s0, ACT.Identity, scale=0.5, bias=T["cb"][:, 4:5], mark=True)
            free_bank(bank, th)
            P.wait("ve", th)
            TT("ve", s1, af(C_SS + oc * 512, 512), af(C_MB + oc * 512, 512),
               ALU.subtract)
            TT("ve", s1, s0, s1, ALU.mult)
            tv = TT("ve", s1, s1, af(C_MB + oc * 512, 512), ALU.add, mark=True)
            gdep[0] = tv
            P.wait("te", tv)
            for j in range(NTT):
                bk = next_bank("pt")
                tt = TR(PSUM[bk][:, 0:128], af(C_S[1] + j * 128, 128), mark=True)
                P.wait("ve", tt)
                tv2 = CP("ve", af(C_Y + j * 1024 + oc * 128, 128),
                         PSUM[bk][:, 0:128], mark=True)
                free_bank(bk, tv2)
                gdep[0] = tv2

        def gate_rhs(kc):
            if kc < 8:
                return ab(AB_SSD + kc * 512, 512)
            return ab(AB_MAM + (kc - 8) * 512, 512)

        mm_pass("gate", gate_rhs, epi_gate, None,
                rhs_dep_fn=lambda kc: ssd_toks[kc] if kc < 8 else mam_toks[kc - 8])
        fm_dump(blk, "C_gate")

        P.wait("sc", sdep)
        prev_use = None
        for j in range(NTT):
            xt = T["x_tm"][:, j * 1024:(j + 1) * 1024]
            yt = af(C_Y + j * 1024, 1024)
            if prev_use is not None:
                P.wait("sc", prev_use)
            t = AV(af(ZT, 1024), xt, ACT.Identity,
                   scale=T["wtmp"][:, 4 + j:5 + j], bias=T["wtmp"][:, j:j + 1],
                   mark=True)
            P.wait("ve", t)
            TT("ve", yt, yt, af(ZT, 1024), ALU.add)
            prev_use = TT("ve", xt, xt, yt, ALU.add, mark=True)

        yln_done = [None]

        def yln_evac(ps, c_lo, j, tt, bank):
            P.wait("ve", tt)
            last = None
            for m in range(4):
                c = c_lo + m
                last = TS("ve", ab(AB_XN + c * 512 + j * 128, 128),
                          ps[:, m * 128:(m + 1) * 128],
                          T["fng"][:, c:c + 1], T["fnb"][:, c:c + 1],
                          ALU.mult, ALU.add, mark=(m == 3))
            free_bank(bank, last)
            yln_done[0] = last

        last_tr2 = [None]
        deps = ln_stats_all(lambda j: af(C_Y + j * 1024, 1024))
        for j in range(NTT):
            yt_off = C_Y + j * 1024
            if last_tr2[0] is not None:
                P.wait("sc", last_tr2[0])
            tz = normalize_zt(af(yt_off, 1024), j, deps[j])
            P.wait("te", tz)
            last_tr2[0] = tm_to_fm(ZT, j, yln_evac)

        hgel_done = [None]
        hgel_toks = {}

        def epi_f1(bank, oc, te_tok):
            P.wait("sc", te_tok)
            t = AV(ab(AB_HQ + oc * 512, 512), PSUM[bank][:, 0:512], ACT.Gelu,
                   bias=T["f1b"][:, oc:oc + 1], mark=True)
            free_bank(bank, t)
            hgel_done[0] = t
            hgel_toks[oc] = t

        fm_dump(blk, "C_y")
        mm_pass("f1", lambda kc: ab(AB_XN + kc * 512, 512), epi_f1, yln_done[0])

        def epi_f2(bank, oc, te_tok):
            P.wait("sc", te_tok)
            e1 = AV(af(C_S[5], 512), PSUM[bank][:, 0:512], ACT.Identity,
                    bias=T["f2b"][:, oc:oc + 1], mark=True)
            free_bank(bank, e1)
            P.wait("te", e1)
            fm_to_tm_add(C_S[5], oc)

        mm_pass("f2", lambda kc: ab(AB_HQ + kc * 512, 512), epi_f2, None,
                rhs_dep_fn=lambda kc: hgel_toks[kc])

    prep()
    prep2_done = [None]
    for blk in range(N_BLK):
        x_dep = load_x(blk)
        tap_dump(blk, "load")
        phase_A(blk, x_dep)
        if blk == 0:
            prep2_done[0] = prep2()
        tap_dump(blk, "A")
        if blk == 0:
            P.wait("ve", prep2_done[0])
            P.wait("sc", prep2_done[0])
        phase_B(blk)
        tap_dump(blk, "B")
        phase_C(blk)
        tap_dump(blk, "C")
        phase_B(blk)
        tap_dump(blk, "D")
        store_x(blk)

    n_oq = P.q["oq"]
    if n_oq:
        P.wait("sy", ("oq", 0, ((n_oq + 1) // 2) * 16))
        P.wait("sy", ("oq", 1, (n_oq // 2) * 16))

    with stack:
        sems = {}
        with contextlib.ExitStack() as semstack:
            for e in ENGINES:
                sems[e] = semstack.enter_context(nc.semaphore("sem_" + e))
            dsems = {}
            for c in DMA_CLASSES:
                for i in (0, 1):
                    dsems[(c, i)] = semstack.enter_context(
                        nc.semaphore(f"dsem_{c}{i}"))
            P.sem = sems
            P.dsem = dsems
            with nc.Block() as block:
                P.eng = {"sy": nc.sync, "ve": nc.vector, "sc": nc.scalar,
                         "te": nc.tensor, "gp": nc.gpsimd}

                @block.sync
                def _(eng):
                    for f in P.ops["sy"]:
                        f()

                @block.vector
                def _(eng):
                    for f in P.ops["ve"]:
                        f()

                @block.scalar
                def _(eng):
                    for f in P.ops["sc"]:
                        f()

                @block.tensor
                def _(eng):
                    for f in P.ops["te"]:
                        f()

                @block.gpsimd
                def _(eng):
                    for f in P.ops["gp"]:
                        f()
    return nc



def kernel(**inputs):
    g = host_prepare(inputs)
    x = np.asarray(inputs["L0_attn_a_x"], np.float32)
    nc = build_program()
    in_maps = []
    for core in range(N_CORES):
        m = dict(g)
        m["x"] = np.ascontiguousarray(x[core * B_LOC:(core + 1) * B_LOC])
        in_maps.append(m)
    res = run_bass_kernel_spmd(nc, in_maps, core_ids=list(range(N_CORES)))
    out = np.concatenate([np.asarray(r["out"]) for r in res.results], axis=0)
    return out.reshape(B_FULL, T_FULL, D)


# version 10
# speedup vs baseline: 1.3124x; 1.2822x over previous
import contextlib
import numpy as np
import ml_dtypes
import concourse.bass as bass
import concourse.mybir as mybir
from concourse.bass_utils import run_bass_kernel_spmd

F32 = mybir.dt.float32
BF16 = mybir.dt.bfloat16
I32 = mybir.dt.int32
U8 = mybir.dt.uint8
ALU = mybir.AluOpType
ACT = mybir.ActivationFunctionType
AX = mybir.AxisListType
bf16 = ml_dtypes.bfloat16

D = 1024
T_FULL = 2048
B_FULL = 16
N_CORES = 8
B_LOC = B_FULL // N_CORES
TB = 512
N_BLK = B_LOC * (T_FULL // TB)
NTT = TB // 128
NCD = D // 128

LN2 = float(np.log(2.0))
INV_LN2 = float(1.0 / np.log(2.0))
BIG = 3.0e38

W_SPECS = {
    "wdt":  (1024, 1024, "sign"),
    "wx":   (1024, 2048, "sign"),
    "w1":   (1024, 4096, "cast"),
    "w2":   (4096, 1024, "e4m3"),
    "fr":   (1024, 1024, "cast"),
    "dp":   (1024, 1024, "cast"),
    "bp":   (1024, 1024, "cast"),
    "cp":   (1024, 1024, "cast"),
    "m1":   (1024, 2048, "cast"),
    "m2":   (2048, 1024, "cast"),
    "gate": (2048, 1024, "cast"),
    "f1":   (1024, 4096, "cast"),
    "f2":   (4096, 1024, "cast"),
}

BIAS_TILES = [("bdt", 8), ("bx", 16), ("alog", 8), ("ng", 8), ("nb", 8),
              ("frb", 8), ("dpb", 8), ("bpb", 8), ("cpb", 8), ("skp", 8),
              ("m1b", 16), ("m2b", 8), ("gab", 8), ("fng", 8), ("fnb", 8),
              ("f1b", 32), ("f2b", 8), ("b1", 32), ("b2", 8)]


def wg_of(K):
    return 512 if K <= 2048 else 256



def _pieces(w_ko, Wg):
    K, O = w_ko.shape
    nk = K // 128
    nog = O // Wg
    out = np.empty((nog * nk, 128, Wg), np.float32)
    i = 0
    for og in range(nog):
        for kc in range(nk):
            out[i] = w_ko[kc * 128:(kc + 1) * 128, og * Wg:(og + 1) * Wg]
            i += 1
    return np.ascontiguousarray(out)


def _coltile(v, n):
    return np.ascontiguousarray(np.asarray(v, np.float32).reshape(n, 128).T)


def host_prepare(inputs):
    f = np.float32
    g = {}
    g["wdt"] = _pieces(np.asarray(inputs["L0_attn_a_W_dt"], f).T, wg_of(1024))
    g["wx"] = _pieces(np.asarray(inputs["L0_attn_a_W_x"], f).T, wg_of(1024))
    g["w1"] = _pieces(np.asarray(inputs["L0_ffn_w1"], f).T, wg_of(1024))
    g["w2"] = _pieces(np.asarray(inputs["L0_ffn_w2"], f).T, wg_of(4096))
    g["fr"] = _pieces(np.asarray(inputs["L0_attn_b_fr_w"], f), wg_of(1024))
    g["dp"] = _pieces(np.asarray(inputs["L0_attn_b_dp_w"], f), wg_of(1024))
    g["bp"] = _pieces(np.asarray(inputs["L0_attn_b_bp_w"], f), wg_of(1024))
    g["cp"] = _pieces(np.asarray(inputs["L0_attn_b_cp_w"], f), wg_of(1024))
    g["m1"] = _pieces(np.asarray(inputs["L0_attn_b_m1_w"], f), wg_of(1024))
    g["m2"] = _pieces(np.asarray(inputs["L0_attn_b_m2_w"], f), wg_of(2048))
    g["gate"] = _pieces(np.asarray(inputs["L0_attn_b_gate_w"], f), wg_of(2048))
    g["f1"] = _pieces(np.asarray(inputs["L0_attn_b_ffn1_w"], f), wg_of(1024))
    g["f2"] = _pieces(np.asarray(inputs["L0_attn_b_ffn2_w"], f), wg_of(4096))
    g["bdt"] = _coltile(inputs["L0_attn_a_b_dt"], 8)
    g["bx"] = _coltile(inputs["L0_attn_a_b_x"], 16)
    g["alog"] = _coltile(inputs["L0_attn_a_A_log"], 8)
    g["ng"] = _coltile(inputs["L0_attn_b_norm_g"], 8)
    g["nb"] = _coltile(inputs["L0_attn_b_norm_b"], 8)
    g["frb"] = _coltile(inputs["L0_attn_b_fr_b"], 8)
    g["dpb"] = _coltile(inputs["L0_attn_b_dp_b"], 8)
    g["bpb"] = _coltile(inputs["L0_attn_b_bp_b"], 8)
    g["cpb"] = _coltile(inputs["L0_attn_b_cp_b"], 8)
    g["skp"] = _coltile(inputs["L0_attn_b_skip"], 8)
    g["m1b"] = _coltile(inputs["L0_attn_b_m1_b"], 16)
    g["m2b"] = _coltile(inputs["L0_attn_b_m2_b"], 8)
    g["gab"] = _coltile(inputs["L0_attn_b_gate_b"], 8)
    g["fng"] = _coltile(inputs["L0_attn_b_ffn_g"], 8)
    g["fnb"] = _coltile(inputs["L0_attn_b_ffn_bn"], 8)
    g["f1b"] = _coltile(inputs["L0_attn_b_ffn1_b"], 32)
    g["f2b"] = _coltile(inputs["L0_attn_b_ffn2_b"], 8)
    g["b1"] = _coltile(inputs["L0_ffn_b1"], 32)
    g["b2"] = _coltile(inputs["L0_ffn_b2"], 8)
    g["ident"] = np.eye(128, dtype=np.float32)
    return g



ENGINES = ("sy", "ve", "sc", "te", "gp")
DEBUG_POINT = ["endA"]
DMA_CLASSES = ("bq", "sq", "wq", "xq", "pw", "oq")
DMA_ENGINE = {"bq": "sy", "sq": "sy", "wq": "sy", "xq": "sy",
              "pw": "gp", "oq": "gp"}


class Prog:
    def __init__(self):
        self.ops = {e: [] for e in ENGINES}
        self.n = {e: 0 for e in ENGINES}
        self.waited = {e: {} for e in ENGINES}
        self.q = {c: 0 for c in DMA_CLASSES}
        self.eng = None
        self.sem = None
        self.dsem = None

    def emit(self, e, fn):
        self.ops[e].append(fn)
        return None

    def mark(self, e, fn):
        self.n[e] += 1
        tok = self.n[e]
        def wrapped(_fn=fn, _e=e):
            _fn().then_inc(self.sem[_e], 1)
        self.ops[e].append(wrapped)
        return (e, tok)

    def wait(self, e, dep):
        if dep is None:
            return
        if dep[0] in ENGINES:
            kind, tok = dep
            assert tok <= self.n[kind], \
                f"deadlock: {e} waits {kind}@{tok} > {self.n[kind]}"
            if kind == e:
                return
            if self.waited[e].get(kind, 0) >= tok:
                return
            self.waited[e][kind] = tok
            def w(_k=kind, _t=tok, _e=e):
                self.eng[_e].wait_ge(self.sem[_k], _t)
            self.ops[e].append(w)
        else:
            cls, idx, val = dep
            key = (cls, idx)
            if self.waited[e].get(key, 0) >= val:
                return
            self.waited[e][key] = val
            def w(_key=key, _v=val, _e=e):
                self.eng[_e].wait_ge(self.dsem[_key], _v)
            self.ops[e].append(w)

    def dma(self, cls, out_fn, in_fn, pre=None):
        eng = DMA_ENGINE[cls]
        if isinstance(pre, list):
            for p in pre:
                self.wait(eng, p)
        else:
            self.wait(eng, pre)
        i = self.q[cls]
        self.q[cls] += 1
        idx = i % 2
        val = ((i // 2) + 1) * 16
        def d(_o=out_fn, _i=in_fn, _k=(cls, idx), _e=eng):
            self.eng[_e].dma_start(out=_o(), in_=_i()).then_inc(self.dsem[_k], 16)
        self.ops[eng].append(d)
        return (cls, idx, val)

    def tok(self, e):
        return (e, self.n[e])


def build_program(debug_taps=(), debug_fm=()):
    nc = bass.Bass("TRN2", target_bir_lowering=False)
    P = Prog()
    stack = contextlib.ExitStack()

    dram = {}
    dram["x"] = nc.dram_tensor("x", [B_LOC, T_FULL, D], F32, kind="ExternalInput").ap()
    dram["out"] = nc.dram_tensor("out", [B_LOC, T_FULL, D], F32,
                                 kind="ExternalOutput").ap()
    for w, (K, O, _) in W_SPECS.items():
        Wg = wg_of(K)
        npc = (K // 128) * (O // Wg)
        dram[w] = nc.dram_tensor(w, [npc, 128, Wg], F32, kind="ExternalInput").ap()
        dram["scr_" + w] = nc.dram_tensor("scr_" + w, [npc, 128, Wg], BF16,
                                          kind="Internal").ap()
    for b, n in BIAS_TILES:
        dram[b] = nc.dram_tensor(b, [128, n], F32, kind="ExternalInput").ap()
    dram["ident"] = nc.dram_tensor("ident", [128, 128], F32,
                                   kind="ExternalInput").ap()
    fmtaps = {}
    for ent in debug_fm:
        tag, off, cols = ent[0], ent[1], ent[2]
        point = ent[3] if len(ent) > 3 else "endA"
        dt_ = BF16 if (isinstance(off, str) and off.startswith("ab:")) else F32
        fmtaps[tag] = (nc.dram_tensor("fmtap_" + tag, [128, cols], dt_,
                                      kind="ExternalOutput").ap(), off, cols, point)
    taps = {}
    for blk, phase in debug_taps:
        taps[(blk, phase)] = nc.dram_tensor(f"tap_{blk}_{phase}", [TB, D], F32,
                                            kind="ExternalOutput").ap()

    T = {}
    def sbuf(name, cols, dtype=F32):
        T[name] = stack.enter_context(nc.sbuf_tensor("sb_" + name, [128, cols], dtype))

    sbuf("ident", 128)
    sbuf("ones1", 128)
    sbuf("zeros", 512)
    sbuf("R", 512)
    sbuf("rowbuf", 640)
    for b, n in BIAS_TILES:
        sbuf(b, n)
    sbuf("acol", 8)
    sbuf("frbh", 8); sbuf("dpbh", 8); sbuf("gabh", 8)
    sbuf("b2q", 8)
    sbuf("wsdt", 2); sbuf("wsx", 2)
    sbuf("wtmp", 8)
    sbuf("cb", 8)
    for s in ["st_sum", "st_ssq", "st_mneg", "st_var", "st_r", "st_rv",
              "st_mrneg", "st_am", "st_scf", "st_sum2", "st_ssq2", "st_am2"]:
        sbuf(s, 8)
    sbuf("wacc", 16)
    sbuf("ccarry", 8)
    sbuf("cscarry", 8)
    sbuf("x_tm", NTT * 1024)
    sbuf("AF", 22016)
    sbuf("AB", 28672, BF16)
    sbuf("wbuf0", 8192, BF16)
    sbuf("wbuf1", 8192, BF16)
    sbuf("stri", 2048, I32)
    sbuf("strm", 512, U8)

    PSUM = {}
    for p in ["pm0", "pm1", "pm2", "pm3", "pm4", "pm5", "pt0", "pt1"]:
        PSUM[p] = stack.enter_context(nc.psum_tensor(p, [128, 512], F32))

    def af(off, n):
        return T["AF"][:, off:off + n]

    def ab(off, n):
        return T["AB"][:, off:off + n]

    def afb(off, n):
        return T["AF"].bitcast(BF16)[:, 2 * off:2 * off + n]

    def ap3(tname, off, stride, n, w):
        h = T[tname]
        row = h.shape[1] if hasattr(h, "shape") else None
        hh = h[:, 0:1]
        rowlen = hh.ap[0][0]
        return bass.AP(hh.tensor, off, [[rowlen, 128], [stride, n], [1, w]])

    def psum3(bank, n, w):
        hh = PSUM[bank][:, 0:1]
        rowlen = hh.ap[0][0]
        return bass.AP(hh.tensor, 0, [[rowlen, 128], [w, n], [1, w]])

    ZT = 0
    WSTG = [1024, 3072]
    QV = [(5120, 7168), (9216, 11264)]
    QG = [13312, 15360]
    A_KF = 1024
    A_DT = 2048
    A_GN = 6144
    A_TC = 10240
    A_PB = 14336
    A_S = [18440 + i * 512 for i in range(6)]
    B_HB = [1024 + i * 512 for i in range(4)]
    B_QV = [(3072 + i * 1024, 3584 + i * 1024) for i in range(4)]
    B_QG = [7168 + i * 512 for i in range(4)]
    B_S = [9216 + i * 512 for i in range(4)]
    C_XN = 1024
    C_MB = 5120
    C_SS = 9216
    C_Y = 13312
    C_S = [17408 + i * 512 for i in range(6)]

    WQB = [0, 2048]
    AB_K = 0
    AB_XN = 0
    AB_LOW = 4096
    AB_HIGH = 8192
    AB_GEL = 12288
    AB_MAM = 20480
    AB_SSD = 24576
    AB_HQ = 4096

    def TT(e, out, a, b, op, mark=False):
        f = lambda: P.eng[e].tensor_tensor(out, a, b, op)
        return P.mark(e, f) if mark else P.emit(e, f)

    def TS(e, out, a, s1, s2, op0, op1=None, mark=False):
        if op1 is None:
            f = lambda: P.eng[e].tensor_scalar(out, a, s1, None, op0)
        else:
            f = lambda: P.eng[e].tensor_scalar(out, a, s1, s2, op0, op1)
        return P.mark(e, f) if mark else P.emit(e, f)

    def STT(e, out, a, s, b, op0, op1, accum=None, mark=False):
        if accum is None:
            f = lambda: P.eng[e].scalar_tensor_tensor(out, a, s, b, op0, op1)
        else:
            f = lambda: P.eng[e].scalar_tensor_tensor(out, a, s, b, op0, op1,
                                                      accum_out=accum)
        return P.mark(e, f) if mark else P.emit(e, f)

    def CP(e, out, a, mark=False):
        f = lambda: P.eng[e].tensor_copy(out, a)
        return P.mark(e, f) if mark else P.emit(e, f)

    def AV(out, a, func, bias=0.0, scale=1.0, mark=False):
        f = lambda: P.eng["sc"].activation(out=out, in_=a, func=func,
                                           bias=bias, scale=scale)
        return P.mark("sc", f) if mark else P.emit("sc", f)

    def AVA(out, a, func, accum, mark=False):
        f = lambda: P.eng["sc"].activation(out=out, in_=a, func=func,
                                           accum_out=accum)
        return P.mark("sc", f) if mark else P.emit("sc", f)

    def RED(out, a, op="sum", absolute=False, mark=False):
        if op == "max":
            f = lambda: P.eng["ve"].reduce_max(out, a, axis=AX.X,
                                               apply_absolute_value=absolute)
        else:
            f = lambda: P.eng["ve"].reduce_sum(out, a, axis=AX.X,
                                               apply_absolute_value=absolute)
        return P.mark("ve", f) if mark else P.emit("ve", f)

    def RECIP(out, a, mark=False):
        f = lambda: P.eng["ve"].reciprocal(out, a)
        return P.mark("ve", f) if mark else P.emit("ve", f)

    def SCAN(out, d0, d1, init, op0, op1, mark=False):
        f = lambda: P.eng["ve"].tensor_tensor_scan(out=out, data0=d0, data1=d1,
                                                   initial=init, op0=op0, op1=op1)
        return P.mark("ve", f) if mark else P.emit("ve", f)

    def SEL(out, mask, t_, f_, mark=False):
        f = lambda: P.eng["ve"].select(out, mask, t_, f_)
        return P.mark("ve", f) if mark else P.emit("ve", f)

    def MM(psum, lhsT, rhs, start, stop, mark=False):
        f = lambda: P.eng["te"].matmul(psum, lhsT, rhs, start=start, stop=stop)
        return P.mark("te", f) if mark else P.emit("te", f)

    def TR(psum, a, mark=False):
        f = lambda: P.eng["te"].transpose(psum, a, T["ident"][:])
        return P.mark("te", f) if mark else P.emit("te", f)

    def MS(e, out, val, mark=False):
        f = lambda: P.eng[e].memset(out, val)
        return P.mark(e, f) if mark else P.emit(e, f)

    psum_free = {}
    rot = {"pm": 0, "pt": 0}

    def next_bank(kind):
        names = (["pm0", "pm1", "pm2", "pm3", "pm4", "pm5"] if kind == "pm"
                 else ["pt0", "pt1"])
        b = names[rot[kind] % len(names)]
        rot[kind] += 1
        if b in psum_free:
            P.wait("te", psum_free[b])
        return b

    def free_bank(b, dep):
        psum_free[b] = dep

    wbuf_free = [None, None]

    def quant_chain(dst, src, n, man, clamp, maxv, pos, sv, sg, i32_off):
        sa, sb_ = af(sv[0], n), af(sv[1], n)
        sG = af(sg, n)
        si = T["stri"][:, i32_off:i32_off + n]
        if pos:
            t1 = AV(sb_, src, ACT.Ln, bias=T["cb"][:, 0:1], mark=True)
            c1 = INV_LN2
        else:
            AV(sa, src, ACT.Square)
            t1 = AV(sb_, sa, ACT.Ln, bias=T["cb"][:, 0:1], mark=True)
            c1 = 0.5 * INV_LN2
        narrow = n < 128
        if narrow:
            si2 = T["stri"][:, i32_off + n:i32_off + 2 * n]
            sa2 = af(sg + n, n)
        P.wait("ve", t1)
        if narrow:
            tn = TS("ve", si, sb_, -c1, 0.5, ALU.mult, ALU.add, mark=True)
            P.wait("gp", tn)
            tg0 = CP("gp", si2, si, mark=True)
            P.wait("ve", tg0)
            tv1 = TS("ve", sa, si2, float(clamp), float(man), ALU.min, ALU.add,
                     mark=True)
        else:
            TS("ve", si, sb_, -c1, 0.5, ALU.mult, ALU.add)
            tv1 = TS("ve", sa, si, float(clamp), float(man), ALU.min, ALU.add,
                     mark=True)
        P.wait("sc", tv1)
        if not narrow:
            sb16 = afb(sv[1], n)
            sg16 = afb(sg, n)
            AV(sb16, sa, ACT.Exp, scale=LN2)
            t2 = AV(sg16, sa, ACT.Exp, scale=-LN2, mark=True)
            P.wait("ve", t2)
            src_done = TT("ve", si, src, sb16, ALU.mult, mark=True)
            tv2 = TT("ve", sb_, si, sg16, ALU.mult, mark=True)
            if pos:
                P.wait("sc", tv2)
                ts = AV(sG, sb_, ACT.Relu, bias=T["cb"][:, 3:4], mark=True)
                P.wait("ve", ts)
                tg = STT("ve", dst, sG, BIG, sb_, ALU.mult, ALU.add, mark=True)
            else:
                TS("ve", sG, sb_, float(maxv), -float(maxv), ALU.min, ALU.max)
                TT("ve", sG, sb_, sG, ALU.subtract)
                tg = STT("ve", dst, sG, BIG, sb_, ALU.mult, ALU.add, mark=True)
            return src_done, tg
        AV(sb_, sa, ACT.Exp, scale=LN2)
        t2 = AV(sa, sa, ACT.Exp, scale=-LN2, mark=True)
        P.wait("ve", t2)
        if narrow:
            src_done = TT("ve", si, src, sb_, ALU.mult, mark=True)
            P.wait("gp", src_done)
            CP("gp", si2, si)
            tg1 = CP("gp", sa2, sa, mark=True)
            P.wait("ve", tg1)
            tv2 = TT("ve", sb_, si2, sa2, ALU.mult, mark=True)
            P.wait("gp", tv2)
            TT("gp", sG, sb_, sb_, ALU.mult)
            TS("gp", sG, sG, float(maxv) * float(maxv), None, ALU.is_gt)
            TS("gp", sG, sG, BIG, 1.0, ALU.mult, ALU.add)
            tg = TT("gp", dst, sb_, sG, ALU.mult, mark=True)
            return src_done, tg
        raise AssertionError("wide path handled above")

    stg_free = {0: None, 1: None}
    qb_free = {0: None, 1: None}
    prep_w_done = {}
    qpar = [0]
    qprev = {0: None, 1: None}

    def prep_stream(w, batch_cb):
        K, O, _ = W_SPECS[w]
        Wg = wg_of(K)
        npc = (K // 128) * (O // Wg)
        per = 2048 // Wg
        for bi, i0 in enumerate(range(0, npc, per)):
            n = min(per, npc - i0)
            sl = bi % 2
            tok = P.dma("sq",
                        lambda _s=sl, _n=n, _Wg=Wg: af(WSTG[_s], _n * _Wg),
                        lambda _w=w, _i=i0, _n=n:
                            dram[_w][_i:_i + _n].rearrange("n p w -> p n w"),
                        pre=stg_free[sl])
            P.wait("ve", tok)
            P.wait("sc", tok)
            stg_free[sl] = batch_cb(WSTG[sl], n * Wg, i0, n, sl)

    def wscale_pass(w, wtile):
        K, O, _ = W_SPECS[w]
        nb = [0]
        def cb(off, ncols, i0, n, sl):
            tv = RED(T["wacc"][:, nb[0]:nb[0] + 1], af(off, ncols), "sum",
                     absolute=True, mark=True)
            nb[0] += 1
            return tv
        prep_stream(w, cb)
        assert nb[0] <= 8
        tv = P.tok("ve")
        P.wait("gp", tv)
        tg = CP("gp", T["st_sum2"][:, 0:nb[0]], T["wacc"][:, 0:nb[0]], mark=True)
        P.wait("ve", tg)
        t1 = RED(T["wtmp"][:, 1:2], T["st_sum2"][:, 0:nb[0]], "sum", mark=True)
        P.wait("gp", t1)
        tg = CP("gp", T["wtmp"][:, 0:1], T["wtmp"][:, 1:2], mark=True)
        P.wait("te", tg)
        bank = next_bank("pt")
        tt = TR(PSUM[bank][0:1, 0:128], T["wtmp"][:, 0:1], mark=True)
        P.wait("ve", tt)
        CP("ve", T["rowbuf"][0:1, 0:128], PSUM[bank][0:1, 0:128])
        tv = RED(T["wtmp"][0:1, 2:3], T["rowbuf"][0:1, 0:128], "sum", mark=True)
        free_bank(bank, tv)
        P.wait("gp", tv)
        tg = TS("gp", T["wtmp"][0:1, 3:4], T["wtmp"][0:1, 2:3],
                1.0 / (K * O), 1e-5, ALU.mult, ALU.add, mark=True)
        bank2 = next_bank("pt")
        P.wait("te", tg)
        tm = MM(PSUM[bank2][:, 0:1], T["ones1"][0:1, 0:128],
                T["wtmp"][0:1, 3:4], True, True, mark=True)
        P.wait("ve", tm)
        tv = CP("ve", T[wtile][:, 0:1], PSUM[bank2][:, 0:1], mark=True)
        P.wait("gp", tv)
        tv = TS("gp", T[wtile][:, 1:2], T[wtile][:, 0:1], 0.5, None, ALU.mult,
                mark=True)
        free_bank(bank2, tv)

    def prep_weight(w, K, O, q):
        wt = "wsdt" if w == "wdt" else "wsx"
        def cb(off, ncols, i0, n, sl):
            qb = ab(WQB[sl], ncols)
            if qb_free[sl] is not None:
                P.wait("ve", qb_free[sl])
                P.wait("gp", qb_free[sl])
            src = af(off, ncols)
            if q == "cast":
                done = CP("ve", qb, src, mark=True)
                wdep = done
            elif q == "sign":
                par = qpar[0] % 2
                qpar[0] += 1
                sa = af(QV[par][0], ncols)
                sb_ = af(QV[par][1], ncols)
                if qprev[par] is not None:
                    P.wait("ve", qprev[par])
                STT("ve", sa, src, -1.0, src, ALU.mult, ALU.max)
                tv0 = TS("ve", sa, sa, T[wt][:, 1:2], None, ALU.is_gt, mark=True)
                P.wait("sc", tv0)
                ts = AV(sb_, src, ACT.Sign, mark=True)
                P.wait("ve", ts)
                done = TT("ve", qb, sa, sb_, ALU.mult, mark=True)
                qprev[par] = done
                wdep = done
            else:
                par = qpar[0] % 2
                qpar[0] += 1
                if qprev[par] is not None:
                    P.wait("ve", qprev[par])
                    P.wait("sc", qprev[par])
                done, tg = quant_chain(qb, src, ncols, man=3, clamp=6,
                                       maxv=240.0, pos=False, sv=QV[par],
                                       sg=QG[par], i32_off=0 if par == 0 else 0)
                qprev[par] = tg
                wdep = tg
            wtok = P.dma("pw",
                         lambda _w=w, _i=i0, _n=n:
                             dram["scr_" + _w][_i:_i + _n].rearrange("n p w -> p n w"),
                         lambda _q=qb: _q,
                         pre=wdep)
            qb_free[sl] = wtok
            prep_w_done[w] = wtok
            return done
        prep_stream(w, cb)

    def prep():
        toks = []
        for b, _ in BIAS_TILES:
            toks.append(P.dma("bq", lambda _b=b: T[_b][:], lambda _b=b: dram[_b]))
        toks.append(P.dma("bq", lambda: T["ident"][:], lambda: dram["ident"]))
        last = toks[-1]
        prev = toks[-2]
        for e in ("ve", "sc", "te"):
            P.wait(e, last)
            P.wait(e, prev)
        MS("ve", T["ones1"][:, :], 1.0)
        MS("ve", T["zeros"][:], 0.0)
        MS("ve", T["cb"][:, 0:1], 1e-38)
        MS("ve", T["cb"][:, 1:2], 1e-5)
        MS("ve", T["cb"][:, 2:3], 1.0)
        MS("ve", T["cb"][:, 3:4], -240.0)
        MS("ve", T["cb"][:, 5:6], 3.0 * LN2)
        MS("ve", T["cb"][:, 6:7], -3.0 * LN2)
        cb_tok = MS("ve", T["cb"][:, 4:5], 0.5, mark=True)
        P.wait("sc", cb_tok)
        TS("ve", T["frbh"][:], T["frb"][:], 0.5, None, ALU.mult)
        TS("ve", T["dpbh"][:], T["dpb"][:], 0.5, None, ALU.mult)
        TS("ve", T["gabh"][:], T["gab"][:], 0.5, None, ALU.mult)
        AV(T["st_sum"][:], T["alog"][:], ACT.Abs)
        AV(T["st_ssq"][:], T["st_sum"][:], ACT.Exp, scale=-1.0)
        t = AV(T["st_var"][:], T["st_ssq"][:], ACT.Ln, bias=T["cb"][:, 2:3], mark=True)
        P.wait("gp", t)
        TS("gp", T["st_mneg"][:], T["alog"][:], 0.0, None, ALU.max)
        TT("gp", T["st_mneg"][:], T["st_mneg"][:], T["st_var"][:], ALU.add)
        tv = TS("gp", T["acol"][:], T["st_mneg"][:], -1.0, None, ALU.mult,
                mark=True)
        P.wait("sc", tv)
        P.wait("ve", tv)
        quant_chain(T["b2q"][:], T["b2"][:], 8, man=10, clamp=14, maxv=65504.0,
                    pos=False, sv=QV[0], sg=QG[0], i32_off=0)
        wscale_pass("wdt", "wsdt")
        wscale_pass("wx", "wsx")
        for w in ["wdt", "wx", "w1", "w2"]:
            K, O, q = W_SPECS[w]
            prep_weight(w, K, O, q)

    def prep2():
        stg2 = [ab(4096, 4096).bitcast(F32), ab(8192, 4096).bitcast(F32)]
        out2 = [ab(12288, 2048), ab(14336, 2048)]
        sfree = {0: None, 1: None}
        ofree = {0: None, 1: None}
        for w in ["fr", "dp", "bp", "cp", "m1", "m2", "gate", "f1", "f2"]:
            K, O, _ = W_SPECS[w]
            Wg = wg_of(K)
            npc = (K // 128) * (O // Wg)
            per = 2048 // Wg
            for bi, i0 in enumerate(range(0, npc, per)):
                n = min(per, npc - i0)
                sl = bi % 2
                tok = P.dma("sq",
                            lambda _s=sl, _n=n, _Wg=Wg: stg2[_s][:, 0:_n * _Wg],
                            lambda _w=w, _i=i0, _n=n:
                                dram[_w][_i:_i + _n].rearrange("n p w -> p n w"),
                            pre=sfree[sl])
                P.wait("sc", tok)
                if ofree[sl] is not None:
                    P.wait("sc", ofree[sl])
                ts = AV(out2[sl][:, 0:n * Wg], stg2[sl][:, 0:n * Wg],
                        ACT.Identity, mark=True)
                sfree[sl] = ts
                wtok = P.dma("pw",
                             lambda _w=w, _i=i0, _n=n:
                                 dram["scr_" + _w][_i:_i + _n].rearrange("n p w -> p n w"),
                             lambda _sl=sl, _n=n, _Wg=Wg: out2[_sl][:, 0:_n * _Wg],
                             pre=ts)
                ofree[sl] = wtok
                prep_w_done[w] = wtok
        return prep_w_done["f2"]

    def ln_stats_all(x_ap_fn, js=range(NTT)):
        trs = {}
        for j in js:
            sl = slice(j, j + 1)
            RED(T["st_sum"][:, sl], x_ap_fn(j), "sum")
            trs[j] = STT("ve", af(6144, 1024), x_ap_fn(j), 1.0, x_ap_fn(j),
                         ALU.bypass, ALU.mult, accum=T["st_ssq"][:, sl],
                         mark=True)
        deps = {}
        for j in js:
            sl = slice(j, j + 1)
            P.wait("gp", trs[j])
            TS("gp", T["st_mneg"][:, sl], T["st_sum"][:, sl], -1.0 / D, None,
               ALU.mult)
            TT("gp", T["st_var"][:, sl], T["st_mneg"][:, sl],
               T["st_mneg"][:, sl], ALU.mult)
            TS("gp", T["st_ssq2"][:, sl], T["st_ssq"][:, sl], 1.0 / D, None,
               ALU.mult)
            tg = TT("gp", T["st_var"][:, sl], T["st_ssq2"][:, sl],
                    T["st_var"][:, sl], ALU.subtract, mark=True)
            P.wait("sc", tg)
            AV(T["st_r"][:, sl], T["st_var"][:, sl], ACT.Ln, bias=T["cb"][:, 1:2])
            t = AV(T["st_r"][:, sl], T["st_r"][:, sl], ACT.Exp, scale=-0.5,
                   mark=True)
            P.wait("gp", t)
            CP("gp", T["st_rv"][:, sl], T["st_r"][:, sl])
            deps[j] = TT("gp", T["st_mrneg"][:, sl], T["st_mneg"][:, sl],
                         T["st_rv"][:, sl], ALU.mult, mark=True)
        return deps

    def ln_stats(x_ap, j):
        sl = slice(j, j + 1)
        RED(T["st_sum"][:, sl], x_ap, "sum")
        tr = STT("ve", af(6144, 1024), x_ap, 1.0, x_ap, ALU.bypass, ALU.mult,
                 accum=T["st_ssq"][:, sl], mark=True)
        P.wait("gp", tr)
        TS("gp", T["st_mneg"][:, sl], T["st_sum"][:, sl], -1.0 / D, None,
           ALU.mult)
        TT("gp", T["st_var"][:, sl], T["st_mneg"][:, sl], T["st_mneg"][:, sl],
           ALU.mult)
        TS("gp", T["st_ssq2"][:, sl], T["st_ssq"][:, sl], 1.0 / D, None,
           ALU.mult)
        tg = TT("gp", T["st_var"][:, sl], T["st_ssq2"][:, sl],
                T["st_var"][:, sl], ALU.subtract, mark=True)
        P.wait("sc", tg)
        AV(T["st_r"][:, sl], T["st_var"][:, sl], ACT.Ln, bias=T["cb"][:, 1:2])
        t = AV(T["st_r"][:, sl], T["st_r"][:, sl], ACT.Exp, scale=-0.5,
               mark=True)
        P.wait("gp", t)
        CP("gp", T["st_rv"][:, sl], T["st_r"][:, sl])
        return TT("gp", T["st_mrneg"][:, sl], T["st_mneg"][:, sl],
                  T["st_rv"][:, sl], ALU.mult, mark=True)

    def normalize_zt(x_ap, j, dep):
        P.wait("sc", dep)
        return AV(af(ZT, 1024), x_ap, ACT.Identity,
                  bias=T["st_mrneg"][:, j:j + 1], scale=T["st_rv"][:, j:j + 1],
                  mark=True)

    def tm_to_fm(src_off, j, evac, src_tensor="AF"):
        tt = None
        for half in range(2):
            bank = next_bank("pt")
            for m in range(4):
                c = half * 4 + m
                src = af(src_off + c * 128, 128)
                tt = TR(PSUM[bank][:, m * 128:(m + 1) * 128], src, mark=(m == 3))
            evac(PSUM[bank][:, 0:512], half * 4, j, tt, bank)
        return tt

    def fm_to_tm_add(src_off, c, dst_tensor="x_tm"):
        bank = next_bank("pt")
        tt = None
        for j in range(NTT):
            tt = TR(PSUM[bank][:, j * 128:(j + 1) * 128],
                    af(src_off + j * 128, 128), mark=(j == NTT - 1))
        P.wait("ve", tt)
        dst = ap3(dst_tensor, c * 128, 1024, NTT, 128)
        tv = TT("ve", dst, dst, psum3(bank, NTT, 128), ALU.add, mark=True)
        free_bank(bank, tv)

    def run_pipe(chains):
        if not chains:
            return
        depth = max(len(c) for c in chains)
        for t in range(len(chains) + depth - 1):
            for i in range(max(0, t - depth + 1), min(t, len(chains) - 1) + 1):
                s = t - i
                if s < len(chains[i]):
                    chains[i][s]()

    def mm_pass(w, rhs_fn, epi, rhs_dep, rhs_dep_fn=None, stages=None):
        K, O, _ = W_SPECS[w]
        Wg = wg_of(K)
        nk = K // 128
        nog = O // Wg
        opg = Wg // 128
        dmatok = {}

        def emit_mms(og, o, sl, wb):
            if o == 0:
                dmatok[og] = P.dma("wq",
                    lambda _wb=wb, _n=nk * Wg: _wb[:, 0:_n],
                    lambda _w=w, _i=og * nk, _n=nk:
                        dram["scr_" + _w][_i:_i + _n].rearrange("n p w -> p n w"),
                    pre=[wbuf_free[sl], prep_w_done.get(w)])
                P.wait("te", dmatok[og])
                if rhs_dep_fn is None:
                    for d in (rhs_dep if isinstance(rhs_dep, list) else [rhs_dep]):
                        P.wait("te", d)
            bank = next_bank("pm")
            for kc in range(nk):
                if rhs_dep_fn is not None:
                    P.wait("te", rhs_dep_fn(kc))
                MM(PSUM[bank][:, 0:512],
                   wb[:, kc * Wg + o * 128: kc * Wg + (o + 1) * 128],
                   rhs_fn(kc), start=(kc == 0), stop=(kc == nk - 1),
                   mark=(kc == nk - 1))
            if o == opg - 1:
                wbuf_free[sl] = P.tok("te")
            return bank, P.tok("te")

        if stages is None:
            for og in range(nog):
                sl = og % 2
                wb = T["wbuf0" if sl == 0 else "wbuf1"]
                for o in range(opg):
                    bank, tt = emit_mms(og, o, sl, wb)
                    epi(bank, og * opg + o, tt)
            return
        chains = []
        for og in range(nog):
            sl = og % 2
            wb = T["wbuf0" if sl == 0 else "wbuf1"]
            for o in range(opg):
                oc = og * opg + o
                mmfn = (lambda _og=og, _o=o, _sl=sl, _wb=wb:
                        emit_mms(_og, _o, _sl, _wb))
                chains.append(stages(oc, mmfn))
        run_pipe(chains)

    x_store_dep = [None]
    final_toks = []

    def load_x(blk):
        seq, t0 = blk // (T_FULL // TB), (blk % (T_FULL // TB)) * TB
        toks = []
        for j in range(NTT):
            toks.append(P.dma("xq",
                lambda _j=j: T["x_tm"][:, _j * 1024:(_j + 1) * 1024],
                lambda _j=j, _s=seq, _t=t0:
                    dram["x"][_s, _t + _j * 128:_t + (_j + 1) * 128, :],
                pre=x_store_dep[0]))
        return toks[-2:]

    def store_x(blk):
        seq, t0 = blk // (T_FULL // TB), (blk % (T_FULL // TB)) * TB
        dep = P.tok("ve")
        toks = []
        for j in range(NTT):
            toks.append(P.dma("oq",
                lambda _j=j, _s=seq, _t=t0:
                    dram["out"][_s, _t + _j * 128:_t + (_j + 1) * 128, :],
                lambda _j=j: T["x_tm"][:, _j * 1024:(_j + 1) * 1024],
                pre=dep))
        x_store_dep[0] = toks[-1]
        final_toks.extend(toks[-2:])

    def tap_dump(blk, phase):
        if (blk, phase) not in taps:
            return
        tap = taps[(blk, phase)]
        dep = P.tok("ve")
        toks = []
        for j in range(NTT):
            toks.append(P.dma("oq",
                lambda _j=j, _t=tap: _t[_j * 128:(_j + 1) * 128, :],
                lambda _j=j: T["x_tm"][:, _j * 1024:(_j + 1) * 1024],
                pre=dep))
        for t_ in toks[-2:]:
            P.wait("ve", t_)
        final_toks.extend(toks[-2:])

    def phase_A(blk, x_dep):
        seq_start = blk % (T_FULL // TB) == 0
        if seq_start:
            for c in range(NCD):
                MS("gp", af(A_PB + c * 513, 513)[:, 0:1], 1.0)
            MS("gp", T["cscarry"][:], 0.0, mark=True)
        kb_done = [None]

        def kb_evac(ps, c_lo, j, tt, bank):
            P.wait("ve", tt)
            dst = bass.AP(T["AB"][:, 0:1].tensor, AB_K + c_lo * 512 + j * 128,
                          [[28672, 128], [512, 4], [1, 128]])
            last = CP("ve", dst, psum3(bank, 4, 128), mark=True)
            free_bank(bank, last)
            kb_done[0] = last

        last_ktr = [None]
        for d in x_dep:
            P.wait("ve", d)
        deps = ln_stats_all(lambda j: T["x_tm"][:, j * 1024:(j + 1) * 1024])
        for j in range(NTT):
            xt = T["x_tm"][:, j * 1024:(j + 1) * 1024]
            tz = normalize_zt(xt, j, deps[j])
            P.wait("ve", tz)
            zt = af(ZT, 1024)
            ta = RED(T["st_am"][:, j:j + 1], zt, "max", absolute=True, mark=True)
            P.wait("gp", ta)
            tg = TS("gp", T["st_am2"][:, j:j + 1], T["st_am"][:, j:j + 1], 1e-5,
                    None, ALU.max, mark=True)
            P.wait("ve", tg)
            tv = RECIP(T["st_scf"][:, j:j + 1], T["st_am2"][:, j:j + 1],
                       mark=True)
            P.wait("gp", tv)
            tg2 = TS("gp", T["st_scf"][:, j:j + 1], T["st_scf"][:, j:j + 1],
                     127.0, None, ALU.mult, mark=True)
            P.wait("sc", tg2)
            if last_ktr[0] is not None:
                P.wait("sc", last_ktr[0])
            t = AV(af(A_KF, 1024), zt, ACT.Identity,
                   scale=T["st_scf"][:, j:j + 1], mark=True)
            P.wait("ve", t)
            CP("ve", T["stri"][:, 0:1024], af(A_KF, 1024))
            tv = TS("ve", af(A_KF, 1024), T["stri"][:, 0:1024], -128.0, 127.0,
                    ALU.max, ALU.min, mark=True)
            P.wait("te", tv)
            last_ktr[0] = tm_to_fm(A_KF, j, kb_evac)

        bank2 = next_bank("pt")
        tprev = None
        P.wait("gp", P.tok("gp"))
        for j in range(NTT):
            sl = j % 2
            rhs = T["rowbuf"][:, sl * 128:(sl + 1) * 128]
            if tprev is not None:
                P.wait("gp", tprev)
            tg = TS("gp", rhs, T["ident"][:], T["st_am2"][:, j:j + 1], None,
                    ALU.mult, mark=True)
            P.wait("te", tg)
            tprev = MM(PSUM[bank2][:, j * 128:(j + 1) * 128], T["ones1"][:],
                       rhs, True, True, mark=True)
        P.wait("ve", tprev)
        tv = TS("ve", T["R"][:], PSUM[bank2][:, 0:512], 1.0 / 127.0, None,
                ALU.mult, mark=True)
        free_bank(bank2, tv)

        kb_dep = kb_done[0]
        epar = [0]

        def epi_dt(bank, oc, te_tok):
            par = epar[0] % 2
            epar[0] += 1
            s0 = af(A_S[par * 2], 512)
            s1 = af(A_S[par * 2 + 1], 512)
            P.wait("ve", te_tok)
            e1 = TT("ve", s0, PSUM[bank][:, 0:512], T["R"][:], ALU.mult,
                    mark=True)
            free_bank(bank, e1)
            tv = TS("ve", s0, s0, T["wsdt"][:, 0:1], T["bdt"][:, oc:oc + 1],
                    ALU.mult, ALU.add, mark=True)
            P.wait("sc", tv)
            AV(s1, s0, ACT.Abs)
            AV(s1, s1, ACT.Exp, scale=-1.0)
            t = AV(s1, s1, ACT.Ln, bias=T["cb"][:, 2:3], mark=True)
            P.wait("ve", t)
            STT("ve", s0, s0, 0.0, s1, ALU.max, ALU.add)
            TS("ve", af(A_DT + oc * 512, 512), s0, 1e-3, 1.0, ALU.max, ALU.min,
               mark=True)

        mm_pass("wdt", lambda kc: ab(AB_K + kc * 512, 512), epi_dt, kb_dep)
        fm_dump(blk, "afterdt")

        last_tanh = {0: None, 1: None}

        def epi_xp(bank, oc, te_tok):
            par = epar[0] % 2
            epar[0] += 1
            s0 = af(A_S[par * 2], 512)
            s1 = af(A_S[par * 2 + 1], 512)
            P.wait("ve", te_tok)
            if last_tanh[par] is not None:
                P.wait("ve", last_tanh[par])
            e1 = TT("ve", s0, PSUM[bank][:, 0:512], T["R"][:], ALU.mult,
                    mark=True)
            free_bank(bank, e1)
            TS("ve", s0, s0, T["wsx"][:, 0:1], T["bx"][:, oc:oc + 1],
               ALU.mult, ALU.add)
            tv = TS("ve", s0, s0, -5.0, 5.0, ALU.max, ALU.min, mark=True)
            if oc < 8:
                TT("ve", af(A_GN + oc * 512, 512), s0,
                   af(A_DT + oc * 512, 512), ALU.mult, mark=True)
            else:
                P.wait("sc", tv)
                last_tanh[par] = AV(af(A_TC + (oc - 8) * 512, 512), s0,
                                    ACT.Tanh, mark=True)

        mm_pass("wx", lambda kc: ab(AB_K + kc * 512, 512), epi_xp, kb_dep)
        fm_dump(blk, "afterxp")

        P.wait("ve", P.tok("sc"))
        last_carry = [None]
        for c in range(NCD):
            if last_carry[0] is not None:
                P.wait("ve", last_carry[0])
            pb = T["AF"][:, A_PB + c * 513: A_PB + (c + 1) * 513]
            dtc = af(A_DT + c * 512, 512)
            tv = TS("ve", af(A_S[0], 512), dtc, T["acol"][:, c:c + 1], -10.0,
                    ALU.mult, ALU.max, mark=True)
            P.wait("sc", tv)
            if c == 0:
                fm_dump(blk, "c0_ua")
            t = AV(af(A_S[1], 512), af(A_S[0], 512), ACT.Exp, mark=True)
            P.wait("ve", t)
            if c == 0:
                fm_dump(blk, "c0_exp")
            SCAN(pb[:, 1:513], af(A_S[1], 512), af(A_S[1], 512), pb[:, 0:1],
                 ALU.mult, ALU.bypass)
            if c == 0:
                fm_dump(blk, "c0_scan")
            RECIP(af(A_S[2], 512), pb[:, 0:512])
            TT("ve", af(A_S[2], 512), af(A_GN + c * 512, 512), af(A_S[2], 512),
               ALU.mult)
            if c == 0:
                fm_dump(blk, "c0_g")
            tsc = SCAN(af(A_S[3], 512), af(A_S[2], 512), af(A_S[2], 512),
                       T["cscarry"][:, c:c + 1], ALU.add, ALU.bypass, mark=True)
            if c == 0:
                fm_dump(blk, "c0_cs")
            tv = TT("ve", af(A_S[4], 512), pb[:, 1:513], af(A_S[3], 512),
                    ALU.mult, mark=True)
            P.wait("gp", tv)
            CP("gp", T["cscarry"][:, c:c + 1], af(A_S[3] + 511, 1))
            last_carry[0] = CP("gp", pb[:, 0:1], pb[:, 512:513], mark=True)
            TT("ve", af(A_S[4], 512), af(A_S[4], 512), af(A_TC + c * 512, 512),
               ALU.mult)
            TT("ve", af(A_S[5], 512), af(A_S[4], 512), af(A_S[4], 512),
               ALU.subtract)
            TS("ve", T["strm"][:], af(A_S[5], 512), 0.0, None, ALU.is_equal)
            tv = SEL(af(A_S[5], 512), T["strm"][:], af(A_S[4], 512),
                     T["zeros"][:], mark=True)
            if c == 0:
                fm_dump(blk, "c0_sel")
            P.wait("te", tv)
            fm_to_tm_add(A_S[5], c)

    def fm_dump(blk, point="endA"):
        if blk != 0 or not fmtaps:
            return
        sel = {t: v for t, v in fmtaps.items() if v[3] == point}
        if not sel:
            return
        dep = P.tok("ve")
        toks = []
        for tag, (tap, off, cols, _pt) in sel.items():
            if isinstance(off, str) and off.startswith("ab:"):
                src_fn = (lambda _o=int(off[3:]), _c=cols: ab(_o, _c))
            elif isinstance(off, str):
                src_fn = (lambda _n=off, _c=cols: T[_n][:, 0:_c])
            else:
                src_fn = (lambda _o=off, _c=cols: af(_o, _c))
            toks.append(P.dma("oq", lambda _t=tap: _t, src_fn, pre=dep))
        for t_ in toks[-2:]:
            P.wait("ve", t_)
        final_toks.extend(toks[-2:])

    def phase_B(blk):
        xn_done = [None]

        def xn_evac(ps, c_lo, j, tt, bank):
            P.wait("ve", tt)
            dst = bass.AP(T["AB"][:, 0:1].tensor, AB_XN + c_lo * 512 + j * 128,
                          [[28672, 128], [512, 4], [1, 128]])
            last = CP("ve", dst, psum3(bank, 4, 128), mark=True)
            free_bank(bank, last)
            xn_done[0] = last

        last_tr = [None]
        deps = ln_stats_all(lambda j: T["x_tm"][:, j * 1024:(j + 1) * 1024])
        for j in range(NTT):
            xt = T["x_tm"][:, j * 1024:(j + 1) * 1024]
            if last_tr[0] is not None:
                P.wait("sc", last_tr[0])
            tz = normalize_zt(xt, j, deps[j])
            P.wait("te", tz)
            last_tr[0] = tm_to_fm(ZT, j, xn_evac)

        qp = [0]
        qdeps = {0: None, 1: None, 2: None, 3: None}
        last_g = [None]
        hq_tok_by_kc = {}

        def epi_h(oc, mmfn):
            par = qp[0] % 4
            qp[0] += 1
            hln = af(B_HB[par], 512)
            sa, sb_ = af(B_QV[par][0], 512), af(B_QV[par][1], 512)
            sa16 = afb(B_QV[par][0], 512)
            sb16 = afb(B_QV[par][1], 512)
            sG = af(B_QG[par], 512)
            si = T["stri"][:, par * 512:(par + 1) * 512]
            dst = ab(AB_HQ + oc * 512, 512)
            st = {}

            def s0():
                st["bank"], te_tok = mmfn()
                P.wait("ve", te_tok)
                if qdeps[par] is not None:
                    P.wait("ve", qdeps[par])
                st["tv0"] = TS("ve", hln, PSUM[st["bank"]][:, 0:512],
                               T["b1"][:, oc:oc + 1], 2.0 ** -6,
                               ALU.add, ALU.max, mark=True)

            def s1():
                P.wait("sc", st["tv0"])
                if qdeps[par] is not None:
                    P.wait("sc", qdeps[par])
                AV(sa, hln, ACT.Ln)
                AV(si, sa, ACT.Identity, scale=-INV_LN2, bias=T["cb"][:, 4:5])
                AV(sa16, si, ACT.Exp, scale=LN2, bias=T["cb"][:, 5:6])
                st["t2"] = AV(sb16, si, ACT.Exp, scale=-LN2,
                              bias=T["cb"][:, 6:7], mark=True)

            def s2():
                P.wait("ve", st["t2"])
                tk = STT("ve", si, PSUM[st["bank"]][:, 0:512],
                         T["b1"][:, oc:oc + 1], sa16, ALU.add, ALU.mult,
                         mark=True)
                free_bank(st["bank"], tk)
                st["tq"] = STT("ve", sG, si, 0.0, sb16, ALU.max, ALU.mult,
                               mark=True)

            def s3():
                P.wait("sc", st["tq"])
                st["to"] = AV(sb_, sG, ACT.Relu, bias=T["cb"][:, 3:4],
                              mark=True)

            def s4():
                P.wait("ve", st["to"])
                tg = STT("ve", dst, sb_, BIG, sG, ALU.mult, ALU.add,
                         mark=True)
                qdeps[par] = tg
                last_g[0] = tg
                hq_tok_by_kc[oc] = tg

            return [s0, s1, s2, s3, s4]

        mm_pass("w1", lambda kc: ab(AB_XN + kc * 512, 512), None, xn_done[0],
                stages=epi_h)

        op = [0]
        hq_toks = dict(hq_tok_by_kc)

        def epi_o(oc, mmfn):
            par = op[0] % 4
            op[0] += 1
            hb = af(B_HB[par], 512)
            sa, sb_ = af(B_QV[par][0], 512), af(B_QV[par][1], 512)
            sb16 = afb(B_QV[par][1], 512)
            sG = af(B_QG[par], 512)
            sg16 = afb(B_QG[par], 512)
            si = T["stri"][:, par * 512:(par + 1) * 512]
            dst = af(B_S[par], 512)
            st = {}

            def s0():
                st["bank"], te_tok = mmfn()
                P.wait("sc", te_tok)
                if qdeps[par] is not None:
                    P.wait("sc", qdeps[par])
                e1 = AV(hb, PSUM[st["bank"]][:, 0:512], ACT.Identity,
                        bias=T["b2q"][:, oc:oc + 1], mark=True)
                free_bank(st["bank"], e1)
                AV(sa, hb, ACT.Square)
                st["t1"] = AV(sb_, sa, ACT.Ln, bias=T["cb"][:, 0:1], mark=True)

            def s1():
                P.wait("ve", st["t1"])
                if qdeps[par] is not None:
                    P.wait("ve", qdeps[par])
                TS("ve", si, sb_, -0.5 * INV_LN2, 0.5, ALU.mult, ALU.add)
                st["tv1"] = TS("ve", sa, si, 14.0, 10.0, ALU.min, ALU.add,
                               mark=True)

            def s2():
                P.wait("sc", st["tv1"])
                AV(sb16, sa, ACT.Exp, scale=LN2)
                st["t2"] = AV(sg16, sa, ACT.Exp, scale=-LN2, mark=True)

            def s3():
                P.wait("ve", st["t2"])
                TT("ve", si, hb, sb16, ALU.mult)
                TT("ve", sb_, si, sg16, ALU.mult)
                TS("ve", sG, sb_, 65504.0, -65504.0, ALU.min, ALU.max)
                TT("ve", sG, sb_, sG, ALU.subtract)
                tg = STT("ve", dst, sG, BIG, sb_, ALU.mult, ALU.add,
                         mark=True)
                qdeps[par] = tg
                st["tg"] = tg

            def s4():
                P.wait("te", st["tg"])
                fm_to_tm_add(B_S[par], oc)

            return [s0, s1, s2, s3, s4]

        mm_pass("w2", lambda kc: ab(AB_HQ + kc * 512, 512), None, None,
                rhs_dep_fn=lambda kc: hq_toks[kc], stages=epi_o)

    def phase_C(blk):
        seq_start = blk % (T_FULL // TB) == 0
        if seq_start:
            MS("gp", T["ccarry"][:], 0.0, mark=True)
        xn_done = [None]

        def xn_evac(ps, c_lo, j, tt, bank):
            P.wait("ve", tt)
            for m in range(4):
                c = c_lo + m
                dst32 = af(C_XN + c * 512 + j * 128, 128)
                TS("ve", dst32, ps[:, m * 128:(m + 1) * 128],
                   T["ng"][:, c:c + 1], T["nb"][:, c:c + 1], ALU.mult, ALU.add,
                   mark=(m == 3))
            free_bank(bank, P.tok("ve"))
            src32 = bass.AP(T["AF"][:, 0:1].tensor, C_XN + c_lo * 512 + j * 128,
                            [[22016, 128], [512, 4], [1, 128]])
            dstb = bass.AP(T["AB"][:, 0:1].tensor, AB_XN + c_lo * 512 + j * 128,
                           [[28672, 128], [512, 4], [1, 128]])
            last = CP("ve", dstb, src32, mark=True)
            xn_done[0] = last

        last_tr = [None]
        deps = ln_stats_all(lambda j: T["x_tm"][:, j * 1024:(j + 1) * 1024])
        for j in range(NTT):
            xt = T["x_tm"][:, j * 1024:(j + 1) * 1024]
            if last_tr[0] is not None:
                P.wait("sc", last_tr[0])
            tz = normalize_zt(xt, j, deps[j])
            P.wait("te", tz)
            last_tr[0] = tm_to_fm(ZT, j, xn_evac)
        CP("gp", T["wtmp"][:, 4:8], T["st_rv"][:, 0:4])
        sdep = CP("gp", T["wtmp"][:, 0:4], T["st_mrneg"][:, 0:4], mark=True)

        lh_done = [None]

        def epi_fr(bank, oc, te_tok):
            s0 = af(C_S[0], 512)
            s1 = af(C_S[1], 512)
            P.wait("sc", te_tok)
            P.wait("sc", lh_done[0])
            th = AV(s0, PSUM[bank][:, 0:512], ACT.Tanh, scale=0.5,
                    bias=T["frbh"][:, oc:oc + 1])
            tl = AV(s1, s0, ACT.Identity, scale=0.5, bias=T["cb"][:, 4:5], mark=True)
            free_bank(bank, tl)
            P.wait("ve", tl)
            xnc = af(C_XN + oc * 512, 512)
            TT("ve", ab(AB_LOW + oc * 512, 512), xnc, s1, ALU.mult)
            TS("ve", s1, s1, -1.0, 1.0, ALU.mult, ALU.add)
            lh_done[0] = TT("ve", ab(AB_HIGH + oc * 512, 512), xnc, s1,
                            ALU.mult, mark=True)

        fm_dump(blk, "C_xn")
        mm_pass("fr", lambda kc: ab(AB_XN + kc * 512, 512), epi_fr, xn_done[0])
        fm_dump(blk, "C_fr")

        ssd_done = [None]
        ssd_toks = {}
        mam_toks = {}
        nk = NCD
        cpbuf_free = [lh_done[0]]

        def cp_wslice(kc, o):
            return ab(AB_XN + kc * 512 + o * 128, 128)

        def scan_block():
            specs = [("dp", 0), ("bp", 1), ("cp", 2)]
            for og in range(2):
                dtoks = {}
                for w, sl in specs:
                    if sl == 2:
                        dtoks[w] = P.dma("wq",
                            lambda: ab(AB_XN, nk * 512),
                            lambda _w=w, _i=og * nk: dram["scr_" + _w][_i:_i + nk]
                                .rearrange("n p w -> p n w"),
                            pre=[cpbuf_free[0], prep_w_done.get(w)])
                    else:
                        dtoks[w] = P.dma("wq",
                            lambda _wb="wbuf0" if sl == 0 else "wbuf1":
                                T[_wb][:, 0:nk * 512],
                            lambda _w=w, _i=og * nk: dram["scr_" + _w][_i:_i + nk]
                                .rearrange("n p w -> p n w"),
                            pre=[wbuf_free[sl], prep_w_done.get(w)])
                for o in range(4):
                    c = og * 4 + o
                    banks = {}
                    for w, sl in specs:
                        P.wait("te", dtoks[w])
                        bank = next_bank("pm")
                        for kc in range(nk):
                            if sl == 2:
                                lhsT = cp_wslice(kc, o)
                                rhs = ab(AB_HIGH + kc * 512, 512)
                            else:
                                wb = T["wbuf0" if sl == 0 else "wbuf1"]
                                lhsT = wb[:, kc * 512 + o * 128: kc * 512 + (o + 1) * 128]
                                rhs = ab(AB_LOW + kc * 512, 512)
                            MM(PSUM[bank][:, 0:512], lhsT, rhs,
                               start=(kc == 0), stop=(kc == nk - 1),
                               mark=(kc == nk - 1))
                        banks[w] = (bank, P.tok("te"))
                    scan_chunk(c, banks)
                for w, sl in specs:
                    if sl == 2:
                        cpbuf_free[0] = P.tok("te")
                    else:
                        wbuf_free[sl] = P.tok("te")

        carry_guard = [None]

        def scan_chunk(c, banks):
            s0, s1, s2, s3, s4 = (af(C_S[i], 512) for i in range(5))
            if carry_guard[0] is not None:
                P.wait("ve", carry_guard[0])
            xnc = af(C_XN + c * 512, 512)
            bank_a, ta = banks["dp"]
            bank_b, tb = banks["bp"]
            bank_c, tc = banks["cp"]
            P.wait("sc", tc)
            P.wait("sc", ssd_done[0])
            AV(s0, PSUM[bank_a][:, 0:512], ACT.Tanh, scale=0.5,
               bias=T["dpbh"][:, c:c + 1])
            AV(s0, s0, ACT.Identity, scale=0.5, bias=T["cb"][:, 4:5])
            AV(s1, PSUM[bank_b][:, 0:512], ACT.Tanh, bias=T["bpb"][:, c:c + 1])
            t3 = AV(s2, PSUM[bank_c][:, 0:512], ACT.Tanh,
                    bias=T["cpb"][:, c:c + 1], mark=True)
            free_bank(bank_a, t3)
            free_bank(bank_b, t3)
            free_bank(bank_c, t3)
            P.wait("ve", t3)
            TT("ve", s1, s1, xnc, ALU.mult)
            tsc = SCAN(s3, s0, s1, T["ccarry"][:, c:c + 1], ALU.mult, ALU.add,
                       mark=True)
            P.wait("gp", tsc)
            carry_guard[0] = CP("gp", T["ccarry"][:, c:c + 1],
                                af(C_S[3] + 511, 1), mark=True)
            TT("ve", s4, s2, s3, ALU.mult)
            ssd32 = af(C_SS + c * 512, 512)
            STT("ve", ssd32, xnc, T["skp"][:, c:c + 1], s4, ALU.mult, ALU.add)
            ssd_done[0] = CP("ve", ab(AB_SSD + c * 512, 512), ssd32, mark=True)
            ssd_toks[c] = ssd_done[0]

        scan_block()
        fm_dump(blk, "C_scan")

        gel_done = [None]
        gel_toks = {}

        def epi_m1(bank, oc, te_tok):
            P.wait("sc", te_tok)
            t = AV(ab(AB_GEL + oc * 512, 512), PSUM[bank][:, 0:512], ACT.Gelu,
                   bias=T["m1b"][:, oc:oc + 1], mark=True)
            free_bank(bank, t)
            gel_done[0] = t
            gel_toks[oc] = t

        mm_pass("m1", lambda kc: ab(AB_LOW + kc * 512, 512), epi_m1, lh_done[0])

        mam_done = [None]

        def epi_m2(bank, oc, te_tok):
            P.wait("sc", te_tok)
            e1 = AV(af(C_MB + oc * 512, 512), PSUM[bank][:, 0:512],
                    ACT.Identity, bias=T["m2b"][:, oc:oc + 1], mark=True)
            free_bank(bank, e1)
            mam_done[0] = AV(ab(AB_MAM + oc * 512, 512),
                             af(C_MB + oc * 512, 512), ACT.Identity, mark=True)
            mam_toks[oc] = mam_done[0]

        mm_pass("m2", lambda kc: ab(AB_GEL + kc * 512, 512), epi_m2, None,
                rhs_dep_fn=lambda kc: gel_toks[kc])
        fm_dump(blk, "C_m2")


        gdep = [None]

        def epi_gate(bank, oc, te_tok):
            s0 = af(C_S[0], 512)
            s1 = af(C_S[1], 512)
            P.wait("sc", te_tok)
            P.wait("sc", gdep[0])
            th = AV(s0, PSUM[bank][:, 0:512], ACT.Tanh, scale=0.5,
                    bias=T["gabh"][:, oc:oc + 1])
            th = AV(s0, s0, ACT.Identity, scale=0.5, bias=T["cb"][:, 4:5], mark=True)
            free_bank(bank, th)
            P.wait("ve", th)
            TT("ve", s1, af(C_SS + oc * 512, 512), af(C_MB + oc * 512, 512),
               ALU.subtract)
            TT("ve", s1, s0, s1, ALU.mult)
            tv = TT("ve", s1, s1, af(C_MB + oc * 512, 512), ALU.add, mark=True)
            gdep[0] = tv
            P.wait("te", tv)
            for j in range(NTT):
                bk = next_bank("pt")
                tt = TR(PSUM[bk][:, 0:128], af(C_S[1] + j * 128, 128), mark=True)
                P.wait("ve", tt)
                tv2 = CP("ve", af(C_Y + j * 1024 + oc * 128, 128),
                         PSUM[bk][:, 0:128], mark=True)
                free_bank(bk, tv2)
                gdep[0] = tv2

        def gate_rhs(kc):
            if kc < 8:
                return ab(AB_SSD + kc * 512, 512)
            return ab(AB_MAM + (kc - 8) * 512, 512)

        mm_pass("gate", gate_rhs, epi_gate, None,
                rhs_dep_fn=lambda kc: ssd_toks[kc] if kc < 8 else mam_toks[kc - 8])
        fm_dump(blk, "C_gate")

        P.wait("sc", sdep)
        prev_use = None
        for j in range(NTT):
            xt = T["x_tm"][:, j * 1024:(j + 1) * 1024]
            yt = af(C_Y + j * 1024, 1024)
            if prev_use is not None:
                P.wait("sc", prev_use)
            t = AV(af(ZT, 1024), xt, ACT.Identity,
                   scale=T["wtmp"][:, 4 + j:5 + j], bias=T["wtmp"][:, j:j + 1],
                   mark=True)
            P.wait("ve", t)
            TT("ve", yt, yt, af(ZT, 1024), ALU.add)
            prev_use = TT("ve", xt, xt, yt, ALU.add, mark=True)

        yln_done = [None]

        def yln_evac(ps, c_lo, j, tt, bank):
            P.wait("ve", tt)
            last = None
            for m in range(4):
                c = c_lo + m
                last = TS("ve", ab(AB_XN + c * 512 + j * 128, 128),
                          ps[:, m * 128:(m + 1) * 128],
                          T["fng"][:, c:c + 1], T["fnb"][:, c:c + 1],
                          ALU.mult, ALU.add, mark=(m == 3))
            free_bank(bank, last)
            yln_done[0] = last

        last_tr2 = [None]
        deps = ln_stats_all(lambda j: af(C_Y + j * 1024, 1024))
        for j in range(NTT):
            yt_off = C_Y + j * 1024
            if last_tr2[0] is not None:
                P.wait("sc", last_tr2[0])
            tz = normalize_zt(af(yt_off, 1024), j, deps[j])
            P.wait("te", tz)
            last_tr2[0] = tm_to_fm(ZT, j, yln_evac)

        hgel_done = [None]
        hgel_toks = {}

        def epi_f1(bank, oc, te_tok):
            P.wait("sc", te_tok)
            t = AV(ab(AB_HQ + oc * 512, 512), PSUM[bank][:, 0:512], ACT.Gelu,
                   bias=T["f1b"][:, oc:oc + 1], mark=True)
            free_bank(bank, t)
            hgel_done[0] = t
            hgel_toks[oc] = t

        fm_dump(blk, "C_y")
        mm_pass("f1", lambda kc: ab(AB_XN + kc * 512, 512), epi_f1, yln_done[0])

        def epi_f2(bank, oc, te_tok):
            P.wait("sc", te_tok)
            e1 = AV(af(C_S[5], 512), PSUM[bank][:, 0:512], ACT.Identity,
                    bias=T["f2b"][:, oc:oc + 1], mark=True)
            free_bank(bank, e1)
            P.wait("te", e1)
            fm_to_tm_add(C_S[5], oc)

        mm_pass("f2", lambda kc: ab(AB_HQ + kc * 512, 512), epi_f2, None,
                rhs_dep_fn=lambda kc: hgel_toks[kc])

    prep()
    prep2_done = [None]
    for blk in range(N_BLK):
        x_dep = load_x(blk)
        tap_dump(blk, "load")
        phase_A(blk, x_dep)
        if blk == 0:
            prep2_done[0] = prep2()
        tap_dump(blk, "A")
        if blk == 0:
            P.wait("ve", prep2_done[0])
            P.wait("sc", prep2_done[0])
        phase_B(blk)
        tap_dump(blk, "B")
        phase_C(blk)
        tap_dump(blk, "C")
        phase_B(blk)
        tap_dump(blk, "D")
        store_x(blk)

    n_oq = P.q["oq"]
    if n_oq:
        P.wait("sy", ("oq", 0, ((n_oq + 1) // 2) * 16))
        P.wait("sy", ("oq", 1, (n_oq // 2) * 16))

    with stack:
        sems = {}
        with contextlib.ExitStack() as semstack:
            for e in ENGINES:
                sems[e] = semstack.enter_context(nc.semaphore("sem_" + e))
            dsems = {}
            for c in DMA_CLASSES:
                for i in (0, 1):
                    dsems[(c, i)] = semstack.enter_context(
                        nc.semaphore(f"dsem_{c}{i}"))
            P.sem = sems
            P.dsem = dsems
            with nc.Block() as block:
                P.eng = {"sy": nc.sync, "ve": nc.vector, "sc": nc.scalar,
                         "te": nc.tensor, "gp": nc.gpsimd}

                @block.sync
                def _(eng):
                    for f in P.ops["sy"]:
                        f()

                @block.vector
                def _(eng):
                    for f in P.ops["ve"]:
                        f()

                @block.scalar
                def _(eng):
                    for f in P.ops["sc"]:
                        f()

                @block.tensor
                def _(eng):
                    for f in P.ops["te"]:
                        f()

                @block.gpsimd
                def _(eng):
                    for f in P.ops["gp"]:
                        f()
    return nc



def kernel(**inputs):
    g = host_prepare(inputs)
    x = np.asarray(inputs["L0_attn_a_x"], np.float32)
    nc = build_program()
    in_maps = []
    for core in range(N_CORES):
        m = dict(g)
        m["x"] = np.ascontiguousarray(x[core * B_LOC:(core + 1) * B_LOC])
        in_maps.append(m)
    res = run_bass_kernel_spmd(nc, in_maps, core_ids=list(range(N_CORES)))
    out = np.concatenate([np.asarray(r["out"]) for r in res.results], axis=0)
    return out.reshape(B_FULL, T_FULL, D)


# version 11
# speedup vs baseline: 1.3726x; 1.0459x over previous
import contextlib
import numpy as np
import ml_dtypes
import concourse.bass as bass
import concourse.mybir as mybir
from concourse.bass_utils import run_bass_kernel_spmd

F32 = mybir.dt.float32
BF16 = mybir.dt.bfloat16
I32 = mybir.dt.int32
U8 = mybir.dt.uint8
ALU = mybir.AluOpType
ACT = mybir.ActivationFunctionType
AX = mybir.AxisListType
bf16 = ml_dtypes.bfloat16

D = 1024
T_FULL = 2048
B_FULL = 16
N_CORES = 8
B_LOC = B_FULL // N_CORES
TB = 512
N_BLK = B_LOC * (T_FULL // TB)
NTT = TB // 128
NCD = D // 128

LN2 = float(np.log(2.0))
INV_LN2 = float(1.0 / np.log(2.0))
BIG = 3.0e38

W_SPECS = {
    "wdt":  (1024, 1024, "sign"),
    "wx":   (1024, 2048, "sign"),
    "w1":   (1024, 4096, "cast"),
    "w2":   (4096, 1024, "e4m3"),
    "fr":   (1024, 1024, "cast"),
    "dp":   (1024, 1024, "cast"),
    "bp":   (1024, 1024, "cast"),
    "cp":   (1024, 1024, "cast"),
    "m1":   (1024, 2048, "cast"),
    "m2":   (2048, 1024, "cast"),
    "gate": (2048, 1024, "cast"),
    "f1":   (1024, 4096, "cast"),
    "f2":   (4096, 1024, "cast"),
}

BIAS_TILES = [("bdt", 8), ("bx", 16), ("alog", 8), ("ng", 8), ("nb", 8),
              ("frb", 8), ("dpb", 8), ("bpb", 8), ("cpb", 8), ("skp", 8),
              ("m1b", 16), ("m2b", 8), ("gab", 8), ("fng", 8), ("fnb", 8),
              ("f1b", 32), ("f2b", 8), ("b1", 32), ("b2", 8)]


def wg_of(K):
    return 512 if K <= 2048 else 256



def _pieces(w_ko, Wg):
    K, O = w_ko.shape
    nk = K // 128
    nog = O // Wg
    out = np.empty((nog * nk, 128, Wg), np.float32)
    i = 0
    for og in range(nog):
        for kc in range(nk):
            out[i] = w_ko[kc * 128:(kc + 1) * 128, og * Wg:(og + 1) * Wg]
            i += 1
    return np.ascontiguousarray(out)


def _coltile(v, n):
    return np.ascontiguousarray(np.asarray(v, np.float32).reshape(n, 128).T)


def host_prepare(inputs):
    f = np.float32
    g = {}
    g["wdt"] = _pieces(np.asarray(inputs["L0_attn_a_W_dt"], f).T, wg_of(1024))
    g["wx"] = _pieces(np.asarray(inputs["L0_attn_a_W_x"], f).T, wg_of(1024))
    g["w1"] = _pieces(np.asarray(inputs["L0_ffn_w1"], f).T, wg_of(1024))
    g["w2"] = _pieces(np.asarray(inputs["L0_ffn_w2"], f).T, wg_of(4096))
    g["fr"] = _pieces(np.asarray(inputs["L0_attn_b_fr_w"], f), wg_of(1024))
    g["dp"] = _pieces(np.asarray(inputs["L0_attn_b_dp_w"], f), wg_of(1024))
    g["bp"] = _pieces(np.asarray(inputs["L0_attn_b_bp_w"], f), wg_of(1024))
    g["cp"] = _pieces(np.asarray(inputs["L0_attn_b_cp_w"], f), wg_of(1024))
    g["m1"] = _pieces(np.asarray(inputs["L0_attn_b_m1_w"], f), wg_of(1024))
    g["m2"] = _pieces(np.asarray(inputs["L0_attn_b_m2_w"], f), wg_of(2048))
    g["gate"] = _pieces(np.asarray(inputs["L0_attn_b_gate_w"], f), wg_of(2048))
    g["f1"] = _pieces(np.asarray(inputs["L0_attn_b_ffn1_w"], f), wg_of(1024))
    g["f2"] = _pieces(np.asarray(inputs["L0_attn_b_ffn2_w"], f), wg_of(4096))
    g["bdt"] = _coltile(inputs["L0_attn_a_b_dt"], 8)
    g["bx"] = _coltile(inputs["L0_attn_a_b_x"], 16)
    g["alog"] = _coltile(inputs["L0_attn_a_A_log"], 8)
    g["ng"] = _coltile(inputs["L0_attn_b_norm_g"], 8)
    g["nb"] = _coltile(inputs["L0_attn_b_norm_b"], 8)
    g["frb"] = _coltile(inputs["L0_attn_b_fr_b"], 8)
    g["dpb"] = _coltile(inputs["L0_attn_b_dp_b"], 8)
    g["bpb"] = _coltile(inputs["L0_attn_b_bp_b"], 8)
    g["cpb"] = _coltile(inputs["L0_attn_b_cp_b"], 8)
    g["skp"] = _coltile(inputs["L0_attn_b_skip"], 8)
    g["m1b"] = _coltile(inputs["L0_attn_b_m1_b"], 16)
    g["m2b"] = _coltile(inputs["L0_attn_b_m2_b"], 8)
    g["gab"] = _coltile(inputs["L0_attn_b_gate_b"], 8)
    g["fng"] = _coltile(inputs["L0_attn_b_ffn_g"], 8)
    g["fnb"] = _coltile(inputs["L0_attn_b_ffn_bn"], 8)
    g["f1b"] = _coltile(inputs["L0_attn_b_ffn1_b"], 32)
    g["f2b"] = _coltile(inputs["L0_attn_b_ffn2_b"], 8)
    g["b1"] = _coltile(inputs["L0_ffn_b1"], 32)
    g["b2"] = _coltile(inputs["L0_ffn_b2"], 8)
    g["ident"] = np.eye(128, dtype=np.float32)
    return g



ENGINES = ("sy", "ve", "sc", "te", "gp")
DEBUG_POINT = ["endA"]
DMA_CLASSES = ("bq", "sq", "wq", "xq", "pw", "oq")
DMA_ENGINE = {"bq": "sy", "sq": "sy", "wq": "sy", "xq": "sy",
              "pw": "gp", "oq": "gp"}


class Prog:
    def __init__(self):
        self.ops = {e: [] for e in ENGINES}
        self.n = {e: 0 for e in ENGINES}
        self.waited = {e: {} for e in ENGINES}
        self.q = {c: 0 for c in DMA_CLASSES}
        self.eng = None
        self.sem = None
        self.dsem = None

    def emit(self, e, fn):
        self.ops[e].append(fn)
        return None

    def mark(self, e, fn):
        self.n[e] += 1
        tok = self.n[e]
        def wrapped(_fn=fn, _e=e):
            _fn().then_inc(self.sem[_e], 1)
        self.ops[e].append(wrapped)
        return (e, tok)

    def wait(self, e, dep):
        if dep is None:
            return
        if dep[0] in ENGINES:
            kind, tok = dep
            assert tok <= self.n[kind], \
                f"deadlock: {e} waits {kind}@{tok} > {self.n[kind]}"
            if kind == e:
                return
            if self.waited[e].get(kind, 0) >= tok:
                return
            self.waited[e][kind] = tok
            def w(_k=kind, _t=tok, _e=e):
                self.eng[_e].wait_ge(self.sem[_k], _t)
            self.ops[e].append(w)
        else:
            cls, idx, val = dep
            key = (cls, idx)
            if self.waited[e].get(key, 0) >= val:
                return
            self.waited[e][key] = val
            def w(_key=key, _v=val, _e=e):
                self.eng[_e].wait_ge(self.dsem[_key], _v)
            self.ops[e].append(w)

    def dma(self, cls, out_fn, in_fn, pre=None):
        eng = DMA_ENGINE[cls]
        if isinstance(pre, list):
            for p in pre:
                self.wait(eng, p)
        else:
            self.wait(eng, pre)
        i = self.q[cls]
        self.q[cls] += 1
        idx = i % 2
        val = ((i // 2) + 1) * 16
        def d(_o=out_fn, _i=in_fn, _k=(cls, idx), _e=eng):
            self.eng[_e].dma_start(out=_o(), in_=_i()).then_inc(self.dsem[_k], 16)
        self.ops[eng].append(d)
        return (cls, idx, val)

    def tok(self, e):
        return (e, self.n[e])


def build_program(debug_taps=(), debug_fm=()):
    nc = bass.Bass("TRN2", target_bir_lowering=False)
    P = Prog()
    stack = contextlib.ExitStack()

    dram = {}
    dram["x"] = nc.dram_tensor("x", [B_LOC, T_FULL, D], F32, kind="ExternalInput").ap()
    dram["out"] = nc.dram_tensor("out", [B_LOC, T_FULL, D], F32,
                                 kind="ExternalOutput").ap()
    for w, (K, O, _) in W_SPECS.items():
        Wg = wg_of(K)
        npc = (K // 128) * (O // Wg)
        dram[w] = nc.dram_tensor(w, [npc, 128, Wg], F32, kind="ExternalInput").ap()
        dram["scr_" + w] = nc.dram_tensor("scr_" + w, [npc, 128, Wg], BF16,
                                          kind="Internal").ap()
    for b, n in BIAS_TILES:
        dram[b] = nc.dram_tensor(b, [128, n], F32, kind="ExternalInput").ap()
    dram["ident"] = nc.dram_tensor("ident", [128, 128], F32,
                                   kind="ExternalInput").ap()
    fmtaps = {}
    for ent in debug_fm:
        tag, off, cols = ent[0], ent[1], ent[2]
        point = ent[3] if len(ent) > 3 else "endA"
        dt_ = BF16 if (isinstance(off, str) and off.startswith("ab:")) else F32
        fmtaps[tag] = (nc.dram_tensor("fmtap_" + tag, [128, cols], dt_,
                                      kind="ExternalOutput").ap(), off, cols, point)
    taps = {}
    for blk, phase in debug_taps:
        taps[(blk, phase)] = nc.dram_tensor(f"tap_{blk}_{phase}", [TB, D], F32,
                                            kind="ExternalOutput").ap()

    T = {}
    def sbuf(name, cols, dtype=F32):
        T[name] = stack.enter_context(nc.sbuf_tensor("sb_" + name, [128, cols], dtype))

    sbuf("ident", 128)
    sbuf("ones1", 128)
    sbuf("zeros", 512)
    sbuf("R", 512)
    sbuf("rowbuf", 640)
    for b, n in BIAS_TILES:
        sbuf(b, n)
    sbuf("acol", 8)
    sbuf("frbh", 8); sbuf("dpbh", 8); sbuf("gabh", 8)
    sbuf("b2q", 8)
    sbuf("wsdt", 2); sbuf("wsx", 2)
    sbuf("wtmp", 8)
    sbuf("cb", 8)
    for s in ["st_sum", "st_ssq", "st_mneg", "st_var", "st_r", "st_rv",
              "st_mrneg", "st_am", "st_scf", "st_sum2", "st_ssq2", "st_am2"]:
        sbuf(s, 8)
    sbuf("wacc", 16)
    sbuf("ccarry", 8)
    sbuf("cscarry", 8)
    sbuf("x_tm", NTT * 1024)
    sbuf("AF", 22016)
    sbuf("AB", 28672, BF16)
    sbuf("wbuf0", 8192, BF16)
    sbuf("wbuf1", 8192, BF16)
    sbuf("stri", 2048, I32)
    sbuf("strm", 512, U8)

    PSUM = {}
    for p in ["pm0", "pm1", "pm2", "pm3", "pm4", "pm5", "pt0", "pt1"]:
        PSUM[p] = stack.enter_context(nc.psum_tensor(p, [128, 512], F32))

    def af(off, n):
        return T["AF"][:, off:off + n]

    def ab(off, n):
        return T["AB"][:, off:off + n]

    def afb(off, n):
        return T["AF"].bitcast(BF16)[:, 2 * off:2 * off + n]

    def ap3(tname, off, stride, n, w):
        h = T[tname]
        row = h.shape[1] if hasattr(h, "shape") else None
        hh = h[:, 0:1]
        rowlen = hh.ap[0][0]
        return bass.AP(hh.tensor, off, [[rowlen, 128], [stride, n], [1, w]])

    def psum3(bank, n, w):
        hh = PSUM[bank][:, 0:1]
        rowlen = hh.ap[0][0]
        return bass.AP(hh.tensor, 0, [[rowlen, 128], [w, n], [1, w]])

    ZT = 0
    WSTG = [1024, 3072]
    QV = [(5120, 7168), (9216, 11264)]
    QG = [13312, 15360]
    A_KF = 1024
    A_DT = 2048
    A_GN = 6144
    A_TC = 10240
    A_PB = 14336
    A_S = [18440 + i * 512 for i in range(6)]
    B_HB = [1024 + i * 512 for i in range(4)]
    B_QV = [(3072 + i * 1024, 3584 + i * 1024) for i in range(4)]
    B_QG = [7168 + i * 512 for i in range(4)]
    B_S = [9216 + i * 512 for i in range(4)]
    C_XN = 1024
    C_MB = 5120
    C_SS = 9216
    C_Y = 13312
    C_S = [17408 + i * 512 for i in range(6)]

    WQB = [0, 2048]
    AB_K = 0
    AB_XN = 0
    AB_LOW = 4096
    AB_HIGH = 8192
    AB_GEL = 12288
    AB_MAM = 20480
    AB_SSD = 24576
    AB_HQ = 4096

    def TT(e, out, a, b, op, mark=False):
        f = lambda: P.eng[e].tensor_tensor(out, a, b, op)
        return P.mark(e, f) if mark else P.emit(e, f)

    def TS(e, out, a, s1, s2, op0, op1=None, mark=False):
        if op1 is None:
            f = lambda: P.eng[e].tensor_scalar(out, a, s1, None, op0)
        else:
            f = lambda: P.eng[e].tensor_scalar(out, a, s1, s2, op0, op1)
        return P.mark(e, f) if mark else P.emit(e, f)

    def STT(e, out, a, s, b, op0, op1, accum=None, mark=False):
        if accum is None:
            f = lambda: P.eng[e].scalar_tensor_tensor(out, a, s, b, op0, op1)
        else:
            f = lambda: P.eng[e].scalar_tensor_tensor(out, a, s, b, op0, op1,
                                                      accum_out=accum)
        return P.mark(e, f) if mark else P.emit(e, f)

    def CP(e, out, a, mark=False):
        f = lambda: P.eng[e].tensor_copy(out, a)
        return P.mark(e, f) if mark else P.emit(e, f)

    def AV(out, a, func, bias=0.0, scale=1.0, mark=False):
        f = lambda: P.eng["sc"].activation(out=out, in_=a, func=func,
                                           bias=bias, scale=scale)
        return P.mark("sc", f) if mark else P.emit("sc", f)

    def AVA(out, a, func, accum, mark=False):
        f = lambda: P.eng["sc"].activation(out=out, in_=a, func=func,
                                           accum_out=accum)
        return P.mark("sc", f) if mark else P.emit("sc", f)

    def RED(out, a, op="sum", absolute=False, mark=False):
        if op == "max":
            f = lambda: P.eng["ve"].reduce_max(out, a, axis=AX.X,
                                               apply_absolute_value=absolute)
        else:
            f = lambda: P.eng["ve"].reduce_sum(out, a, axis=AX.X,
                                               apply_absolute_value=absolute)
        return P.mark("ve", f) if mark else P.emit("ve", f)

    def RECIP(out, a, mark=False):
        f = lambda: P.eng["ve"].reciprocal(out, a)
        return P.mark("ve", f) if mark else P.emit("ve", f)

    def SCAN(out, d0, d1, init, op0, op1, mark=False):
        f = lambda: P.eng["ve"].tensor_tensor_scan(out=out, data0=d0, data1=d1,
                                                   initial=init, op0=op0, op1=op1)
        return P.mark("ve", f) if mark else P.emit("ve", f)

    def SEL(out, mask, t_, f_, mark=False):
        f = lambda: P.eng["ve"].select(out, mask, t_, f_)
        return P.mark("ve", f) if mark else P.emit("ve", f)

    def MM(psum, lhsT, rhs, start, stop, mark=False):
        f = lambda: P.eng["te"].matmul(psum, lhsT, rhs, start=start, stop=stop)
        return P.mark("te", f) if mark else P.emit("te", f)

    def TR(psum, a, mark=False):
        f = lambda: P.eng["te"].transpose(psum, a, T["ident"][:])
        return P.mark("te", f) if mark else P.emit("te", f)

    def MS(e, out, val, mark=False):
        f = lambda: P.eng[e].memset(out, val)
        return P.mark(e, f) if mark else P.emit(e, f)

    psum_free = {}
    rot = {"pm": 0, "pt": 0}

    def next_bank(kind):
        names = (["pm0", "pm1", "pm2", "pm3", "pm4", "pm5"] if kind == "pm"
                 else ["pt0", "pt1"])
        b = names[rot[kind] % len(names)]
        rot[kind] += 1
        if b in psum_free:
            P.wait("te", psum_free[b])
        return b

    def free_bank(b, dep):
        psum_free[b] = dep

    wbuf_free = [None, None]

    def quant_chain(dst, src, n, man, clamp, maxv, pos, sv, sg, i32_off):
        sa, sb_ = af(sv[0], n), af(sv[1], n)
        sG = af(sg, n)
        si = T["stri"][:, i32_off:i32_off + n]
        if pos:
            t1 = AV(sb_, src, ACT.Ln, bias=T["cb"][:, 0:1], mark=True)
            c1 = INV_LN2
        else:
            AV(sa, src, ACT.Square)
            t1 = AV(sb_, sa, ACT.Ln, bias=T["cb"][:, 0:1], mark=True)
            c1 = 0.5 * INV_LN2
        narrow = n < 128
        if narrow:
            si2 = T["stri"][:, i32_off + n:i32_off + 2 * n]
            sa2 = af(sg + n, n)
        P.wait("ve", t1)
        if narrow:
            tn = TS("ve", si, sb_, -c1, 0.5, ALU.mult, ALU.add, mark=True)
            P.wait("gp", tn)
            tg0 = CP("gp", si2, si, mark=True)
            P.wait("ve", tg0)
            tv1 = TS("ve", sa, si2, float(clamp), float(man), ALU.min, ALU.add,
                     mark=True)
        else:
            TS("ve", si, sb_, -c1, 0.5, ALU.mult, ALU.add)
            tv1 = TS("ve", sa, si, float(clamp), float(man), ALU.min, ALU.add,
                     mark=True)
        P.wait("sc", tv1)
        if not narrow:
            sb16 = afb(sv[1], n)
            sg16 = afb(sg, n)
            AV(sb16, sa, ACT.Exp, scale=LN2)
            t2 = AV(sg16, sa, ACT.Exp, scale=-LN2, mark=True)
            P.wait("ve", t2)
            src_done = TT("ve", si, src, sb16, ALU.mult, mark=True)
            tv2 = TT("ve", sb_, si, sg16, ALU.mult, mark=True)
            if pos:
                P.wait("sc", tv2)
                ts = AV(sG, sb_, ACT.Relu, bias=T["cb"][:, 3:4], mark=True)
                P.wait("ve", ts)
                tg = STT("ve", dst, sG, BIG, sb_, ALU.mult, ALU.add, mark=True)
            else:
                TS("ve", sG, sb_, float(maxv), -float(maxv), ALU.min, ALU.max)
                TT("ve", sG, sb_, sG, ALU.subtract)
                tg = STT("ve", dst, sG, BIG, sb_, ALU.mult, ALU.add, mark=True)
            return src_done, tg
        AV(sb_, sa, ACT.Exp, scale=LN2)
        t2 = AV(sa, sa, ACT.Exp, scale=-LN2, mark=True)
        P.wait("ve", t2)
        if narrow:
            src_done = TT("ve", si, src, sb_, ALU.mult, mark=True)
            P.wait("gp", src_done)
            CP("gp", si2, si)
            tg1 = CP("gp", sa2, sa, mark=True)
            P.wait("ve", tg1)
            tv2 = TT("ve", sb_, si2, sa2, ALU.mult, mark=True)
            P.wait("gp", tv2)
            TT("gp", sG, sb_, sb_, ALU.mult)
            TS("gp", sG, sG, float(maxv) * float(maxv), None, ALU.is_gt)
            TS("gp", sG, sG, BIG, 1.0, ALU.mult, ALU.add)
            tg = TT("gp", dst, sb_, sG, ALU.mult, mark=True)
            return src_done, tg
        raise AssertionError("wide path handled above")

    stg_free = {0: None, 1: None}
    qb_free = {0: None, 1: None}
    prep_w_done = {}
    qpar = [0]
    qprev = {0: None, 1: None}

    def prep_stream(w, batch_cb):
        K, O, _ = W_SPECS[w]
        Wg = wg_of(K)
        npc = (K // 128) * (O // Wg)
        per = 2048 // Wg
        for bi, i0 in enumerate(range(0, npc, per)):
            n = min(per, npc - i0)
            sl = bi % 2
            tok = P.dma("sq",
                        lambda _s=sl, _n=n, _Wg=Wg: af(WSTG[_s], _n * _Wg),
                        lambda _w=w, _i=i0, _n=n:
                            dram[_w][_i:_i + _n].rearrange("n p w -> p n w"),
                        pre=stg_free[sl])
            P.wait("ve", tok)
            P.wait("sc", tok)
            stg_free[sl] = batch_cb(WSTG[sl], n * Wg, i0, n, sl)

    def wscale_pass(w, wtile):
        K, O, _ = W_SPECS[w]
        nb = [0]
        def cb(off, ncols, i0, n, sl):
            tv = RED(T["wacc"][:, nb[0]:nb[0] + 1], af(off, ncols), "sum",
                     absolute=True, mark=True)
            nb[0] += 1
            return tv
        prep_stream(w, cb)
        assert nb[0] <= 8
        tv = P.tok("ve")
        P.wait("gp", tv)
        tg = CP("gp", T["st_sum2"][:, 0:nb[0]], T["wacc"][:, 0:nb[0]], mark=True)
        P.wait("ve", tg)
        t1 = RED(T["wtmp"][:, 1:2], T["st_sum2"][:, 0:nb[0]], "sum", mark=True)
        P.wait("gp", t1)
        tg = CP("gp", T["wtmp"][:, 0:1], T["wtmp"][:, 1:2], mark=True)
        P.wait("te", tg)
        bank = next_bank("pt")
        tt = TR(PSUM[bank][0:1, 0:128], T["wtmp"][:, 0:1], mark=True)
        P.wait("ve", tt)
        CP("ve", T["rowbuf"][0:1, 0:128], PSUM[bank][0:1, 0:128])
        tv = RED(T["wtmp"][0:1, 2:3], T["rowbuf"][0:1, 0:128], "sum", mark=True)
        free_bank(bank, tv)
        P.wait("gp", tv)
        tg = TS("gp", T["wtmp"][0:1, 3:4], T["wtmp"][0:1, 2:3],
                1.0 / (K * O), 1e-5, ALU.mult, ALU.add, mark=True)
        bank2 = next_bank("pt")
        P.wait("te", tg)
        tm = MM(PSUM[bank2][:, 0:1], T["ones1"][0:1, 0:128],
                T["wtmp"][0:1, 3:4], True, True, mark=True)
        P.wait("ve", tm)
        tv = CP("ve", T[wtile][:, 0:1], PSUM[bank2][:, 0:1], mark=True)
        P.wait("gp", tv)
        tv = TS("gp", T[wtile][:, 1:2], T[wtile][:, 0:1], 0.5, None, ALU.mult,
                mark=True)
        free_bank(bank2, tv)

    def prep_weight(w, K, O, q):
        wt = "wsdt" if w == "wdt" else "wsx"
        def cb(off, ncols, i0, n, sl):
            qb = ab(WQB[sl], ncols)
            if qb_free[sl] is not None:
                P.wait("ve", qb_free[sl])
                P.wait("gp", qb_free[sl])
            src = af(off, ncols)
            if q == "cast":
                done = CP("ve", qb, src, mark=True)
                wdep = done
            elif q == "sign":
                par = qpar[0] % 2
                qpar[0] += 1
                sa = af(QV[par][0], ncols)
                sb_ = af(QV[par][1], ncols)
                if qprev[par] is not None:
                    P.wait("ve", qprev[par])
                STT("ve", sa, src, -1.0, src, ALU.mult, ALU.max)
                tv0 = TS("ve", sa, sa, T[wt][:, 1:2], None, ALU.is_gt, mark=True)
                P.wait("sc", tv0)
                ts = AV(sb_, src, ACT.Sign, mark=True)
                P.wait("ve", ts)
                done = TT("ve", qb, sa, sb_, ALU.mult, mark=True)
                qprev[par] = done
                wdep = done
            else:
                par = qpar[0] % 2
                qpar[0] += 1
                if qprev[par] is not None:
                    P.wait("ve", qprev[par])
                    P.wait("sc", qprev[par])
                done, tg = quant_chain(qb, src, ncols, man=3, clamp=6,
                                       maxv=240.0, pos=False, sv=QV[par],
                                       sg=QG[par], i32_off=0 if par == 0 else 0)
                qprev[par] = tg
                wdep = tg
            wtok = P.dma("pw",
                         lambda _w=w, _i=i0, _n=n:
                             dram["scr_" + _w][_i:_i + _n].rearrange("n p w -> p n w"),
                         lambda _q=qb: _q,
                         pre=wdep)
            qb_free[sl] = wtok
            prep_w_done[w] = wtok
            return done
        prep_stream(w, cb)

    def prep():
        toks = []
        for b, _ in BIAS_TILES:
            toks.append(P.dma("bq", lambda _b=b: T[_b][:], lambda _b=b: dram[_b]))
        toks.append(P.dma("bq", lambda: T["ident"][:], lambda: dram["ident"]))
        last = toks[-1]
        prev = toks[-2]
        for e in ("ve", "sc", "te"):
            P.wait(e, last)
            P.wait(e, prev)
        MS("ve", T["ones1"][:, :], 1.0)
        MS("ve", T["zeros"][:], 0.0)
        MS("ve", T["cb"][:, 0:1], 1e-38)
        MS("ve", T["cb"][:, 1:2], 1e-5)
        MS("ve", T["cb"][:, 2:3], 1.0)
        MS("ve", T["cb"][:, 3:4], -240.0)
        MS("ve", T["cb"][:, 5:6], 3.0 * LN2)
        MS("ve", T["cb"][:, 6:7], -3.0 * LN2)
        cb_tok = MS("ve", T["cb"][:, 4:5], 0.5, mark=True)
        P.wait("sc", cb_tok)
        TS("ve", T["frbh"][:], T["frb"][:], 0.5, None, ALU.mult)
        TS("ve", T["dpbh"][:], T["dpb"][:], 0.5, None, ALU.mult)
        TS("ve", T["gabh"][:], T["gab"][:], 0.5, None, ALU.mult)
        AV(T["st_sum"][:], T["alog"][:], ACT.Abs)
        AV(T["st_ssq"][:], T["st_sum"][:], ACT.Exp, scale=-1.0)
        t = AV(T["st_var"][:], T["st_ssq"][:], ACT.Ln, bias=T["cb"][:, 2:3], mark=True)
        P.wait("gp", t)
        TS("gp", T["st_mneg"][:], T["alog"][:], 0.0, None, ALU.max)
        TT("gp", T["st_mneg"][:], T["st_mneg"][:], T["st_var"][:], ALU.add)
        tv = TS("gp", T["acol"][:], T["st_mneg"][:], -1.0, None, ALU.mult,
                mark=True)
        P.wait("sc", tv)
        P.wait("ve", tv)
        quant_chain(T["b2q"][:], T["b2"][:], 8, man=10, clamp=14, maxv=65504.0,
                    pos=False, sv=QV[0], sg=QG[0], i32_off=0)
        wscale_pass("wdt", "wsdt")
        wscale_pass("wx", "wsx")
        for w in ["wdt", "wx", "w1", "w2"]:
            K, O, q = W_SPECS[w]
            prep_weight(w, K, O, q)

    def prep2():
        stg2 = [ab(4096, 4096).bitcast(F32), ab(8192, 4096).bitcast(F32)]
        out2 = [ab(12288, 2048), ab(14336, 2048)]
        sfree = {0: None, 1: None}
        ofree = {0: None, 1: None}
        for w in ["fr", "dp", "bp", "cp", "m1", "m2", "gate", "f1", "f2"]:
            K, O, _ = W_SPECS[w]
            Wg = wg_of(K)
            npc = (K // 128) * (O // Wg)
            per = 2048 // Wg
            for bi, i0 in enumerate(range(0, npc, per)):
                n = min(per, npc - i0)
                sl = bi % 2
                tok = P.dma("sq",
                            lambda _s=sl, _n=n, _Wg=Wg: stg2[_s][:, 0:_n * _Wg],
                            lambda _w=w, _i=i0, _n=n:
                                dram[_w][_i:_i + _n].rearrange("n p w -> p n w"),
                            pre=sfree[sl])
                P.wait("sc", tok)
                if ofree[sl] is not None:
                    P.wait("sc", ofree[sl])
                wsc = {"dp": 0.5, "bp": 0.5, "m1": 0.5, "cp": -0.5}.get(w, 1.0)
                ts = AV(out2[sl][:, 0:n * Wg], stg2[sl][:, 0:n * Wg],
                        ACT.Identity, scale=wsc, mark=True)
                sfree[sl] = ts
                wtok = P.dma("pw",
                             lambda _w=w, _i=i0, _n=n:
                                 dram["scr_" + _w][_i:_i + _n].rearrange("n p w -> p n w"),
                             lambda _sl=sl, _n=n, _Wg=Wg: out2[_sl][:, 0:_n * _Wg],
                             pre=ts)
                ofree[sl] = wtok
                prep_w_done[w] = wtok
        return prep_w_done["f2"]

    def ln_stats_all(x_ap_fn, js=range(NTT)):
        trs = {}
        for j in js:
            sl = slice(j, j + 1)
            RED(T["st_sum"][:, sl], x_ap_fn(j), "sum")
            trs[j] = STT("ve", af(6144, 1024), x_ap_fn(j), 1.0, x_ap_fn(j),
                         ALU.bypass, ALU.mult, accum=T["st_ssq"][:, sl],
                         mark=True)
        deps = {}
        for j in js:
            sl = slice(j, j + 1)
            P.wait("gp", trs[j])
            TS("gp", T["st_mneg"][:, sl], T["st_sum"][:, sl], -1.0 / D, None,
               ALU.mult)
            TT("gp", T["st_var"][:, sl], T["st_mneg"][:, sl],
               T["st_mneg"][:, sl], ALU.mult)
            TS("gp", T["st_ssq2"][:, sl], T["st_ssq"][:, sl], 1.0 / D, None,
               ALU.mult)
            tg = TT("gp", T["st_var"][:, sl], T["st_ssq2"][:, sl],
                    T["st_var"][:, sl], ALU.subtract, mark=True)
            P.wait("sc", tg)
            AV(T["st_r"][:, sl], T["st_var"][:, sl], ACT.Ln, bias=T["cb"][:, 1:2])
            t = AV(T["st_r"][:, sl], T["st_r"][:, sl], ACT.Exp, scale=-0.5,
                   mark=True)
            P.wait("gp", t)
            CP("gp", T["st_rv"][:, sl], T["st_r"][:, sl])
            deps[j] = TT("gp", T["st_mrneg"][:, sl], T["st_mneg"][:, sl],
                         T["st_rv"][:, sl], ALU.mult, mark=True)
        return deps

    def ln_stats(x_ap, j):
        sl = slice(j, j + 1)
        RED(T["st_sum"][:, sl], x_ap, "sum")
        tr = STT("ve", af(6144, 1024), x_ap, 1.0, x_ap, ALU.bypass, ALU.mult,
                 accum=T["st_ssq"][:, sl], mark=True)
        P.wait("gp", tr)
        TS("gp", T["st_mneg"][:, sl], T["st_sum"][:, sl], -1.0 / D, None,
           ALU.mult)
        TT("gp", T["st_var"][:, sl], T["st_mneg"][:, sl], T["st_mneg"][:, sl],
           ALU.mult)
        TS("gp", T["st_ssq2"][:, sl], T["st_ssq"][:, sl], 1.0 / D, None,
           ALU.mult)
        tg = TT("gp", T["st_var"][:, sl], T["st_ssq2"][:, sl],
                T["st_var"][:, sl], ALU.subtract, mark=True)
        P.wait("sc", tg)
        AV(T["st_r"][:, sl], T["st_var"][:, sl], ACT.Ln, bias=T["cb"][:, 1:2])
        t = AV(T["st_r"][:, sl], T["st_r"][:, sl], ACT.Exp, scale=-0.5,
               mark=True)
        P.wait("gp", t)
        CP("gp", T["st_rv"][:, sl], T["st_r"][:, sl])
        return TT("gp", T["st_mrneg"][:, sl], T["st_mneg"][:, sl],
                  T["st_rv"][:, sl], ALU.mult, mark=True)

    def normalize_zt(x_ap, j, dep):
        P.wait("sc", dep)
        return AV(af(ZT, 1024), x_ap, ACT.Identity,
                  bias=T["st_mrneg"][:, j:j + 1], scale=T["st_rv"][:, j:j + 1],
                  mark=True)

    def tm_to_fm(src_off, j, evac, src_tensor="AF"):
        tt = None
        for half in range(2):
            bank = next_bank("pt")
            for m in range(4):
                c = half * 4 + m
                src = af(src_off + c * 128, 128)
                tt = TR(PSUM[bank][:, m * 128:(m + 1) * 128], src, mark=(m == 3))
            evac(PSUM[bank][:, 0:512], half * 4, j, tt, bank)
        return tt

    def fm_to_tm_add(src_off, c, dst_tensor="x_tm"):
        bank = next_bank("pt")
        tt = None
        for j in range(NTT):
            tt = TR(PSUM[bank][:, j * 128:(j + 1) * 128],
                    af(src_off + j * 128, 128), mark=(j == NTT - 1))
        P.wait("ve", tt)
        dst = ap3(dst_tensor, c * 128, 1024, NTT, 128)
        tv = TT("ve", dst, dst, psum3(bank, NTT, 128), ALU.add, mark=True)
        free_bank(bank, tv)

    def run_pipe(chains):
        if not chains:
            return
        depth = max(len(c) for c in chains)
        for t in range(len(chains) + depth - 1):
            for i in range(max(0, t - depth + 1), min(t, len(chains) - 1) + 1):
                s = t - i
                if s < len(chains[i]):
                    chains[i][s]()

    def mm_pass(w, rhs_fn, epi, rhs_dep, rhs_dep_fn=None, stages=None):
        K, O, _ = W_SPECS[w]
        Wg = wg_of(K)
        nk = K // 128
        nog = O // Wg
        opg = Wg // 128
        dmatok = {}

        def emit_mms(og, o, sl, wb):
            if o == 0:
                dmatok[og] = P.dma("wq",
                    lambda _wb=wb, _n=nk * Wg: _wb[:, 0:_n],
                    lambda _w=w, _i=og * nk, _n=nk:
                        dram["scr_" + _w][_i:_i + _n].rearrange("n p w -> p n w"),
                    pre=[wbuf_free[sl], prep_w_done.get(w)])
                P.wait("te", dmatok[og])
                if rhs_dep_fn is None:
                    for d in (rhs_dep if isinstance(rhs_dep, list) else [rhs_dep]):
                        P.wait("te", d)
            bank = next_bank("pm")
            for kc in range(nk):
                if rhs_dep_fn is not None:
                    P.wait("te", rhs_dep_fn(kc))
                MM(PSUM[bank][:, 0:512],
                   wb[:, kc * Wg + o * 128: kc * Wg + (o + 1) * 128],
                   rhs_fn(kc), start=(kc == 0), stop=(kc == nk - 1),
                   mark=(kc == nk - 1))
            if o == opg - 1:
                wbuf_free[sl] = P.tok("te")
            return bank, P.tok("te")

        if stages is None:
            for og in range(nog):
                sl = og % 2
                wb = T["wbuf0" if sl == 0 else "wbuf1"]
                for o in range(opg):
                    bank, tt = emit_mms(og, o, sl, wb)
                    epi(bank, og * opg + o, tt)
            return
        chains = []
        for og in range(nog):
            sl = og % 2
            wb = T["wbuf0" if sl == 0 else "wbuf1"]
            for o in range(opg):
                oc = og * opg + o
                mmfn = (lambda _og=og, _o=o, _sl=sl, _wb=wb:
                        emit_mms(_og, _o, _sl, _wb))
                chains.append(stages(oc, mmfn))
        run_pipe(chains)

    x_store_dep = [None]
    final_toks = []

    def load_x(blk):
        seq, t0 = blk // (T_FULL // TB), (blk % (T_FULL // TB)) * TB
        toks = []
        for j in range(NTT):
            toks.append(P.dma("xq",
                lambda _j=j: T["x_tm"][:, _j * 1024:(_j + 1) * 1024],
                lambda _j=j, _s=seq, _t=t0:
                    dram["x"][_s, _t + _j * 128:_t + (_j + 1) * 128, :],
                pre=x_store_dep[0]))
        return toks[-2:]

    def store_x(blk):
        seq, t0 = blk // (T_FULL // TB), (blk % (T_FULL // TB)) * TB
        dep = P.tok("ve")
        toks = []
        for j in range(NTT):
            toks.append(P.dma("oq",
                lambda _j=j, _s=seq, _t=t0:
                    dram["out"][_s, _t + _j * 128:_t + (_j + 1) * 128, :],
                lambda _j=j: T["x_tm"][:, _j * 1024:(_j + 1) * 1024],
                pre=dep))
        x_store_dep[0] = toks[-1]
        final_toks.extend(toks[-2:])

    def tap_dump(blk, phase):
        if (blk, phase) not in taps:
            return
        tap = taps[(blk, phase)]
        dep = P.tok("ve")
        toks = []
        for j in range(NTT):
            toks.append(P.dma("oq",
                lambda _j=j, _t=tap: _t[_j * 128:(_j + 1) * 128, :],
                lambda _j=j: T["x_tm"][:, _j * 1024:(_j + 1) * 1024],
                pre=dep))
        for t_ in toks[-2:]:
            P.wait("ve", t_)
        final_toks.extend(toks[-2:])

    def phase_A(blk, x_dep):
        seq_start = blk % (T_FULL // TB) == 0
        if seq_start:
            for c in range(NCD):
                MS("gp", af(A_PB + c * 513, 513)[:, 0:1], 1.0)
            MS("gp", T["cscarry"][:], 0.0, mark=True)
        kb_done = [None]

        def kb_evac(ps, c_lo, j, tt, bank):
            P.wait("ve", tt)
            dst = bass.AP(T["AB"][:, 0:1].tensor, AB_K + c_lo * 512 + j * 128,
                          [[28672, 128], [512, 4], [1, 128]])
            last = CP("ve", dst, psum3(bank, 4, 128), mark=True)
            free_bank(bank, last)
            kb_done[0] = last

        last_ktr = [None]
        for d in x_dep:
            P.wait("ve", d)
        deps = ln_stats_all(lambda j: T["x_tm"][:, j * 1024:(j + 1) * 1024])
        for j in range(NTT):
            xt = T["x_tm"][:, j * 1024:(j + 1) * 1024]
            tz = normalize_zt(xt, j, deps[j])
            P.wait("ve", tz)
            zt = af(ZT, 1024)
            ta = RED(T["st_am"][:, j:j + 1], zt, "max", absolute=True, mark=True)
            P.wait("gp", ta)
            tg = TS("gp", T["st_am2"][:, j:j + 1], T["st_am"][:, j:j + 1], 1e-5,
                    None, ALU.max, mark=True)
            P.wait("ve", tg)
            tv = RECIP(T["st_scf"][:, j:j + 1], T["st_am2"][:, j:j + 1],
                       mark=True)
            P.wait("gp", tv)
            tg2 = TS("gp", T["st_scf"][:, j:j + 1], T["st_scf"][:, j:j + 1],
                     127.0, None, ALU.mult, mark=True)
            P.wait("sc", tg2)
            if last_ktr[0] is not None:
                P.wait("sc", last_ktr[0])
            t = AV(af(A_KF, 1024), zt, ACT.Identity,
                   scale=T["st_scf"][:, j:j + 1], mark=True)
            P.wait("ve", t)
            CP("ve", T["stri"][:, 0:1024], af(A_KF, 1024))
            tv = TS("ve", af(A_KF, 1024), T["stri"][:, 0:1024], -128.0, 127.0,
                    ALU.max, ALU.min, mark=True)
            P.wait("te", tv)
            last_ktr[0] = tm_to_fm(A_KF, j, kb_evac)

        bank2 = next_bank("pt")
        tprev = None
        P.wait("gp", P.tok("gp"))
        for j in range(NTT):
            sl = j % 2
            rhs = T["rowbuf"][:, sl * 128:(sl + 1) * 128]
            if tprev is not None:
                P.wait("gp", tprev)
            tg = TS("gp", rhs, T["ident"][:], T["st_am2"][:, j:j + 1], None,
                    ALU.mult, mark=True)
            P.wait("te", tg)
            tprev = MM(PSUM[bank2][:, j * 128:(j + 1) * 128], T["ones1"][:],
                       rhs, True, True, mark=True)
        P.wait("ve", tprev)
        tv = TS("ve", T["R"][:], PSUM[bank2][:, 0:512], 1.0 / 127.0, None,
                ALU.mult, mark=True)
        free_bank(bank2, tv)

        kb_dep = kb_done[0]
        epar = [0]

        dtdep = {0: None, 1: None}

        def epi_dt(oc, mmfn):
            par = epar[0] % 2
            epar[0] += 1
            s0 = af(A_S[par * 2], 512)
            s1 = af(A_S[par * 2 + 1], 512)
            st = {}

            def s0f():
                st["bank"], tt = mmfn()
                P.wait("ve", tt)
                if dtdep[par] is not None:
                    P.wait("ve", dtdep[par])
                e1 = TT("ve", s0, PSUM[st["bank"]][:, 0:512], T["R"][:],
                        ALU.mult, mark=True)
                free_bank(st["bank"], e1)
                st["tv"] = TS("ve", s0, s0, T["wsdt"][:, 0:1],
                              T["bdt"][:, oc:oc + 1], ALU.mult, ALU.add,
                              mark=True)

            def s1f():
                P.wait("sc", st["tv"])
                if dtdep[par] is not None:
                    P.wait("sc", dtdep[par])
                AV(s1, s0, ACT.Abs)
                AV(s1, s1, ACT.Exp, scale=-1.0)
                st["t"] = AV(s1, s1, ACT.Ln, bias=T["cb"][:, 2:3], mark=True)

            def s2f():
                P.wait("ve", st["t"])
                STT("ve", s0, s0, 0.0, s1, ALU.max, ALU.add)
                dtdep[par] = TS("ve", af(A_DT + oc * 512, 512), s0, 1e-3, 1.0,
                                ALU.max, ALU.min, mark=True)

            return [s0f, s1f, s2f]

        mm_pass("wdt", lambda kc: ab(AB_K + kc * 512, 512), None, kb_dep,
                stages=epi_dt)
        fm_dump(blk, "afterdt")

        last_tanh = {0: None, 1: None}

        def epi_xp(oc, mmfn):
            par = epar[0] % 2
            epar[0] += 1
            s0 = af(A_S[par * 2], 512)
            st = {}

            def s0f():
                st["bank"], tt = mmfn()
                P.wait("ve", tt)
                if dtdep[par] is not None:
                    P.wait("ve", dtdep[par])
                if last_tanh[par] is not None:
                    P.wait("ve", last_tanh[par])
                e1 = TT("ve", s0, PSUM[st["bank"]][:, 0:512], T["R"][:],
                        ALU.mult, mark=True)
                free_bank(st["bank"], e1)
                TS("ve", s0, s0, T["wsx"][:, 0:1], T["bx"][:, oc:oc + 1],
                   ALU.mult, ALU.add)
                st["tv"] = TS("ve", s0, s0, -5.0, 5.0, ALU.max, ALU.min,
                              mark=True)
                if oc < 8:
                    dtdep[par] = TT("ve", af(A_GN + oc * 512, 512), s0,
                                    af(A_DT + oc * 512, 512), ALU.mult,
                                    mark=True)

            def s1f():
                if oc >= 8:
                    P.wait("sc", st["tv"])
                    last_tanh[par] = AV(af(A_TC + (oc - 8) * 512, 512), s0,
                                        ACT.Tanh, mark=True)

            return [s0f, s1f]

        mm_pass("wx", lambda kc: ab(AB_K + kc * 512, 512), None, kb_dep,
                stages=epi_xp)
        fm_dump(blk, "afterxp")

        P.wait("ve", P.tok("sc"))
        last_carry = [None]
        for c in range(NCD):
            if last_carry[0] is not None:
                P.wait("ve", last_carry[0])
            pb = T["AF"][:, A_PB + c * 513: A_PB + (c + 1) * 513]
            dtc = af(A_DT + c * 512, 512)
            tv = TS("ve", af(A_S[0], 512), dtc, T["acol"][:, c:c + 1], -10.0,
                    ALU.mult, ALU.max, mark=True)
            P.wait("sc", tv)
            if c == 0:
                fm_dump(blk, "c0_ua")
            t = AV(af(A_S[1], 512), af(A_S[0], 512), ACT.Exp, mark=True)
            P.wait("ve", t)
            if c == 0:
                fm_dump(blk, "c0_exp")
            SCAN(pb[:, 1:513], af(A_S[1], 512), af(A_S[1], 512), pb[:, 0:1],
                 ALU.mult, ALU.bypass)
            if c == 0:
                fm_dump(blk, "c0_scan")
            RECIP(af(A_S[2], 512), pb[:, 0:512])
            TT("ve", af(A_S[2], 512), af(A_GN + c * 512, 512), af(A_S[2], 512),
               ALU.mult)
            if c == 0:
                fm_dump(blk, "c0_g")
            tsc = SCAN(af(A_S[3], 512), af(A_S[2], 512), af(A_S[2], 512),
                       T["cscarry"][:, c:c + 1], ALU.add, ALU.bypass, mark=True)
            if c == 0:
                fm_dump(blk, "c0_cs")
            tv = TT("ve", af(A_S[4], 512), pb[:, 1:513], af(A_S[3], 512),
                    ALU.mult, mark=True)
            P.wait("gp", tv)
            CP("gp", T["cscarry"][:, c:c + 1], af(A_S[3] + 511, 1))
            last_carry[0] = CP("gp", pb[:, 0:1], pb[:, 512:513], mark=True)
            TT("ve", af(A_S[4], 512), af(A_S[4], 512), af(A_TC + c * 512, 512),
               ALU.mult)
            TT("ve", af(A_S[5], 512), af(A_S[4], 512), af(A_S[4], 512),
               ALU.subtract)
            TS("ve", T["strm"][:], af(A_S[5], 512), 0.0, None, ALU.is_equal)
            tv = SEL(af(A_S[5], 512), T["strm"][:], af(A_S[4], 512),
                     T["zeros"][:], mark=True)
            if c == 0:
                fm_dump(blk, "c0_sel")
            P.wait("te", tv)
            fm_to_tm_add(A_S[5], c)

    def fm_dump(blk, point="endA"):
        if blk != 0 or not fmtaps:
            return
        sel = {t: v for t, v in fmtaps.items() if v[3] == point}
        if not sel:
            return
        dep = P.tok("ve")
        toks = []
        for tag, (tap, off, cols, _pt) in sel.items():
            if isinstance(off, str) and off.startswith("ab:"):
                src_fn = (lambda _o=int(off[3:]), _c=cols: ab(_o, _c))
            elif isinstance(off, str):
                src_fn = (lambda _n=off, _c=cols: T[_n][:, 0:_c])
            else:
                src_fn = (lambda _o=off, _c=cols: af(_o, _c))
            toks.append(P.dma("oq", lambda _t=tap: _t, src_fn, pre=dep))
        for t_ in toks[-2:]:
            P.wait("ve", t_)
        final_toks.extend(toks[-2:])

    def phase_B(blk):
        xn_done = [None]

        def xn_evac(ps, c_lo, j, tt, bank):
            P.wait("ve", tt)
            dst = bass.AP(T["AB"][:, 0:1].tensor, AB_XN + c_lo * 512 + j * 128,
                          [[28672, 128], [512, 4], [1, 128]])
            last = CP("ve", dst, psum3(bank, 4, 128), mark=True)
            free_bank(bank, last)
            xn_done[0] = last

        last_tr = [None]
        deps = ln_stats_all(lambda j: T["x_tm"][:, j * 1024:(j + 1) * 1024])
        for j in range(NTT):
            xt = T["x_tm"][:, j * 1024:(j + 1) * 1024]
            if last_tr[0] is not None:
                P.wait("sc", last_tr[0])
            tz = normalize_zt(xt, j, deps[j])
            P.wait("te", tz)
            last_tr[0] = tm_to_fm(ZT, j, xn_evac)

        qp = [0]
        qdeps = {0: None, 1: None, 2: None, 3: None}
        last_g = [None]
        hq_tok_by_kc = {}

        def epi_h(oc, mmfn):
            par = qp[0] % 4
            qp[0] += 1
            hln = af(B_HB[par], 512)
            sa, sb_ = af(B_QV[par][0], 512), af(B_QV[par][1], 512)
            sa16 = afb(B_QV[par][0], 512)
            sb16 = afb(B_QV[par][1], 512)
            sG = af(B_QG[par], 512)
            si = T["stri"][:, par * 512:(par + 1) * 512]
            dst = ab(AB_HQ + oc * 512, 512)
            st = {}

            def s0():
                st["bank"], te_tok = mmfn()
                P.wait("ve", te_tok)
                if qdeps[par] is not None:
                    P.wait("ve", qdeps[par])
                st["tv0"] = TS("ve", hln, PSUM[st["bank"]][:, 0:512],
                               T["b1"][:, oc:oc + 1], 2.0 ** -6,
                               ALU.add, ALU.max, mark=True)

            def s1():
                P.wait("sc", st["tv0"])
                if qdeps[par] is not None:
                    P.wait("sc", qdeps[par])
                AV(sa, hln, ACT.Ln)
                AV(si, sa, ACT.Identity, scale=-INV_LN2, bias=T["cb"][:, 4:5])
                AV(sa16, si, ACT.Exp, scale=LN2, bias=T["cb"][:, 5:6])
                st["t2"] = AV(sb16, si, ACT.Exp, scale=-LN2,
                              bias=T["cb"][:, 6:7], mark=True)

            def s2():
                P.wait("ve", st["t2"])
                tk = STT("ve", si, PSUM[st["bank"]][:, 0:512],
                         T["b1"][:, oc:oc + 1], sa16, ALU.add, ALU.mult,
                         mark=True)
                free_bank(st["bank"], tk)
                st["tq"] = STT("ve", sG, si, 0.0, sb16, ALU.max, ALU.mult,
                               mark=True)

            def s3():
                P.wait("sc", st["tq"])
                st["to"] = AV(sb_, sG, ACT.Relu, bias=T["cb"][:, 3:4],
                              mark=True)

            def s4():
                P.wait("ve", st["to"])
                tg = STT("ve", dst, sb_, BIG, sG, ALU.mult, ALU.add,
                         mark=True)
                qdeps[par] = tg
                last_g[0] = tg
                hq_tok_by_kc[oc] = tg

            return [s0, s1, s2, s3, s4]

        mm_pass("w1", lambda kc: ab(AB_XN + kc * 512, 512), None, xn_done[0],
                stages=epi_h)

        op = [0]
        hq_toks = dict(hq_tok_by_kc)

        def epi_o(oc, mmfn):
            par = op[0] % 4
            op[0] += 1
            hb = af(B_HB[par], 512)
            sa, sb_ = af(B_QV[par][0], 512), af(B_QV[par][1], 512)
            sb16 = afb(B_QV[par][1], 512)
            sG = af(B_QG[par], 512)
            sg16 = afb(B_QG[par], 512)
            si = T["stri"][:, par * 512:(par + 1) * 512]
            dst = af(B_S[par], 512)
            st = {}

            def s0():
                st["bank"], te_tok = mmfn()
                P.wait("sc", te_tok)
                if qdeps[par] is not None:
                    P.wait("sc", qdeps[par])
                e1 = AV(hb, PSUM[st["bank"]][:, 0:512], ACT.Identity,
                        bias=T["b2q"][:, oc:oc + 1], mark=True)
                free_bank(st["bank"], e1)
                AV(sa, hb, ACT.Square)
                st["t1"] = AV(sb_, sa, ACT.Ln, bias=T["cb"][:, 0:1], mark=True)

            def s1():
                P.wait("ve", st["t1"])
                if qdeps[par] is not None:
                    P.wait("ve", qdeps[par])
                TS("ve", si, sb_, -0.5 * INV_LN2, 0.5, ALU.mult, ALU.add)
                st["tv1"] = TS("ve", sa, si, 14.0, 10.0, ALU.min, ALU.add,
                               mark=True)

            def s2():
                P.wait("sc", st["tv1"])
                AV(sb16, sa, ACT.Exp, scale=LN2)
                st["t2"] = AV(sg16, sa, ACT.Exp, scale=-LN2, mark=True)

            def s3():
                P.wait("ve", st["t2"])
                TT("ve", si, hb, sb16, ALU.mult)
                TT("ve", sb_, si, sg16, ALU.mult)
                TS("ve", sG, sb_, 65504.0, -65504.0, ALU.min, ALU.max)
                TT("ve", sG, sb_, sG, ALU.subtract)
                tg = STT("ve", dst, sG, BIG, sb_, ALU.mult, ALU.add,
                         mark=True)
                qdeps[par] = tg
                st["tg"] = tg

            def s4():
                P.wait("te", st["tg"])
                fm_to_tm_add(B_S[par], oc)

            return [s0, s1, s2, s3, s4]

        mm_pass("w2", lambda kc: ab(AB_HQ + kc * 512, 512), None, None,
                rhs_dep_fn=lambda kc: hq_toks[kc], stages=epi_o)

    def phase_C(blk):
        seq_start = blk % (T_FULL // TB) == 0
        if seq_start:
            MS("gp", T["ccarry"][:], 0.0, mark=True)
        xn_done = [None]

        def xn_evac(ps, c_lo, j, tt, bank):
            P.wait("ve", tt)
            for m in range(4):
                c = c_lo + m
                dst32 = af(C_XN + c * 512 + j * 128, 128)
                TS("ve", dst32, ps[:, m * 128:(m + 1) * 128],
                   T["ng"][:, c:c + 1], T["nb"][:, c:c + 1], ALU.mult, ALU.add,
                   mark=(m == 3))
            free_bank(bank, P.tok("ve"))
            src32 = bass.AP(T["AF"][:, 0:1].tensor, C_XN + c_lo * 512 + j * 128,
                            [[22016, 128], [512, 4], [1, 128]])
            dstb = bass.AP(T["AB"][:, 0:1].tensor, AB_XN + c_lo * 512 + j * 128,
                           [[28672, 128], [512, 4], [1, 128]])
            last = CP("ve", dstb, src32, mark=True)
            xn_done[0] = last

        last_tr = [None]
        deps = ln_stats_all(lambda j: T["x_tm"][:, j * 1024:(j + 1) * 1024])
        for j in range(NTT):
            xt = T["x_tm"][:, j * 1024:(j + 1) * 1024]
            if last_tr[0] is not None:
                P.wait("sc", last_tr[0])
            tz = normalize_zt(xt, j, deps[j])
            P.wait("te", tz)
            last_tr[0] = tm_to_fm(ZT, j, xn_evac)
        CP("gp", T["wtmp"][:, 4:8], T["st_rv"][:, 0:4])
        sdep = CP("gp", T["wtmp"][:, 0:4], T["st_mrneg"][:, 0:4], mark=True)

        lh_done = [None]
        lh_toks = {}
        frp = [0]
        frdep = {0: None, 1: None}

        def epi_fr(oc, mmfn):
            par = frp[0] % 2
            frp[0] += 1
            s0 = af(C_S[par], 512)
            st = {}

            def s0f():
                st["bank"], tt = mmfn()
                P.wait("sc", tt)
                if frdep[par] is not None:
                    P.wait("sc", frdep[par])
                st["th"] = AV(s0, PSUM[st["bank"]][:, 0:512], ACT.Tanh,
                              scale=0.5, bias=T["frbh"][:, oc:oc + 1],
                              mark=True)
                free_bank(st["bank"], st["th"])

            def s1f():
                P.wait("ve", st["th"])
                xnc = af(C_XN + oc * 512, 512)
                STT("ve", ab(AB_LOW + oc * 512, 512), s0, 1.0, xnc,
                    ALU.add, ALU.mult)
                tv = STT("ve", ab(AB_HIGH + oc * 512, 512), s0, -1.0, xnc,
                         ALU.add, ALU.mult, mark=True)
                frdep[par] = tv
                lh_toks[oc] = tv
                lh_done[0] = tv

            return [s0f, s1f]

        fm_dump(blk, "C_xn")
        mm_pass("fr", lambda kc: ab(AB_XN + kc * 512, 512), None, xn_done[0],
                stages=epi_fr)
        fm_dump(blk, "C_fr")

        ssd_done = [None]
        ssd_toks = {}
        mam_toks = {}
        nk = NCD
        cpbuf_free = [lh_done[0]]

        def cp_wslice(kc, o):
            return ab(AB_XN + kc * 512 + o * 128, 128)

        def scan_block():
            specs = [("dp", 0), ("bp", 1), ("cp", 2)]
            for og in range(2):
                dtoks = {}
                for w, sl in specs:
                    if sl == 2:
                        dtoks[w] = P.dma("wq",
                            lambda: ab(AB_XN, nk * 512),
                            lambda _w=w, _i=og * nk: dram["scr_" + _w][_i:_i + nk]
                                .rearrange("n p w -> p n w"),
                            pre=[cpbuf_free[0], prep_w_done.get(w)])
                    else:
                        dtoks[w] = P.dma("wq",
                            lambda _wb="wbuf0" if sl == 0 else "wbuf1":
                                T[_wb][:, 0:nk * 512],
                            lambda _w=w, _i=og * nk: dram["scr_" + _w][_i:_i + nk]
                                .rearrange("n p w -> p n w"),
                            pre=[wbuf_free[sl], prep_w_done.get(w)])
                for o in range(4):
                    c = og * 4 + o
                    banks = {}
                    for w, sl in specs:
                        P.wait("te", dtoks[w])
                        bank = next_bank("pm")
                        for kc in range(nk):
                            P.wait("te", lh_toks[kc])
                            if sl == 2:
                                lhsT = cp_wslice(kc, o)
                                rhs = ab(AB_HIGH + kc * 512, 512)
                            else:
                                wb = T["wbuf0" if sl == 0 else "wbuf1"]
                                lhsT = wb[:, kc * 512 + o * 128: kc * 512 + (o + 1) * 128]
                                rhs = ab(AB_LOW + kc * 512, 512)
                            MM(PSUM[bank][:, 0:512], lhsT, rhs,
                               start=(kc == 0), stop=(kc == nk - 1),
                               mark=(kc == nk - 1))
                        banks[w] = (bank, P.tok("te"))
                    scan_chunk(c, banks)
                for w, sl in specs:
                    if sl == 2:
                        cpbuf_free[0] = P.tok("te")
                    else:
                        wbuf_free[sl] = P.tok("te")

        carry_guard = [None]

        def scan_chunk(c, banks):
            s0, s1, s2, s3, s4 = (af(C_S[i], 512) for i in range(5))
            if carry_guard[0] is not None:
                P.wait("ve", carry_guard[0])
            xnc = af(C_XN + c * 512, 512)
            bank_a, ta = banks["dp"]
            bank_b, tb = banks["bp"]
            bank_c, tc = banks["cp"]
            P.wait("sc", tc)
            P.wait("sc", ssd_done[0])
            AV(s0, PSUM[bank_a][:, 0:512], ACT.Tanh, scale=0.5,
               bias=T["dpbh"][:, c:c + 1])
            AV(s0, s0, ACT.Identity, scale=0.5, bias=T["cb"][:, 4:5])
            AV(s1, PSUM[bank_b][:, 0:512], ACT.Tanh, bias=T["bpb"][:, c:c + 1])
            t3 = AV(s2, PSUM[bank_c][:, 0:512], ACT.Tanh,
                    bias=T["cpb"][:, c:c + 1], mark=True)
            free_bank(bank_a, t3)
            free_bank(bank_b, t3)
            free_bank(bank_c, t3)
            P.wait("ve", t3)
            TT("ve", s1, s1, xnc, ALU.mult)
            tsc = SCAN(s3, s0, s1, T["ccarry"][:, c:c + 1], ALU.mult, ALU.add,
                       mark=True)
            P.wait("gp", tsc)
            carry_guard[0] = CP("gp", T["ccarry"][:, c:c + 1],
                                af(C_S[3] + 511, 1), mark=True)
            TT("ve", s4, s2, s3, ALU.mult)
            ssd32 = af(C_SS + c * 512, 512)
            STT("ve", ssd32, xnc, T["skp"][:, c:c + 1], s4, ALU.mult, ALU.add)
            ssd_done[0] = CP("ve", ab(AB_SSD + c * 512, 512), ssd32, mark=True)
            ssd_toks[c] = ssd_done[0]

        scan_block()
        fm_dump(blk, "C_scan")

        gel_done = [None]
        gel_toks = {}

        def epi_m1(bank, oc, te_tok):
            P.wait("sc", te_tok)
            t = AV(ab(AB_GEL + oc * 512, 512), PSUM[bank][:, 0:512], ACT.Gelu,
                   bias=T["m1b"][:, oc:oc + 1], mark=True)
            free_bank(bank, t)
            gel_done[0] = t
            gel_toks[oc] = t

        mm_pass("m1", lambda kc: ab(AB_LOW + kc * 512, 512), epi_m1, lh_done[0])

        mam_done = [None]

        def epi_m2(bank, oc, te_tok):
            P.wait("sc", te_tok)
            e1 = AV(af(C_MB + oc * 512, 512), PSUM[bank][:, 0:512],
                    ACT.Identity, bias=T["m2b"][:, oc:oc + 1], mark=True)
            free_bank(bank, e1)
            mam_done[0] = AV(ab(AB_MAM + oc * 512, 512),
                             af(C_MB + oc * 512, 512), ACT.Identity, mark=True)
            mam_toks[oc] = mam_done[0]

        mm_pass("m2", lambda kc: ab(AB_GEL + kc * 512, 512), epi_m2, None,
                rhs_dep_fn=lambda kc: gel_toks[kc])
        fm_dump(blk, "C_m2")


        gdp = [0]
        gdep = {0: None, 1: None}

        def epi_gate(oc, mmfn):
            par = gdp[0] % 2
            gdp[0] += 1
            s0 = af(C_S[par], 512)
            s1 = af(C_S[2 + par], 512)
            mamc = af(C_MB + oc * 512, 512)
            st = {}

            def s0f():
                st["bank"], tt = mmfn()
                P.wait("sc", tt)
                if gdep[par] is not None:
                    P.wait("sc", gdep[par])
                st["th"] = AV(s0, PSUM[st["bank"]][:, 0:512], ACT.Tanh,
                              scale=0.5, bias=T["gabh"][:, oc:oc + 1],
                              mark=True)
                free_bank(st["bank"], st["th"])

            def s1f():
                P.wait("ve", st["th"])
                if gdep[par] is not None:
                    P.wait("ve", gdep[par])
                TT("ve", s1, af(C_SS + oc * 512, 512), mamc, ALU.subtract)
                STT("ve", s1, s0, 1.0, s1, ALU.add, ALU.mult)
                st["tv"] = STT("ve", s1, s1, 0.5, mamc, ALU.mult, ALU.add,
                               mark=True)

            def s2f():
                P.wait("te", st["tv"])
                bk = next_bank("pt")
                tt = None
                for j in range(NTT):
                    tt = TR(PSUM[bk][:, j * 128:(j + 1) * 128],
                            af(C_S[2 + par] + j * 128, 128),
                            mark=(j == NTT - 1))
                P.wait("ve", tt)
                dst = ap3("AF", C_Y + oc * 128, 1024, NTT, 128)
                tv2 = CP("ve", dst, psum3(bk, NTT, 128), mark=True)
                free_bank(bk, tv2)
                gdep[par] = tv2

            return [s0f, s1f, s2f]

        def gate_rhs(kc):
            if kc < 8:
                return ab(AB_SSD + kc * 512, 512)
            return ab(AB_MAM + (kc - 8) * 512, 512)

        mm_pass("gate", gate_rhs, None, None,
                rhs_dep_fn=lambda kc: ssd_toks[kc] if kc < 8 else mam_toks[kc - 8],
                stages=epi_gate)
        fm_dump(blk, "C_gate")

        P.wait("sc", sdep)
        prev_use = None
        for j in range(NTT):
            xt = T["x_tm"][:, j * 1024:(j + 1) * 1024]
            yt = af(C_Y + j * 1024, 1024)
            if prev_use is not None:
                P.wait("sc", prev_use)
            t = AV(af(ZT, 1024), xt, ACT.Identity,
                   scale=T["wtmp"][:, 4 + j:5 + j], bias=T["wtmp"][:, j:j + 1],
                   mark=True)
            P.wait("ve", t)
            TT("ve", yt, yt, af(ZT, 1024), ALU.add)
            prev_use = TT("ve", xt, xt, yt, ALU.add, mark=True)

        yln_done = [None]

        def yln_evac(ps, c_lo, j, tt, bank):
            P.wait("ve", tt)
            last = None
            for m in range(4):
                c = c_lo + m
                last = TS("ve", ab(AB_XN + c * 512 + j * 128, 128),
                          ps[:, m * 128:(m + 1) * 128],
                          T["fng"][:, c:c + 1], T["fnb"][:, c:c + 1],
                          ALU.mult, ALU.add, mark=(m == 3))
            free_bank(bank, last)
            yln_done[0] = last

        last_tr2 = [None]
        deps = ln_stats_all(lambda j: af(C_Y + j * 1024, 1024))
        for j in range(NTT):
            yt_off = C_Y + j * 1024
            if last_tr2[0] is not None:
                P.wait("sc", last_tr2[0])
            tz = normalize_zt(af(yt_off, 1024), j, deps[j])
            P.wait("te", tz)
            last_tr2[0] = tm_to_fm(ZT, j, yln_evac)

        hgel_done = [None]
        hgel_toks = {}

        def epi_f1(bank, oc, te_tok):
            P.wait("sc", te_tok)
            t = AV(ab(AB_HQ + oc * 512, 512), PSUM[bank][:, 0:512], ACT.Gelu,
                   bias=T["f1b"][:, oc:oc + 1], mark=True)
            free_bank(bank, t)
            hgel_done[0] = t
            hgel_toks[oc] = t

        fm_dump(blk, "C_y")
        mm_pass("f1", lambda kc: ab(AB_XN + kc * 512, 512), epi_f1, yln_done[0])

        f2p = [0]
        f2dep = {0: None, 1: None}

        def epi_f2(oc, mmfn):
            par = f2p[0] % 2
            f2p[0] += 1
            slot = C_S[5] if par == 0 else C_S[4]
            st = {}

            def s0f():
                st["bank"], tt = mmfn()
                P.wait("sc", tt)
                if f2dep[par] is not None:
                    P.wait("sc", f2dep[par])
                st["e1"] = AV(af(slot, 512), PSUM[st["bank"]][:, 0:512],
                              ACT.Identity, bias=T["f2b"][:, oc:oc + 1],
                              mark=True)
                free_bank(st["bank"], st["e1"])

            def s1f():
                P.wait("te", st["e1"])
                fm_to_tm_add(slot, oc)
                f2dep[par] = P.tok("ve")

            return [s0f, s1f]

        mm_pass("f2", lambda kc: ab(AB_HQ + kc * 512, 512), None, None,
                rhs_dep_fn=lambda kc: hgel_toks[kc], stages=epi_f2)

    prep()
    prep2_done = [None]
    for blk in range(N_BLK):
        x_dep = load_x(blk)
        tap_dump(blk, "load")
        phase_A(blk, x_dep)
        if blk == 0:
            prep2_done[0] = prep2()
        tap_dump(blk, "A")
        if blk == 0:
            P.wait("ve", prep2_done[0])
            P.wait("sc", prep2_done[0])
        phase_B(blk)
        tap_dump(blk, "B")
        phase_C(blk)
        tap_dump(blk, "C")
        phase_B(blk)
        tap_dump(blk, "D")
        store_x(blk)

    n_oq = P.q["oq"]
    if n_oq:
        P.wait("sy", ("oq", 0, ((n_oq + 1) // 2) * 16))
        P.wait("sy", ("oq", 1, (n_oq // 2) * 16))

    with stack:
        sems = {}
        with contextlib.ExitStack() as semstack:
            for e in ENGINES:
                sems[e] = semstack.enter_context(nc.semaphore("sem_" + e))
            dsems = {}
            for c in DMA_CLASSES:
                for i in (0, 1):
                    dsems[(c, i)] = semstack.enter_context(
                        nc.semaphore(f"dsem_{c}{i}"))
            P.sem = sems
            P.dsem = dsems
            with nc.Block() as block:
                P.eng = {"sy": nc.sync, "ve": nc.vector, "sc": nc.scalar,
                         "te": nc.tensor, "gp": nc.gpsimd}

                @block.sync
                def _(eng):
                    for f in P.ops["sy"]:
                        f()

                @block.vector
                def _(eng):
                    for f in P.ops["ve"]:
                        f()

                @block.scalar
                def _(eng):
                    for f in P.ops["sc"]:
                        f()

                @block.tensor
                def _(eng):
                    for f in P.ops["te"]:
                        f()

                @block.gpsimd
                def _(eng):
                    for f in P.ops["gp"]:
                        f()
    return nc



def kernel(**inputs):
    g = host_prepare(inputs)
    x = np.asarray(inputs["L0_attn_a_x"], np.float32)
    nc = build_program()
    in_maps = []
    for core in range(N_CORES):
        m = dict(g)
        m["x"] = np.ascontiguousarray(x[core * B_LOC:(core + 1) * B_LOC])
        in_maps.append(m)
    res = run_bass_kernel_spmd(nc, in_maps, core_ids=list(range(N_CORES)))
    out = np.concatenate([np.asarray(r["out"]) for r in res.results], axis=0)
    return out.reshape(B_FULL, T_FULL, D)
